# version 22
# baseline (speedup 1.0000x reference)
import numpy as np
from contextlib import ExitStack

import concourse.bass as bass
import concourse.mybir as mybir
from concourse.bass_utils import run_bass_kernel_spmd
from concourse.alu_op_type import AluOpType as alu

F32 = mybir.dt.float32
BF16 = mybir.dt.bfloat16
AF = mybir.ActivationFunctionType

D = 1024
HID = 4096
NCHUNK = 8
HCHUNK = 32
LN_EPS = 1e-5
NEWTON_ITERS = 2
DEBUG_TAPS = False


def _stage_chunk(w_s, s, c, transpose=False):
    S = 2 ** s
    M = np.zeros((128, 128), dtype=np.float32)
    w = np.asarray(w_s, dtype=np.float32).reshape(HID // (2 * S), S, 2, 2)
    for p in range(128):
        i = 128 * c + p
        if i & S:
            continue
        n1 = i // (2 * S)
        j = i % S
        M[p, p] = w[n1, j, 0, 0]
        M[p, p + S] = w[n1, j, 0, 1] if not transpose else w[n1, j, 1, 0]
        M[p + S, p] = w[n1, j, 1, 0] if not transpose else w[n1, j, 0, 1]
        M[p + S, p + S] = w[n1, j, 1, 1]
    return M


def _diag_vecs(w_s, s, c, transpose=False):
    S = 2 ** s
    w = np.asarray(w_s, dtype=np.float32).reshape(HID // (2 * S), S, 2, 2)
    vs = np.zeros(128, dtype=np.float32)
    vp = np.zeros(128, dtype=np.float32)
    for p in range(128):
        i = 128 * c + p
        if not (i & S):
            n1 = i // (2 * S)
            j = i % S
            vs[p] = w[n1, j, 0, 0]
            vp[p] = w[n1, j, 0, 1] if not transpose else w[n1, j, 1, 0]
        else:
            a = i - S
            n1 = a // (2 * S)
            j = a % S
            vs[p] = w[n1, j, 1, 1]
            vp[p] = w[n1, j, 1, 0] if not transpose else w[n1, j, 0, 1]
    return vs, vp


def host_constants(inp):
    c = {}
    c["wq"] = np.ascontiguousarray(inp["L0_attn_W_Q"], dtype=np.float32)
    c["wk"] = np.ascontiguousarray(inp["L0_attn_W_K"], dtype=np.float32)
    c["wv"] = np.ascontiguousarray(inp["L0_attn_W_V"], dtype=np.float32)
    c["wfc"] = np.ascontiguousarray(inp["L0_attn_W_fc"], dtype=np.float32)
    c["gamma"] = np.ascontiguousarray(
        np.asarray(inp["L0_attn_gamma"], np.float32).reshape(8, 128).T)
    c["beta"] = np.ascontiguousarray(
        np.asarray(inp["L0_attn_beta"], np.float32).reshape(8, 128).T)
    c["wexpT"] = np.ascontiguousarray(
        np.asarray(inp["L0_ffn_W_exp"], np.float32).T)
    c["wcon"] = np.ascontiguousarray(inp["L0_ffn_W_con"], dtype=np.float32)
    c["bexp"] = np.ascontiguousarray(
        np.asarray(inp["L0_ffn_b_exp"], np.float32).reshape(32, 128).T)
    c["upbias"] = np.ascontiguousarray(
        np.asarray(inp["L0_ffn_up_bias"], np.float32).reshape(32, 128).T)
    c["bcon"] = np.ascontiguousarray(
        np.asarray(inp["L0_ffn_b_con"], np.float32).reshape(8, 128).T)
    c["dbias"] = np.ascontiguousarray(
        np.asarray(inp["L0_ffn_down_bias"], np.float32).reshape(8, 128).T)

    upw = np.asarray(inp["L0_ffn_up_weights"], np.float32)
    dnw = np.asarray(inp["L0_ffn_down_weights"], np.float32)
    slo_u = np.zeros((7, 32, 128, 128), dtype=np.float32)
    slo_dT = np.zeros((7, 32, 128, 128), dtype=np.float32)
    for s in range(7):
        for ch in range(32):
            slo_u[s, ch] = _stage_chunk(upw[s], s, ch)
            slo_dT[s, ch] = _stage_chunk(dnw[s], s, ch, transpose=True)
    c["slo_u"] = slo_u.reshape(7 * 32 * 128, 128)
    c["slo_dT"] = slo_dT.reshape(7 * 32 * 128, 128)
    dv_u = np.zeros((128, 5 * 32 * 2), dtype=np.float32)
    dv_d = np.zeros((128, 5 * 32 * 2), dtype=np.float32)
    for si in range(5):
        s = si + 7
        for ch in range(32):
            vs_u, vp_u = _diag_vecs(upw[s], s, ch)
            dv_u[:, (si * 32 + ch) * 2 + 0] = vs_u
            dv_u[:, (si * 32 + ch) * 2 + 1] = vp_u
            vs_d, vp_d = _diag_vecs(dnw[s], s, ch, transpose=True)
            dv_d[:, (si * 32 + ch) * 2 + 0] = vs_d
            dv_d[:, (si * 32 + ch) * 2 + 1] = vp_d
    c["dv_u"] = dv_u
    c["dv_d"] = dv_d
    c["ident"] = np.eye(128, dtype=np.float32)
    ej = np.zeros((16, 8 * 128), dtype=np.float32)
    for j in range(8):
        ej[2 * j, 128 * j:128 * j + 64] = 1.0
        ej[2 * j + 1, 128 * j + 64:128 * j + 128] = 1.0
    c["ej"] = ej
    return c



ENGINES = ("tensor", "vector", "scalar", "sync", "gpsimd")


class Prog:
    def __init__(self, nc, ctx):
        self.nc = nc
        self.stream = {e: [] for e in ENGINES}
        self.sem = {}
        self.cnt = {}
        for e in ENGINES:
            self.sem[e] = ctx.enter_context(nc.semaphore("sem_" + e))
            self.cnt[e] = 0
        self.ctx = ctx
        self.wr = {}
        self.rd = {}
        self.floor = {e: {} for e in ENGINES}

    def _waits_for(self, eng, reads, writes):
        toks = []
        for r in reads:
            t = self.wr.get(r)
            if t:
                toks.append(t)
        for r in writes:
            t = self.wr.get(r)
            if t:
                toks.append(t)
            toks.extend(self.rd.get(r, ()))
        need = {}
        for s, v in toks:
            if v > need.get(s, -1):
                need[s] = v
        fl = self.floor[eng]
        out = []
        for s, v in sorted(need.items()):
            if fl.get(s, 0) >= v:
                continue
            assert v <= self.cnt[s], (eng, s, v, self.cnt[s])
            out.append((s, v))
            fl[s] = v
        return out

    def dma_ch(self, ch):
        name = "dch_" + ch
        if name not in self.sem:
            self.sem[name] = self.ctx.enter_context(nc_sem(self.nc, name))
            self.cnt[name] = 0
        return name

    def op(self, eng, build, reads=(), writes=(), dma=False, inc=True,
           ch=None):
        semname = self.dma_ch(ch) if dma else eng
        waits = self._waits_for(eng, reads, writes)
        tok = None
        if inc:
            self.cnt[semname] += 16 if dma else 1
            tok = (semname, self.cnt[semname])
        semh = self.sem[semname]
        amount = 16 if dma else 1
        sems = self.sem

        def closure(e, waits=waits, build=build, tok=tok):
            for s, v in waits:
                e.wait_ge(sems[s], v)
            instr = build(e)
            if tok is not None:
                instr.then_inc(semh, amount)
        self.stream[eng].append(closure)
        if tok is not None:
            for r in reads:
                self.rd.setdefault(r, []).append(tok)
            for r in writes:
                self.wr[r] = tok
                self.rd[r] = []
        return tok

    def mm_group(self, builders, reads, writes, reads_each=None):
        n = len(builders)
        for i, b in enumerate(builders[:-1]):
            r = reads + (tuple(reads_each[i]) if reads_each else ())
            self.op("tensor", b, reads=r, writes=writes, inc=False)
        r = reads + (tuple(reads_each[n - 1]) if reads_each else ())
        return self.op("tensor", builders[-1], reads=r, writes=writes)

    def raw_wait(self, eng, semname, value):
        if value <= 0:
            return
        fl = self.floor[eng]
        if fl.get(semname, 0) >= value:
            return
        fl[semname] = value
        sems = self.sem

        def closure(e):
            e.wait_ge(sems[semname], value)
        self.stream[eng].append(closure)

    def realize(self, blk):
        for e in ENGINES:
            ops = self.stream[e]

            def runner(eng, ops=ops):
                for f in ops:
                    f(eng)
            getattr(blk, e)(runner)



def nc_sem(nc, name):
    return nc.semaphore(name)


def build_program(n_seq=2, seq_len=4096, T=512):
    n_tiles = seq_len // T
    NG = T // 128
    nc = bass.Bass("TRN2", target_bir_lowering=False)

    dx = nc.dram_tensor("x", [n_seq * D, seq_len], F32,
                        kind="ExternalInput").ap()
    dout = nc.dram_tensor("out", [n_seq * D, seq_len], F32,
                          kind="ExternalOutput").ap()
    dins = {}
    for name, shape in [
        ("wq", [D, D]), ("wk", [D, D]), ("wv", [D, D]), ("wfc", [D, D]),
        ("gamma", [128, 8]), ("beta", [128, 8]),
        ("wexpT", [HID, D]), ("wcon", [HID, D]),
        ("bexp", [128, 32]), ("upbias", [128, 32]),
        ("bcon", [128, 8]), ("dbias", [128, 8]),
        ("slo_u", [7 * 32 * 128, 128]), ("slo_dT", [7 * 32 * 128, 128]),
        ("dv_u", [128, 5 * 32 * 2]), ("dv_d", [128, 5 * 32 * 2]),
        ("ident", [128, 128]), ("ej", [16, 8 * 128]),
    ]:
        dins[name] = nc.dram_tensor(name, shape, F32, kind="ExternalInput").ap()
    asp = nc.dram_tensor("asp", [D, HID], BF16).ap()
    wqsp = nc.dram_tensor("wqsp", [128, 8 * 1024], BF16).ap()
    wfsp = nc.dram_tensor("wfsp", [128, 8 * 1024], BF16).ap()
    x1sp = nc.dram_tensor("x1sp", [n_seq * D, seq_len], F32).ap()
    xn2hi = nc.dram_tensor("xn2hi", [n_seq * D, seq_len], BF16).ap()
    xn2lo = nc.dram_tensor("xn2lo", [n_seq * D, seq_len], BF16).ap()
    if DEBUG_TAPS:
        dbg1 = nc.dram_tensor("dbg1", [n_seq * D, seq_len], F32).ap()
        dbg2 = nc.dram_tensor("dbg2", [n_seq * D, seq_len], F32).ap()
        dbg3 = nc.dram_tensor("dbg3", [n_seq * D, seq_len], F32).ap()
        dbg4 = nc.dram_tensor("dbg4", [n_seq * D, seq_len], F32).ap()

    ctx = ExitStack()
    with ctx:
        S = lambda name, shape, dt: ctx.enter_context(
            nc.sbuf_tensor(name, shape, dt))
        c_sb = S("c_sb", [128, HCHUNK * 1024], BF16)
        w_a = S("w_a", [128, NCHUNK * 1024], BF16)
        w_b = S("w_b", [128, NCHUNK * 1024], BF16)
        ident_f = S("ident_f", [128, 128], F32)
        ident_b = S("ident_b", [128, 128], BF16)
        onesc = S("onesc", [128, 1], BF16)
        onesr = S("onesr", [1, 512], BF16)
        onesr_f = S("onesr_f", [1, 128], F32)
        epsr = S("epsr", [1, 1], BF16)
        ej_b = S("ej_b", [16, 8 * 128], BF16)
        bfull = S("bfull", [128, 32], F32)
        bvec = S("bvec", [128, 32], BF16)
        upb_f = S("upb_f", [128, 32], F32)
        cch = S("cch", [128, 8], F32)
        gam = S("gam", [128, 8], F32)
        bet = S("bet", [128, 8], F32)
        kvbd = S("kvbd", [128, n_seq * 8 * 128], BF16)
        ksc = S("ksc", [128, n_seq * 8 * 16], BF16)
        x_t = S("x_t", [128, NCHUNK * T], F32)
        s1 = S("s1", [128, NCHUNK * T], BF16)
        s2 = S("s2", [128, NCHUNK * T], BF16)
        s3 = S("s3", [128, NCHUNK * T], BF16)
        g_t = S("g_t", [128, HCHUNK * T], BF16)
        u_t = g_t.bitcast(F32)
        kv_k = S("kv_k", [128, 1024], BF16)
        kv_v = S("kv_v", [128, 8 * 129], BF16)
        a_st = [S("a_st0", [128, 8 * 128], BF16),
                S("a_st1", [128, 8 * 128], BF16),
                S("a_st2", [128, 8 * 128], BF16)]
        stats_sb = S("stats_sb", [33, T], F32)
        small = S("small", [128, 48], F32)
        small_i = small.bitcast(mybir.dt.int32)
        den_sb = S("den_sb", [16, T], F32)
        dsp = S("dsp", [128, 64], F32)
        rdl_sp = S("rdl_sp", [128, 64], F32)
        rdh_sb = S("rdh_sb", [16, T], BF16)
        rdl_sb = S("rdl_sb", [16, T], BF16)
        
        wtmp = S("wtmp", [128, 1024], F32)
        rdb_sb = wtmp
        stg_all = S("stg_all", [128, 2688], BF16)
        stg_fv = stg_all.bitcast(F32)
        kvacc = stg_fv
        stg_f = stg_fv
        STGB0 = 1792
        dvv = S("dvv", [128, 320], F32)
        dgb = S("dgb", [128, 8 * 128], BF16)
        ypp = S("ypp", [128, 2 * 128], BF16)
        trn = S("trn", [128, 2 * 128], BF16)

        P = [ctx.enter_context(nc.psum_tensor("pb%d" % i, [128, 512], F32))
             for i in range(8)]

        blk = ctx.enter_context(nc.Block())
        pr = Prog(nc, ctx)

        tens = lambda b, **kw: pr.op("tensor", b, **kw)
        vec = lambda b, **kw: pr.op("vector", b, **kw)
        sca = lambda b, **kw: pr.op("scalar", b, **kw)

        SERIAL_CH = {"misc"}

        def dma(queue, dst, src, reads=(), writes=(), ch="misc"):
            semname = pr.dma_ch(ch)
            if ch in SERIAL_CH:
                pr.raw_wait(queue, semname, pr.cnt[semname])
            return pr.op(queue, lambda e: e.dma_start(dst, src),
                         reads=reads, writes=writes, dma=True, ch=ch)

        def warm(n):
            for _ in range(n):
                pr.op("tensor", lambda e: e.matmul(
                    P[7][:, 0:512], c_sb[:, 0:128], c_sb[:, 0:512],
                    start=True, stop=True, skip_group_check=True), inc=False)

        ALLC = tuple("C:%d" % k for k in range(HCHUNK))
        ALLWA = tuple("wa:%d" % d for d in range(8))
        ALLWB = tuple("wb:%d" % d for d in range(8))

        dma("sync", ident_f[:, :], dins["ident"][:, :], writes=("idf",), ch="misc")
        vec(lambda e: e.tensor_copy(ident_b[:, :], ident_f[:, :]),
            reads=("idf",), writes=("idb",))
        vec(lambda e: e.memset(onesc[:, :], 1.0), writes=("ones",))
        vec(lambda e: e.memset(onesr[:, :], 1.0), writes=("onesr",))
        vec(lambda e: e.memset(onesr_f[:, :], 1.0), writes=("onesrf",))
        vec(lambda e: e.memset(epsr[:, :], float(D) * LN_EPS), writes=("epsr",))
        for h in range(2):
            dma("sync", wtmp[0:16, 0:512],
                dins["ej"][:, 512 * h:512 * h + 512], writes=("wtmp0",),
                ch="wload0")
            vec(lambda e, h=h: e.tensor_copy(
                ej_b[:, 512 * h:512 * h + 512], wtmp[0:16, 0:512]),
                reads=("wtmp0",), writes=("ej",))
        dma("sync", gam[:, :], dins["gamma"][:, :], writes=("gam",), ch="misc")
        dma("sync", bet[:, :], dins["beta"][:, :], writes=("bet",), ch="misc")
        dma("sync", upb_f[:, :], dins["upbias"][:, :], writes=("upb",), ch="misc")
        dma("sync", cch[:, :], dins["bcon"][:, :], writes=("cch",), ch="misc")
        dma("sync", wtmp[:, 0:8], dins["dbias"][:, :], writes=("wtmp0",), ch="wload0")
        vec(lambda e: e.scalar_tensor_tensor(cch[:, :], wtmp[:, 0:8], 1.0,
                                             cch[:, :], alu.mult, alu.add),
            reads=("wtmp0", "cch"), writes=("cch",))
        vec(lambda e: e.tensor_scalar(cch[:, :], cch[:, :], 0.5, None,
                                      alu.mult, alu.bypass),
            reads=("cch",), writes=("cch",))
        dma("sync", wtmp[:, 0:32], dins["bexp"][:, :], writes=("wtmp0",), ch="wload0")
        vec(lambda e: e.tensor_copy(bvec[:, :], wtmp[:, 0:32]),
            reads=("wtmp0",), writes=tuple("bv:%d" % ch for ch in range(32)))
        vec(lambda e: e.memset(kvbd[:, :], 0.0), writes=("kvbd",))
        vec(lambda e: e.memset(ksc[:, :], 0.0), writes=("ksc",))
        vec(lambda e: e.memset(
            bass.AP(kv_v, 128, [[8 * 129, 128], [129, 8]]), 1.0),
            writes=("kvv1",))

        def emit_x_load(iseq, it):
            for h in range(2):
                xk = tuple("x:%d" % c for c in range(4 * h, 4 * h + 4))
                dma("scalar",
                    bass.AP(x_t, 4 * h * T,
                            [[NCHUNK * T, 128], [T, 4], [1, T]]),
                    bass.AP(dx.tensor,
                            iseq * D * seq_len + 4 * h * 128 * seq_len + it * T,
                            [[seq_len, 128], [128 * seq_len, 4], [1, T]]),
                    reads=xk, writes=xk, ch="x%d" % h)


        def load_w(dsrc, dst_sb, tag):
            for i in range(16):
                ch, h = divmod(i, 2)
                p = i % 2
                dma("sync" if p == 0 else "gpsimd",
                    wtmp[:, 512 * p:512 * p + 512],
                    dsrc[128 * ch:128 * ch + 128, 512 * h:512 * h + 512],
                    writes=("wtmp%d" % p,),
                    ch="wload0" if p == 0 else "wload1g")
                pr.op("gpsimd" if p == 0 else "vector",
                      lambda e, ch=ch, h=h, p=p, dst_sb=dst_sb: e.tensor_copy(
                          dst_sb[:, 1024 * ch + 512 * h:
                                 1024 * ch + 512 * h + 512],
                          wtmp[:, 512 * p:512 * p + 512]),
                      reads=("wtmp%d" % p,), writes=("%s:%d" % (tag, ch),))

        def fill_R_ops(src):
            ops = []
            for i in range(64):
                ch, h = divmod(i, 2)
                p = i % 2

                def one(ch=ch, h=h, p=p):
                    dma("sync" if p == 0 else "gpsimd",
                        wtmp[:, 512 * p:512 * p + 512],
                        src[128 * ch:128 * ch + 128,
                            512 * h:512 * h + 512],
                        writes=("wtmp%d" % p,),
                        ch="wload0" if p == 0 else "wload1g")
                    pr.op("gpsimd" if p == 0 else "vector",
                          lambda e, ch=ch, h=h, p=p: e.tensor_copy(
                              c_sb[:, 1024 * ch + 512 * h:
                                   1024 * ch + 512 * h + 512],
                              wtmp[:, 512 * p:512 * p + 512]),
                          reads=("wtmp%d" % p,), writes=("C:%d" % ch,))
                ops.append(one)
            return ops

        def fill_R(src):
            for op in fill_R_ops(src):
                op()

        DGBALL = tuple("dgb:%d" % k for k in range(7))

        def load_stage_chunk(src_ap, ch, p):
            src = bass.AP(src_ap.tensor, ch * 128 * 128,
                          [[128, 128], [32 * 128 * 128, 7], [1, 128]])
            dst = bass.AP(stg_f, 0, [[1344, 128], [128, 7], [1, 128]])
            dma("scalar" if p == 0 else "sync", dst, src,
                writes=("stgf",), ch="stg" if p == 0 else "stg1")
            if p == 0:
                pr.op("gpsimd", lambda e: e.tensor_copy(
                    stg_all[:, STGB0:STGB0 + 896], stg_f[:, 0:896]),
                    reads=("stgf",), writes=("stg",))
            else:
                pr.op("gpsimd", lambda e: e.tensor_copy(
                    dgb[:, 0:896], stg_f[:, 0:896]),
                    reads=("stgf",), writes=DGBALL)

        def lo_chain_pair(src_key, cA, down, fill):
            holds = {}
            st = {0: 0, 1: 0}
            order = range(0, 7) if down else range(6, -1, -1)
            for k, s in enumerate(order):
                for p, ch in ((0, cA), (1, cA + 1)):
                    lc = s * 128
                    if p == 0:
                        lhs = stg_all[:, STGB0 + lc:STGB0 + lc + 128]
                        lkey = ("stg",)
                    else:
                        lhs = dgb[:, lc:lc + 128]
                        lkey = ("dgb:%d" % s,)
                    slots = ypp if p == 0 else trn
                    sname = "ypp" if p == 0 else "trn"
                    pb = 4 + p
                    if k == 0:
                        rhs, rkeyn = ident_b[:, :], ("idb",)
                    else:
                        cur = st[p]
                        rhs = slots[:, cur * 128:cur * 128 + 128]
                        rkeyn = ("%s:%d" % (sname, cur),)
                    tens(lambda e, lhs=lhs, rhs=rhs, pb=pb: e.matmul(
                        P[pb][0:128, 0:128], lhs, rhs, start=True, stop=True),
                        reads=lkey + rkeyn, writes=("p%d" % pb,))
                    if k == 6:
                        sca(lambda e, p=p, pb=pb: e.activation(
                            kv_k[:, 128 * p:128 * p + 128],
                            P[pb][0:128, 0:128], AF.Identity),
                            reads=("p%d" % pb,), writes=("kvh:%d" % p,))
                        holds[p] = (kv_k[:, 128 * p:128 * p + 128],
                                    "kvh:%d" % p)
                    else:
                        nxt = 1 - st[p]
                        sca(lambda e, nxt=nxt, slots=slots, pb=pb:
                            e.activation(slots[:, nxt * 128:nxt * 128 + 128],
                                         P[pb][0:128, 0:128], AF.Identity),
                            reads=("p%d" % pb,),
                            writes=("%s:%d" % (sname, nxt),))
                        st[p] = nxt
                    if fill:
                        fill.pop(0)()
            return holds

        def cupd_ops(ch, hold, down):
            pc, pkey = hold
            ops = []
            pbs = (0, 1) if ch % 2 == 0 else (2, 3)
            for h in range(2):
                pb = pbs[h]

                def mm(e, pc=pc, ch=ch, h=h, pb=pb):
                    return e.matmul(
                        P[pb][:, :], pc,
                        c_sb[:, 1024 * ch + 512 * h:1024 * ch + 512 * h + 512],
                        start=True, stop=True)
                ops.append((lambda mm=mm, pkey=pkey, ch=ch, pb=pb:
                            tens(mm, reads=(pkey, "C:%d" % ch),
                                 writes=("p%d" % pb,))))
                if down:
                    ops.append((lambda ch=ch, h=h, pb=pb:
                                vec(lambda e: e.tensor_scalar(
                                    c_sb[:, 1024 * ch + 512 * h:
                                         1024 * ch + 512 * h + 512],
                                    P[pb][:, :], 0.5, None,
                                    alu.mult, alu.bypass),
                                    reads=("p%d" % pb,),
                                    writes=("C:%d" % ch,))))
                else:
                    ops.append((lambda ch=ch, h=h, pb=pb:
                                vec(lambda e: e.tensor_copy(
                                    c_sb[:, 1024 * ch + 512 * h:
                                         1024 * ch + 512 * h + 512],
                                    P[pb][:, :]),
                                    reads=("p%d" % pb,),
                                    writes=("C:%d" % ch,))))
            if not down:
                pbb = 6 if ch % 2 == 0 else 7
                ops.append((lambda pc=pc, pkey=pkey, ch=ch, pbb=pbb:
                            tens(lambda e: e.matmul(
                                P[pbb][0:128, 0:1], pc, bvec[:, ch:ch + 1],
                                start=True, stop=True),
                                reads=(pkey, "bv:%d" % ch),
                                writes=("p%d" % pbb,))))
                ops.append((lambda ch=ch, pbb=pbb:
                            vec(lambda e: e.tensor_copy(
                                bvec[:, ch:ch + 1], P[pbb][0:128, 0:1]),
                                reads=("p%d" % pbb,),
                                writes=("bv:%d" % ch,))))
            return ops

        def lo_stages(src_key, down, wload=None):
            load_stage_chunk(dins[src_key], 0, 0)
            load_stage_chunk(dins[src_key], 1, 1)
            fill = []
            for base in range(0, 32, 2):
                holds = lo_chain_pair(src_key, base, down, fill)
                if base + 2 < 32:
                    load_stage_chunk(dins[src_key], base + 2, 0)
                    load_stage_chunk(dins[src_key], base + 3, 1)
                if wload:
                    for _ in range(4):
                        if wload:
                            wload.pop(0)()
                while fill:
                    fill.pop(0)()
                fill = (cupd_ops(base, holds[0], down)
                        + cupd_ops(base + 1, holds[1], down))
            while fill:
                fill.pop(0)()
            while wload:
                wload.pop(0)()

        def hi_stages(dv_key, bias_too):
            npair = 0
            for si in range(5):
                k = 1 << si
                for cbase in range(32):
                    if cbase & k:
                        continue
                    cp = cbase + k
                    db = 512 * (npair % 2)
                    dk = 4 * (npair % 2)
                    npair += 1
                    cols = [(si * 32 + cbase) * 2, (si * 32 + cbase) * 2 + 1,
                            (si * 32 + cp) * 2, (si * 32 + cp) * 2 + 1]
                    for q, col in enumerate(cols):
                        vec(lambda e, q=q, col=col, db=db: e.tensor_scalar(
                            dgb[:, db + 128 * q:db + 128 * q + 128],
                            ident_b[:, :],
                            dvv[:, col:col + 1], None, alu.mult, alu.bypass),
                            reads=("idb", "dvv"), writes=("dgb:%d" % (dk + q),))
                    for h in range(2):
                        a0 = 1024 * cbase + 512 * h
                        a1 = 1024 * cp + 512 * h
                        pr.mm_group(
                            [lambda e, a0=a0, h=h, db=db: e.matmul(
                                P[h][:, :], dgb[:, db:db + 128],
                                c_sb[:, a0:a0 + 512], start=True, stop=False),
                             lambda e, a1=a1, h=h, db=db: e.matmul(
                                P[h][:, :], dgb[:, db + 128:db + 256],
                                c_sb[:, a1:a1 + 512], start=False, stop=True)],
                            reads=("dgb:%d" % dk, "dgb:%d" % (dk + 1),
                                   "C:%d" % cbase, "C:%d" % cp),
                            writes=("p%d" % h,))
                        pr.mm_group(
                            [lambda e, a1=a1, h=h, db=db: e.matmul(
                                P[2 + h][:, :], dgb[:, db + 256:db + 384],
                                c_sb[:, a1:a1 + 512], start=True, stop=False),
                             lambda e, a0=a0, h=h, db=db: e.matmul(
                                P[2 + h][:, :], dgb[:, db + 384:db + 512],
                                c_sb[:, a0:a0 + 512], start=False, stop=True)],
                            reads=("dgb:%d" % (dk + 2), "dgb:%d" % (dk + 3),
                                   "C:%d" % cbase, "C:%d" % cp),
                            writes=("p%d" % (2 + h),))
                    if bias_too:
                        pr.mm_group(
                            [lambda e, db=db: e.matmul(
                                P[4][0:128, 0:1], dgb[:, db:db + 128],
                                bvec[:, cbase:cbase + 1],
                                start=True, stop=False),
                             lambda e, db=db: e.matmul(
                                P[4][0:128, 0:1], dgb[:, db + 128:db + 256],
                                bvec[:, cp:cp + 1], start=False, stop=True)],
                            reads=("dgb:%d" % dk, "dgb:%d" % (dk + 1),
                                   "bv:%d" % cbase, "bv:%d" % cp),
                            writes=("p4",))
                        pr.mm_group(
                            [lambda e, db=db: e.matmul(
                                P[4][0:128, 1:2], dgb[:, db + 256:db + 384],
                                bvec[:, cp:cp + 1], start=True, stop=False),
                             lambda e, db=db: e.matmul(
                                P[4][0:128, 1:2], dgb[:, db + 384:db + 512],
                                bvec[:, cbase:cbase + 1],
                                start=False, stop=True)],
                            reads=("dgb:%d" % (dk + 2), "dgb:%d" % (dk + 3),
                                   "bv:%d" % cbase, "bv:%d" % cp),
                            writes=("p4",))
                    for h in range(2):
                        sca(lambda e, cbase=cbase, h=h: e.activation(
                            c_sb[:, 1024 * cbase + 512 * h:
                                 1024 * cbase + 512 * h + 512], P[h][:, :],
                            AF.Identity),
                            reads=("p%d" % h,), writes=("C:%d" % cbase,))
                        sca(lambda e, cp=cp, h=h: e.activation(
                            c_sb[:, 1024 * cp + 512 * h:
                                 1024 * cp + 512 * h + 512], P[2 + h][:, :],
                            AF.Identity),
                            reads=("p%d" % (2 + h),), writes=("C:%d" % cp,))
                    if bias_too:
                        vec(lambda e, cbase=cbase: e.tensor_copy(
                            bvec[:, cbase:cbase + 1], P[4][0:128, 0:1]),
                            reads=("p4",), writes=("bv:%d" % cbase,))
                        vec(lambda e, cp=cp: e.tensor_copy(
                            bvec[:, cp:cp + 1], P[4][0:128, 1:2]),
                            reads=("p4",), writes=("bv:%d" % cp,))

        dma("scalar", dvv[:, :], dins["dv_u"][:, :], writes=("dvv",), ch="dvv")
        lo_stages("slo_u", down=False, wload=fill_R_ops(dins["wexpT"]))
        emit_x_load(0, 0)
        hi_stages("dv_u", bias_too=True)
        vec(lambda e: e.tensor_copy(bfull[:, :], bvec[:, :]),
            reads=tuple("bv:%d" % ch for ch in range(32)), writes=("bfull",))
        vec(lambda e: e.scalar_tensor_tensor(bfull[:, :], upb_f[:, :], 1.0,
                                             bfull[:, :], alu.mult, alu.add),
            reads=("upb", "bfull"), writes=("bfull",))

        for ch in range(32):
            for d in range(8):
                par = (ch * 8 + d) % 4
                pb = 4 + par
                tens(lambda e, ch=ch, d=d, pb=pb: e.matmul(
                    P[pb][0:128, 0:128],
                    c_sb[:, 1024 * ch + 128 * d:1024 * ch + 128 * d + 128],
                    ident_b[:, :], start=True, stop=True),
                    reads=("C:%d" % ch, "idb"), writes=("p%d" % pb,))
                if par < 2:
                    slot, skey = trn[:, 128 * par:128 * par + 128], \
                        "trn:%d" % par
                else:
                    slot, skey = ypp[:, 128 * (par - 2):128 * (par - 2) + 128], \
                        "ypp:%d" % (par - 2)
                if par % 2 == 0:
                    sca(lambda e, slot=slot, pb=pb: e.activation(
                        slot, P[pb][0:128, 0:128], AF.Identity),
                        reads=("p%d" % pb,), writes=(skey,))
                else:
                    vec(lambda e, slot=slot, pb=pb: e.tensor_copy(
                        slot, P[pb][0:128, 0:128]),
                        reads=("p%d" % pb,), writes=(skey,))
                dma("gpsimd" if par < 2 else "sync",
                    bass.AP(asp.tensor, (128 * d) * HID + 128 * ch,
                            [[HID, 128], [1, 128]]),
                    slot,
                    reads=(skey,), writes=("asp:%d:%d" % (ch, d),),
                    ch="asp%d" % par)

        fill_R(dins["wcon"])
        dma("scalar", dvv[:, :], dins["dv_d"][:, :], writes=("dvv",), ch="dvv")
        load_w(dins["wk"], w_a, "wa")
        load_w(dins["wv"], w_b, "wb")
        hi_stages("dv_d", bias_too=False)
        lo_stages("slo_dT", down=True)

        for i in range(16):
            for p, (dsrc, dspill) in enumerate(((dins["wq"], wqsp),
                                                (dins["wfc"], wfsp))):
                ch, h = divmod(i, 2)
                ceng = "vector" if p == 0 else "gpsimd"
                ktok = "kvkA" if p == 0 else "kvkB"
                extra = ("kvh:0", "kvh:1") if p == 0 else ()
                dma("sync" if p == 0 else "gpsimd",
                    wtmp[:, 512 * p:512 * p + 512],
                    dsrc[128 * ch:128 * ch + 128, 512 * h:512 * h + 512],
                    writes=("wtmp%d" % p,),
                    ch="wload0" if p == 0 else "wload1g")
                pr.op(ceng, lambda e, p=p: e.tensor_copy(
                    kv_k[:, 512 * p:512 * p + 512],
                    wtmp[:, 512 * p:512 * p + 512]),
                    reads=("wtmp%d" % p,), writes=(ktok,) + extra)
                dma("sync" if p == 0 else "gpsimd",
                    bass.AP(dspill.tensor, 1024 * ch + 512 * h,
                            [[8192, 128], [1, 512]]),
                    kv_k[:, 512 * p:512 * p + 512],
                    reads=(ktok,), writes=("wsp%d" % p,),
                    ch="wspo%d" % p)

        SB = {"s1": s1, "s2": s2, "s3": s3}
        def skey(src_key, c):
            return src_key % c if "%d" in src_key else src_key
        GUALL = tuple("gu:%d" % _c for _c in range(8))
        def wkeys(name, c=None):
            if c is None:
                base = "s1x" if name == "s1" else name
                return (name,) + tuple("%s:%d" % (base, _c) for _c in range(8))
            base = "s1x" if name == "s1" else name
            return ("%s:%d" % (base, _c if False else c),)
        def rkey(name, c):
            base = "s1x" if name == "s1" else name
            return "%s:%d" % (base, c)
        def emit_rows_and_bcast_r(sp_sb, rcol, rkeyn, hilo, ng):
            for g in range(ng):
                tens(lambda e, g=g: e.transpose(
                    P[4][0:1, 128 * g:128 * g + 128],
                    sp_sb[:, rcol + g:rcol + g + 1], ident_f[:, :]),
                    reads=(rkeyn,), writes=("p4",))
            vec(lambda e, ng=ng: e.tensor_copy(stats_sb[0:1, 0:128 * ng],
                                               P[4][0:1, 0:128 * ng]),
                reads=("p4",), writes=("stats",))
            for g in range(ng):
                tens(lambda e, g=g: e.matmul(
                    P[4][0:128, 128 * g:128 * g + 128], onesr_f[0:1, 0:128],
                    stats_sb[0:1, 128 * g:128 * g + 128],
                    start=True, stop=True),
                    reads=("onesrf", "stats"), writes=("p4",))

        def emit_rows_and_bcast_m(sp_sb, mcol, mkeyn, hilo, ng):
            for g in range(ng):
                tens(lambda e, g=g: e.transpose(
                    P[6][0:1, 128 * g:128 * g + 128],
                    sp_sb[:, mcol + g:mcol + g + 1], ident_f[:, :]),
                    reads=(mkeyn,), writes=("p6",))
            vec(lambda e, ng=ng: e.tensor_copy(stats_sb[0:1, 0:128 * ng],
                                               P[6][0:1, 0:128 * ng]),
                reads=("p6", "stats"), writes=("statsm", "stats"))
            for g in range(ng):
                tens(lambda e, g=g: e.matmul(
                    P[6][0:128, 128 * g:128 * g + 128], onesr_f[0:1, 0:128],
                    stats_sb[0:1, 128 * g:128 * g + 128],
                    start=True, stop=True),
                    reads=("onesrf", "statsm"), writes=("p6",))

        def emit_bcast_normalize(sp_sb, rcol, mcol, rkeyn, mkeyn,
                                 src_f32, src_key, outname,
                                 need_f32, hilo, affine, ng, use_bf_src,
                                 cast_n="s1"):
            nrm_eng = lambda c: "vector"
            emit_rows_and_bcast_r(sp_sb, rcol, rkeyn, hilo, ng)
            emit_rows_and_bcast_m(sp_sb, mcol, mkeyn, hilo, ng)
            if use_bf_src:
                ttile, tsrc_t = SB[outname], SB[cast_n]
                for c in range(NCHUNK):
                    pr.op(nrm_eng(c),
                          lambda e, c=c, ttile=ttile, tsrc_t=tsrc_t:
                          e.tensor_tensor(
                              ttile[:, T * c:T * c + T],
                              tsrc_t[:, T * c:T * c + T],
                              P[4][0:128, 0:T], alu.mult),
                          reads=(rkey(cast_n, c), "p4"),
                          writes=wkeys(outname, c))
            else:
                for c in range(NCHUNK):
                    pr.op(nrm_eng(c), lambda e, c=c: e.tensor_tensor(
                        u_t[:, T * c:T * c + T], src_f32[:, T * c:T * c + T],
                        P[4][0:128, 0:T], alu.mult),
                        reads=(skey(src_key, c), "p4", "gu:%d" % c),
                        writes=("gu:%d" % c,))
            if use_bf_src:
                out_bf = SB[outname]
                for c in range(NCHUNK):
                    pr.op(nrm_eng(c),
                          lambda e, c=c, out_bf=out_bf: e.tensor_tensor(
                              out_bf[:, T * c:T * c + T],
                              out_bf[:, T * c:T * c + T],
                              P[6][0:128, 0:T], alu.subtract),
                          reads=(rkey(outname, c), "p6"),
                          writes=wkeys(outname, c))
            else:
                for c in range(NCHUNK):
                    pr.op(nrm_eng(c), lambda e, c=c: e.tensor_tensor(
                        u_t[:, T * c:T * c + T], u_t[:, T * c:T * c + T],
                        P[6][0:128, 0:T], alu.subtract),
                        reads=("gu:%d" % c, "p6"), writes=("gu:%d" % c,))
            if affine:
                for c in range(NCHUNK):
                    sca(lambda e, c=c: e.activation(
                        u_t[:, T * c:T * c + T], u_t[:, T * c:T * c + T],
                        AF.Identity, bias=bet[:, c:c + 1],
                        scale=gam[:, c:c + 1]),
                        reads=("gu:%d" % c, "gam", "bet"),
                        writes=("gu:%d" % c,))
            if need_f32 and outname is not None:
                out_bf = SB[outname]
                for c in range(NCHUNK):
                    sca(lambda e, c=c, out_bf=out_bf: e.activation(
                        out_bf[:, T * c:T * c + T], u_t[:, T * c:T * c + T],
                        AF.Identity),
                        reads=("gu:%d" % c,), writes=wkeys(outname, c))

        def emit_ln_stats_chunk(src_f32, src_key, c, cast_n, sq_n,
                                cast_eng="vector", sq_eng="vector"):
            ct, qt = SB[cast_n], SB[sq_n]
            if cast_eng == "scalar":
                sca(lambda e, c=c, ct=ct: e.activation(
                    ct[:, T * c:T * c + T], src_f32[:, T * c:T * c + T],
                    AF.Identity),
                    reads=(skey(src_key, c),), writes=wkeys(cast_n, c))
            else:
                pr.op(cast_eng, lambda e, c=c, ct=ct: e.tensor_copy(
                    ct[:, T * c:T * c + T], src_f32[:, T * c:T * c + T]),
                    reads=(skey(src_key, c),), writes=wkeys(cast_n, c))
            pr.op(sq_eng, lambda e, c=c, ct=ct, qt=qt: e.tensor_tensor(
                qt[:, T * c:T * c + T], ct[:, T * c:T * c + T],
                ct[:, T * c:T * c + T], alu.mult),
                reads=(rkey(cast_n, c),), writes=wkeys(sq_n, c))

        def emit_ln_stats_mm(c, cast_n, sq_n):
            ct, qt = SB[cast_n], SB[sq_n]
            tens(lambda e, c=c, ct=ct: e.matmul(
                P[4][0:1, 0:T], onesc[:, 0:1], ct[:, T * c:T * c + T],
                start=(c == 0), stop=(c == 7), skip_group_check=True),
                reads=("ones", rkey(cast_n, c)), writes=("p4",))
            tens(lambda e, c=c, qt=qt: e.matmul(
                P[5][0:1, 0:T], onesc[:, 0:1], qt[:, T * c:T * c + T],
                start=(c == 0), stop=False, skip_group_check=True),
                reads=("ones", rkey(sq_n, c)), writes=("p5",))
            if c == 7:
                tens(lambda e: e.matmul(
                    P[5][0:1, 0:T], epsr[0:1, 0:1], onesr[0:1, 0:T],
                    start=False, stop=True, skip_group_check=True),
                    reads=("epsr", "onesr"), writes=("p5",))

        def emit_ln_head():
            ng = NG
            vec(lambda e: e.tensor_copy(stats_sb[0:1, 0:T], P[4][0:1, 0:T]),
                reads=("p4",), writes=("stats",))
            vec(lambda e: e.tensor_copy(stats_sb[32:33, 0:T], P[5][0:1, 0:T]),
                reads=("p5",), writes=("stats",))
            for g in range(ng):
                tens(lambda e, g=g: e.transpose(
                    P[4][0:128, g:g + 1],
                    stats_sb[0:1, 128 * g:128 * g + 128], ident_f[0:1, 0:1]),
                    reads=("stats",), writes=("p4",))
                tens(lambda e, g=g, ng=ng: e.transpose(
                    P[4][0:128, ng + g:ng + g + 1],
                    stats_sb[32:33, 128 * g:128 * g + 128],
                    ident_f[32:33, 32:33]),
                    reads=("stats",), writes=("p4",))
            vec(lambda e, ng=ng: e.tensor_copy(small[:, 0:2 * ng],
                                               P[4][0:128, 0:2 * ng]),
                reads=("p4",), writes=("sm0",))
            mu0, e20, t10, t20, r0, m20 = 8, 12, 16, 20, 24, 28
            vec(lambda e, ng=ng: e.tensor_scalar(
                small[:, mu0:mu0 + 2 * ng], small[:, 0:2 * ng], 1.0 / D, None,
                alu.mult, alu.bypass),
                reads=("sm0",), writes=("smmu", "sme2"))
            vec(lambda e, ng=ng: e.tensor_tensor(
                small[:, t10:t10 + ng], small[:, mu0:mu0 + ng],
                small[:, mu0:mu0 + ng], alu.mult),
                reads=("smmu",), writes=("smt1",))
            vec(lambda e, ng=ng: e.scalar_tensor_tensor(
                small[:, t20:t20 + ng], small[:, t10:t10 + ng], -1.0,
                small[:, e20:e20 + ng], alu.mult, alu.add),
                reads=("smt1", "sme2"), writes=("smv",))
            vec(lambda e, ng=ng: e.tensor_copy(
                small[:, t10:t10 + ng], small_i[:, t20:t20 + ng]),
                reads=("smv",), writes=("smt1",))
            vec(lambda e, ng=ng: e.tensor_scalar(
                small[:, t10:t10 + ng], small[:, t10:t10 + ng], -0.5,
                1597463007.0, alu.mult, alu.add),
                reads=("smt1",), writes=("smt1",))
            vec(lambda e, ng=ng: e.tensor_copy(
                small_i[:, r0:r0 + ng], small[:, t10:t10 + ng]),
                reads=("smt1",), writes=("smr",))
            for _ in range(NEWTON_ITERS):
                vec(lambda e, ng=ng: e.tensor_tensor(
                    small[:, t10:t10 + ng], small[:, r0:r0 + ng],
                    small[:, r0:r0 + ng], alu.mult),
                    reads=("smr",), writes=("smt1",))
                vec(lambda e, ng=ng: e.scalar_tensor_tensor(
                    small[:, m20:m20 + ng], small[:, t10:t10 + ng], -0.5,
                    small[:, t20:t20 + ng], alu.mult, alu.mult),
                    reads=("smt1", "smv"), writes=("smt2",))
                vec(lambda e, ng=ng: e.scalar_tensor_tensor(
                    small[:, r0:r0 + ng], small[:, m20:m20 + ng], 1.5,
                    small[:, r0:r0 + ng], alu.add, alu.mult),
                    reads=("smt2", "smr"), writes=("smr",))
            vec(lambda e, ng=ng: e.tensor_tensor(
                small[:, m20:m20 + ng], small[:, mu0:mu0 + ng],
                small[:, r0:r0 + ng], alu.mult),
                reads=("smmu", "smr"), writes=("smm2",))

        def emit_ln_tail2(src_f32, src_key, outname, need_f32, hilo,
                          affine, cast_n="s1"):
            r0, m20 = 24, 28
            emit_bcast_normalize(small, r0, m20, "smr", "smm2",
                                 src_f32, src_key, outname,
                                 need_f32, hilo, affine, NG,
                                 use_bf_src=not (need_f32 or affine),
                                 cast_n=cast_n)

        def emit_ln_tail(src_f32, src_key, outname, need_f32, hilo,
                         affine, cast_n="s1", sq_n="s2"):
            emit_ln_head()
            emit_ln_tail2(src_f32, src_key, outname, need_f32, hilo,
                          affine, cast_n=cast_n)

        def emit_ln(src_f32, src_key, outname, need_f32, hilo,
                    affine, cast_n="s1", sq_n="s2"):
            for c in range(NCHUNK):
                emit_ln_stats_chunk(src_f32, src_key, c, cast_n, sq_n)
                emit_ln_stats_mm(c, cast_n, sq_n)
            emit_ln_tail(src_f32, src_key, outname, need_f32, hilo,
                         affine, cast_n, sq_n)

        aqueues = ["sync", "gpsimd", "scalar"]

        def emit_ffn(srcname="s3", post_oc=None):
            src = SB[srcname]
            for o in range(HCHUNK):
                par = o % 3
                q = aqueues[par]
                dma(q, bass.AP(a_st[par], 0, [[1024, 128], [128, 8], [1, 128]]),
                    bass.AP(asp.tensor, 128 * o,
                            [[HID, 128], [128 * HID, 8], [1, 128]]),
                    reads=tuple("asp:%d:%d" % (o, d) for d in range(8)),
                    writes=("ast:%d" % par,), ch="ast%d" % par)
                pr.mm_group(
                    [(lambda e, d=d, par=par, src=src: e.matmul(
                        P[par][:, 0:T], a_st[par][:, 128 * d:128 * d + 128],
                        src[:, T * d:T * d + T],
                        start=(d == 0), stop=(d == 7)))
                     for d in range(8)],
                    reads=("ast:%d" % par,), writes=("p%d" % par,),
                    reads_each=[(rkey(srcname, d),) for d in range(8)])
                sca(lambda e, o=o, par=par: e.activation(
                    g_t[:, T * o:T * o + T], P[par][:, 0:T], AF.Gelu,
                    bias=bfull[:, o:o + 1]),
                    reads=("p%d" % par, "bfull"), writes=("gu:%d" % (o // 2),))
            for oc in range(8):
                par = oc % 4
                pr.mm_group(
                    [(lambda e, k=k, oc=oc, par=par: e.matmul(
                        P[par][:, 0:T],
                        c_sb[:, 1024 * k + 128 * oc:1024 * k + 128 * oc + 128],
                        g_t[:, T * k:T * k + T],
                        start=(k == 0), stop=(k == HCHUNK - 1)))
                     for k in range(HCHUNK)],
                    reads=ALLC, writes=("p%d" % par,),
                    reads_each=[("gu:%d" % (k // 2),) for k in range(HCHUNK)])
                vec(lambda e, oc=oc, par=par: e.scalar_tensor_tensor(
                    x_t[:, T * oc:T * oc + T], P[par][:, 0:T],
                    cch[:, oc:oc + 1], x_t[:, T * oc:T * oc + T],
                    alu.add, alu.add),
                    reads=("p%d" % par, "cch", "x:%d" % oc),
                    writes=("x:%d" % oc,))
                if post_oc is not None:
                    post_oc(oc)

        def emit_kv_group(iseq, it, g):
            if True:
                for half in range(2):
                    kp = (2 * g + half) % 2
                    pr.mm_group(
                        [(lambda e, d=d, g=g, half=half, kp=kp: e.matmul(
                            P[kp][:, 0:512],
                            s3[:, T * d + 128 * g:T * d + 128 * g + 128],
                            w_a[:, 1024 * d + 512 * half:
                                1024 * d + 512 * half + 512],
                            start=(d == 0), stop=(d == 7)))
                         for d in range(8)],
                        reads=ALLWA, writes=("p%d" % kp,),
                        reads_each=[("s3:%d" % d,) for d in range(8)])
                    sca(lambda e, half=half, kp=kp: e.activation(
                        u_t[:, 512 * half:512 * half + 512],
                        P[kp][:, 0:512], AF.Tanh, scale=0.5),
                        reads=("p%d" % kp, "gu:%d" % half),
                        writes=("gu:%d" % half,)),
                    vec(lambda e, half=half: e.tensor_scalar(
                        kv_k[:, 512 * half:512 * half + 512],
                        u_t[:, 512 * half:512 * half + 512],
                        0.5, 0.5, alu.mult, alu.add),
                        reads=("gu:%d" % half,),
                        writes=("kvk", "kvkA", "kvkB"))
                    pr.mm_group(
                        [(lambda e, d=d, g=g, half=half, kp=kp: e.matmul(
                            P[2 + kp][:, 0:512],
                            s3[:, T * d + 128 * g:T * d + 128 * g + 128],
                            w_b[:, 1024 * d + 512 * half:
                                1024 * d + 512 * half + 512],
                            start=(d == 0), stop=(d == 7)))
                         for d in range(8)],
                        reads=ALLWB, writes=("p%d" % (2 + kp),),
                        reads_each=[("s3:%d" % d,) for d in range(8)])
                    vec(lambda e, half=half, kp=kp: e.tensor_copy(
                        bass.AP(kv_v, 129 * 4 * half,
                                [[8 * 129, 128], [129, 4], [1, 128]]),
                        bass.AP(P[2 + kp], 0,
                                [[512, 128], [128, 4], [1, 128]])),
                        reads=("p%d" % (2 + kp),), writes=("kvv",))
                first = (it == 0 and g == 0)
                banks = [(0, [0, 1, 2]), (1, [3, 4, 5]), (2, [6, 7])]
                for bank, pairs in banks:
                    pr.mm_group(
                        [(lambda e, j=j, bank=bank, st0=(j == pairs[0]),
                          st1=(j == pairs[-1]): e.matmul(
                              P[5 + bank][:, 129 * (j - 3 * bank):
                                          129 * (j - 3 * bank) + 129],
                              kv_k[:, 128 * j:128 * j + 128],
                              kv_v[:, 129 * j:129 * j + 129],
                              start=st0, stop=st1))
                         for j in pairs],
                        reads=("kvk", "kvv", "kvv1"),
                        writes=("p%d" % (5 + bank),))
                    w0 = 387 * bank
                    wd = 129 * len(pairs)
                    if first:
                        vec(lambda e, bank=bank, w0=w0, wd=wd:
                            e.tensor_copy(kvacc[:, w0:w0 + wd],
                                          P[5 + bank][:, 0:wd]),
                            reads=("p%d" % (5 + bank), "stg", "stgf"),
                            writes=("kvacc",))
                    else:
                        vec(lambda e, bank=bank, w0=w0, wd=wd:
                            e.tensor_tensor(kvacc[:, w0:w0 + wd],
                                            kvacc[:, w0:w0 + wd],
                                            P[5 + bank][:, 0:wd], alu.add),
                            reads=("p%d" % (5 + bank), "kvacc"),
                            writes=("kvacc",))

        def emit_kv(iseq, it):
            for g in range(NG):
                emit_kv_group(iseq, it, g)

        def emit_kv_evacs(iseq):
            for j in range(8):
                base = (iseq * 8 + j) * 128
                vec(lambda e, j=j, base=base: e.tensor_copy(
                    kvbd[0:64, base:base + 64],
                    kvacc[0:64, 129 * j:129 * j + 64]),
                    reads=("kvacc",), writes=("kvbd",))
                vec(lambda e, j=j, base=base: e.tensor_copy(
                    kvbd[64:128, base + 64:base + 128],
                    kvacc[64:128, 129 * j + 64:129 * j + 128]),
                    reads=("kvacc",), writes=("kvbd",))
                kb = (iseq * 8 + j) * 16
                vec(lambda e, kb=kb, j=j: e.tensor_copy(
                    ksc[0:64, kb + 2 * j:kb + 2 * j + 1],
                    kvacc[0:64, 129 * j + 128:129 * j + 129]),
                    reads=("kvacc",), writes=("ksc",))
                vec(lambda e, kb=kb, j=j: e.tensor_copy(
                    ksc[64:128, kb + 2 * j + 1:kb + 2 * j + 2],
                    kvacc[64:128, 129 * j + 128:129 * j + 129]),
                    reads=("kvacc",), writes=("ksc",))

        pending_kv = None
        for iseq in range(n_seq):
            for it in range(n_tiles):
                if (iseq, it) != (0, 0):
                    emit_x_load(iseq, it)
                if pending_kv is not None:
                    ps, pt = pending_kv
                    emit_kv_group(ps, pt, 0)
                    for c in range(4):
                        emit_ln_stats_chunk(x_t, "x:%d", c, "s1", "s2",
                                            cast_eng="gpsimd",
                                            sq_eng="gpsimd")
                    emit_kv_group(ps, pt, 1)
                    for c in range(4, 8):
                        emit_ln_stats_chunk(x_t, "x:%d", c, "s1", "s2",
                                            cast_eng="gpsimd",
                                            sq_eng="gpsimd")
                    emit_kv_group(ps, pt, 2)
                    for c in range(NCHUNK):
                        emit_ln_stats_mm(c, "s1", "s2")
                    emit_ln_head()
                    emit_kv_group(ps, pt, 3)
                    if pt == n_tiles - 1:
                        emit_kv_evacs(ps)
                    emit_ln_tail2(x_t, "x:%d", "s2", False, False, False)
                else:
                    emit_ln(x_t, "x:%d", "s2", False, False, False)

                def ln2_cb(oc):
                    emit_ln_stats_chunk(x_t, "x:%d", oc, "s1", "s2")
                    if oc > 0:
                        emit_ln_stats_mm(oc - 1, "s1", "s2")
                emit_ffn("s2", post_oc=ln2_cb)
                emit_ln_stats_mm(7, "s1", "s2")
                dma("gpsimd",
                    bass.AP(x1sp.tensor, iseq * D * seq_len + it * T,
                            [[seq_len, 128], [128 * seq_len, 8], [1, T]]),
                    bass.AP(x_t, 0, [[NCHUNK * T, 128], [T, 8], [1, T]]),
                    reads=tuple("x:%d" % c for c in range(NCHUNK)),
                    writes=("x1sp:%d:%d" % (iseq, it),),
                    ch="spill")
                emit_ln_tail(x_t, "x:%d", "s3", True, True, False)
                for c in range(NCHUNK):
                    pr.op("vector" if c % 2 == 0 else "gpsimd",
                          lambda e, c=c: e.tensor_tensor(
                              s1[:, T * c:T * c + T], u_t[:, T * c:T * c + T],
                              s3[:, T * c:T * c + T], alu.subtract),
                          reads=("gu:%d" % c, "s3:%d" % c),
                          writes=wkeys("s1", c))
                dma("sync",
                    bass.AP(xn2hi.tensor, iseq * D * seq_len + it * T,
                            [[seq_len, 128], [128 * seq_len, 8], [1, T]]),
                    bass.AP(s3, 0, [[NCHUNK * T, 128], [T, 8], [1, T]]),
                    reads=tuple("s3:%d" % c for c in range(8)),
                    writes=("xh:%d:%d" % (iseq, it),), ch="xh")
                dma("sync",
                    bass.AP(xn2lo.tensor, iseq * D * seq_len + it * T,
                            [[seq_len, 128], [128 * seq_len, 8], [1, T]]),
                    bass.AP(s1, 0, [[NCHUNK * T, 128], [T, 8], [1, T]]),
                    reads=tuple("s1x:%d" % c for c in range(8)),
                    writes=("xl:%d:%d" % (iseq, it),), ch="xl")
                pending_kv = (iseq, it)
        emit_kv(*pending_kv)
        emit_kv_evacs(n_seq - 1)

        for half in range(2):
            wa_toks = tuple("wa:%d" % c for c in range(4 * half, 4 * half + 4))
            wb_toks = tuple("wb:%d" % c for c in range(4 * half, 4 * half + 4))
            dma("sync",
                bass.AP(w_a, 4096 * half, [[8192, 128], [1, 4096]]),
                bass.AP(wqsp.tensor, 4096 * half, [[8192, 128], [1, 4096]]),
                reads=("wsp0",) + wa_toks, writes=wa_toks, ch="wrl0")
            dma("gpsimd",
                bass.AP(w_b, 4096 * half, [[8192, 128], [1, 4096]]),
                bass.AP(wfsp.tensor, 4096 * half, [[8192, 128], [1, 4096]]),
                reads=("wsp1",) + wb_toks, writes=wb_toks, ch="wrl1")

        for iseq in range(n_seq):
            for it in range(n_tiles):
                dma("scalar",
                    bass.AP(x_t, 0, [[NCHUNK * T, 128], [T, 8], [1, T]]),
                    bass.AP(x1sp.tensor, iseq * D * seq_len + it * T,
                            [[seq_len, 128], [128 * seq_len, 8], [1, T]]),
                    reads=("x1sp:%d:%d" % (iseq, it),)
                    + tuple("x:%d" % c for c in range(NCHUNK)),
                    writes=tuple("x:%d" % c for c in range(NCHUNK)),
                    ch="x")
                dma("sync",
                    bass.AP(s3, 0, [[NCHUNK * T, 128], [T, 8], [1, T]]),
                    bass.AP(xn2hi.tensor, iseq * D * seq_len + it * T,
                            [[seq_len, 128], [128 * seq_len, 8], [1, T]]),
                    reads=("s3", "xh:%d:%d" % (iseq, it))
                    + tuple("s3:%d" % c for c in range(8)),
                    writes=wkeys("s3"), ch="xh")
                dma("sync",
                    bass.AP(s1, 0, [[NCHUNK * T, 128], [T, 8], [1, T]]),
                    bass.AP(xn2lo.tensor, iseq * D * seq_len + it * T,
                            [[seq_len, 128], [128 * seq_len, 8], [1, T]]),
                    reads=("s1", "xl:%d:%d" % (iseq, it))
                    + tuple("s1x:%d" % c for c in range(8)),
                    writes=wkeys("s1"), ch="xl")
                for c in range(NCHUNK):
                    pr.op("vector" if c % 2 == 0 else "gpsimd",
                          lambda e, c=c: e.tensor_tensor(
                              u_t[:, T * c:T * c + T], s3[:, T * c:T * c + T],
                              s1[:, T * c:T * c + T], alu.add),
                          reads=("s3:%d" % c, "s1x:%d" % c, "gu:%d" % c),
                          writes=("gu:%d" % c,))
                for oc in range(8):
                    par = oc % 4
                    pr.mm_group(
                        [(lambda e, d=d, oc=oc, par=par: e.matmul(
                            P[par][:, 0:T],
                            w_a[:, 1024 * d + 128 * oc:
                                1024 * d + 128 * oc + 128],
                            s3[:, T * d:T * d + T],
                            start=(d == 0), stop=(d == 7)))
                         for d in range(8)],
                        reads=ALLWA, writes=("p%d" % par,),
                        reads_each=[("s3:%d" % d,) for d in range(8)])
                    sca(lambda e, oc=oc, par=par: e.activation(
                        s2[:, T * oc:T * oc + T], P[par][:, 0:T],
                        AF.Tanh, scale=0.5),
                        reads=("p%d" % par,), writes=("s2",) + wkeys("s2", oc))
                    vec(lambda e, oc=oc: e.tensor_scalar(
                        s2[:, T * oc:T * oc + T], s2[:, T * oc:T * oc + T],
                        0.5, 0.5, alu.mult, alu.add),
                        reads=wkeys("s2", oc), writes=wkeys("s2", oc))
                kb0 = iseq * 8 * 16
                pr.mm_group(
                    [(lambda e, d=d, kb0=kb0: e.matmul(
                        P[4][0:16, 0:T],
                        ksc[:, kb0 + 16 * d:kb0 + 16 * d + 16],
                        s2[:, T * d:T * d + T],
                        start=(d == 0), stop=(d == 7)))
                     for d in range(8)],
                    reads=("ksc",), writes=("p4",),
                    reads_each=[("s2:%d" % d,) for d in range(8)])
                vec(lambda e: e.tensor_scalar(
                    den_sb[0:16, 0:T], P[4][0:16, 0:T], 1e-8, None,
                    alu.add, alu.bypass),
                    reads=("p4",), writes=("den",))
                warm(4)
                for g in range(NG):
                    tens(lambda e, g=g: e.transpose(
                        P[4][0:128, 16 * g:16 * g + 16],
                        den_sb[0:16, 128 * g:128 * g + 128],
                        ident_f[0:16, 0:16]),
                        reads=("den",), writes=("p4",))
                warm(6)
                vec(lambda e: e.tensor_copy(dsp[:, 0:16 * NG],
                                            P[4][0:128, 0:16 * NG]),
                    reads=("p4",), writes=("dsp",))
                vec(lambda e: e.reciprocal(dsp[:, 0:16 * NG],
                                           dsp[:, 0:16 * NG]),
                    reads=("dsp",), writes=("dsp",))
                for g in range(NG):
                    tens(lambda e, g=g: e.transpose(
                        P[4][0:16, 128 * g:128 * g + 128],
                        dsp[:, 16 * g:16 * g + 16], ident_f[:, :]),
                        reads=("dsp",), writes=("p4",))
                vec(lambda e: e.tensor_copy(rdh_sb[0:16, 0:T],
                                            P[4][0:16, 0:T]),
                    reads=("p4",), writes=("rdh",))
                vec(lambda e: e.tensor_tensor(
                    rdl_sb[0:16, 0:T], P[4][0:16, 0:T], rdh_sb[0:16, 0:T],
                    alu.subtract),
                    reads=("p4", "rdh"), writes=("rdl",))
                for j in range(8):
                    pn = P[j % 2]
                    pnk = "p%d" % (j % 2)
                    pe = P[5 + (j % 2)]
                    pek = "p%d" % (5 + (j % 2))
                    kvb = (iseq * 8 + j) * 128
                    tens(lambda e, pn=pn, kvb=kvb, j=j: e.matmul(
                        pn[:, 0:T], kvbd[:, kvb:kvb + 128],
                        s2[:, T * j:T * j + T], start=True, stop=True),
                        reads=("kvbd", "s2:%d" % j), writes=(pnk,))
                    pr.mm_group(
                        [lambda e, pe=pe, j=j: e.matmul(
                            pe[:, 0:T], ej_b[:, 128 * j:128 * j + 128],
                            rdh_sb[0:16, 0:T], start=True, stop=False),
                         lambda e, pe=pe, j=j: e.matmul(
                            pe[:, 0:T], ej_b[:, 128 * j:128 * j + 128],
                            rdl_sb[0:16, 0:T], start=False, stop=True)],
                        reads=("ej", "rdh", "rdl"), writes=(pek,))
                    sca(lambda e, pn=pn: e.activation(
                        rdb_sb[:, 0:T], pn[:, 0:T], AF.Identity),
                        reads=(pnk,), writes=("rdb", "wtmp0", "wtmp1"))
                    vec(lambda e, pe=pe, j=j: e.tensor_tensor(
                        s1[:, T * j:T * j + T], rdb_sb[:, 0:T],
                        pe[:, 0:T], alu.mult),
                        reads=(pek, "rdb"), writes=("s1x:%d" % j,))
                for oc in range(8):
                    par = oc % 4
                    pr.mm_group(
                        [(lambda e, d=d, oc=oc, par=par: e.matmul(
                            P[par][:, 0:T],
                            w_b[:, 1024 * d + 128 * oc:
                                1024 * d + 128 * oc + 128],
                            s1[:, T * d:T * d + T],
                            start=(d == 0), stop=(d == 7)))
                         for d in range(8)],
                        reads=ALLWB, writes=("p%d" % par,),
                        reads_each=[("s1x:%d" % d,) for d in range(8)])
                    vec(lambda e, oc=oc, par=par: e.tensor_tensor(
                        u_t[:, T * oc:T * oc + T], P[par][:, 0:T],
                        u_t[:, T * oc:T * oc + T], alu.add),
                        reads=("p%d" % par, "gu:%d" % oc),
                        writes=("gu:%d" % oc,))
                    emit_ln_stats_chunk(u_t, "gu:%d", oc, "s2", "s3")
                    if oc > 0:
                        emit_ln_stats_mm(oc - 1, "s2", "s3")
                emit_ln_stats_mm(7, "s2", "s3")
                emit_ln_tail(u_t, "gu:%d", None, False, True, True,
                             cast_n="s2", sq_n="s3")
                warm(26)
                for oc in range(8):
                    pr.op("vector" if oc % 2 == 0 else "gpsimd",
                          lambda e, oc=oc: e.tensor_tensor(
                              x_t[:, T * oc:T * oc + T],
                              x_t[:, T * oc:T * oc + T],
                              u_t[:, T * oc:T * oc + T], alu.add),
                          reads=("x:%d" % oc, "gu:%d" % oc),
                          writes=("x:%d" % oc,))
                emit_ln(x_t, "x:%d", "s2", False, False, False,
                        cast_n="s1", sq_n="s3")
                warm(14)
                emit_ffn("s2")
                dma("gpsimd",
                    bass.AP(dout.tensor, iseq * D * seq_len + it * T,
                            [[seq_len, 128], [128 * seq_len, 8], [1, T]]),
                    bass.AP(x_t, 0, [[NCHUNK * T, 128], [T, 8], [1, T]]),
                    reads=tuple("x:%d" % c for c in range(NCHUNK)),
                    writes=("outd",), ch="out")

        pr.raw_wait("gpsimd", "dch_out", pr.cnt["dch_out"])
        pr.realize(blk)
    return nc



def kernel(**inputs):
    x = np.asarray(inputs["L0_attn_inputs"], dtype=np.float32)
    consts = host_constants(inputs)
    nc = build_program(n_seq=2, seq_len=4096, T=512)
    in_maps = []
    for core in range(8):
        xs = x[2 * core:2 * core + 2]
        xfm = np.ascontiguousarray(xs.transpose(0, 2, 1)).reshape(2 * D, 4096)
        m = {"x": xfm}
        m.update(consts)
        in_maps.append(m)
    res = run_bass_kernel_spmd(nc, in_maps, core_ids=list(range(8)))
    outs = []
    for core in range(8):
        o = np.asarray(res.results[core]["out"]).reshape(2, D, 4096)
        outs.append(o.transpose(0, 2, 1))
    return np.ascontiguousarray(np.concatenate(outs, axis=0), dtype=np.float32)


# version 23
# speedup vs baseline: 1.2128x; 1.2128x over previous
import numpy as np
from contextlib import ExitStack

import concourse.bass as bass
import concourse.mybir as mybir
from concourse.bass_utils import run_bass_kernel_spmd
from concourse.alu_op_type import AluOpType as alu

F32 = mybir.dt.float32
BF16 = mybir.dt.bfloat16
AF = mybir.ActivationFunctionType

D = 1024
HID = 4096
NCHUNK = 8
HCHUNK = 32
LN_EPS = 1e-5
NEWTON_ITERS = 2
DEBUG_TAPS = False


def _stage_chunk(w_s, s, c, transpose=False):
    S = 2 ** s
    M = np.zeros((128, 128), dtype=np.float32)
    w = np.asarray(w_s, dtype=np.float32).reshape(HID // (2 * S), S, 2, 2)
    for p in range(128):
        i = 128 * c + p
        if i & S:
            continue
        n1 = i // (2 * S)
        j = i % S
        M[p, p] = w[n1, j, 0, 0]
        M[p, p + S] = w[n1, j, 0, 1] if not transpose else w[n1, j, 1, 0]
        M[p + S, p] = w[n1, j, 1, 0] if not transpose else w[n1, j, 0, 1]
        M[p + S, p + S] = w[n1, j, 1, 1]
    return M


def _diag_vecs(w_s, s, c, transpose=False):
    S = 2 ** s
    w = np.asarray(w_s, dtype=np.float32).reshape(HID // (2 * S), S, 2, 2)
    vs = np.zeros(128, dtype=np.float32)
    vp = np.zeros(128, dtype=np.float32)
    for p in range(128):
        i = 128 * c + p
        if not (i & S):
            n1 = i // (2 * S)
            j = i % S
            vs[p] = w[n1, j, 0, 0]
            vp[p] = w[n1, j, 0, 1] if not transpose else w[n1, j, 1, 0]
        else:
            a = i - S
            n1 = a // (2 * S)
            j = a % S
            vs[p] = w[n1, j, 1, 1]
            vp[p] = w[n1, j, 1, 0] if not transpose else w[n1, j, 0, 1]
    return vs, vp


def host_constants(inp):
    c = {}
    c["wq"] = np.ascontiguousarray(inp["L0_attn_W_Q"], dtype=np.float32)
    c["wk"] = np.ascontiguousarray(inp["L0_attn_W_K"], dtype=np.float32)
    c["wv"] = np.ascontiguousarray(inp["L0_attn_W_V"], dtype=np.float32)
    c["wfc"] = np.ascontiguousarray(inp["L0_attn_W_fc"], dtype=np.float32)
    c["gamma"] = np.ascontiguousarray(
        np.asarray(inp["L0_attn_gamma"], np.float32).reshape(8, 128).T)
    c["beta"] = np.ascontiguousarray(
        np.asarray(inp["L0_attn_beta"], np.float32).reshape(8, 128).T)
    c["wexpT"] = np.ascontiguousarray(
        np.asarray(inp["L0_ffn_W_exp"], np.float32).T)
    c["wcon"] = np.ascontiguousarray(inp["L0_ffn_W_con"], dtype=np.float32)
    c["bexp"] = np.ascontiguousarray(
        np.asarray(inp["L0_ffn_b_exp"], np.float32).reshape(32, 128).T)
    c["upbias"] = np.ascontiguousarray(
        np.asarray(inp["L0_ffn_up_bias"], np.float32).reshape(32, 128).T)
    c["bcon"] = np.ascontiguousarray(
        np.asarray(inp["L0_ffn_b_con"], np.float32).reshape(8, 128).T)
    c["dbias"] = np.ascontiguousarray(
        np.asarray(inp["L0_ffn_down_bias"], np.float32).reshape(8, 128).T)

    upw = np.asarray(inp["L0_ffn_up_weights"], np.float32)
    dnw = np.asarray(inp["L0_ffn_down_weights"], np.float32)
    slo_u = np.zeros((7, 32, 128, 128), dtype=np.float32)
    slo_dT = np.zeros((7, 32, 128, 128), dtype=np.float32)
    for s in range(7):
        for ch in range(32):
            slo_u[s, ch] = _stage_chunk(upw[s], s, ch)
            slo_dT[s, ch] = _stage_chunk(dnw[s], s, ch, transpose=True)
    c["slo_u"] = slo_u.reshape(7 * 32 * 128, 128)
    c["slo_dT"] = slo_dT.reshape(7 * 32 * 128, 128)
    dv_u = np.zeros((128, 5 * 32 * 2), dtype=np.float32)
    dv_d = np.zeros((128, 5 * 32 * 2), dtype=np.float32)
    for si in range(5):
        s = si + 7
        for ch in range(32):
            vs_u, vp_u = _diag_vecs(upw[s], s, ch)
            dv_u[:, (si * 32 + ch) * 2 + 0] = vs_u
            dv_u[:, (si * 32 + ch) * 2 + 1] = vp_u
            vs_d, vp_d = _diag_vecs(dnw[s], s, ch, transpose=True)
            dv_d[:, (si * 32 + ch) * 2 + 0] = vs_d
            dv_d[:, (si * 32 + ch) * 2 + 1] = vp_d
    c["dv_u"] = dv_u
    c["dv_d"] = dv_d
    c["ident"] = np.eye(128, dtype=np.float32)
    ej = np.zeros((16, 8 * 128), dtype=np.float32)
    for j in range(8):
        ej[2 * j, 128 * j:128 * j + 64] = 1.0
        ej[2 * j + 1, 128 * j + 64:128 * j + 128] = 1.0
    c["ej"] = ej
    return c



ENGINES = ("tensor", "vector", "scalar", "sync", "gpsimd")


class Prog:
    def __init__(self, nc, ctx):
        self.nc = nc
        self.stream = {e: [] for e in ENGINES}
        self.sem = {}
        self.cnt = {}
        for e in ENGINES:
            self.sem[e] = ctx.enter_context(nc.semaphore("sem_" + e))
            self.cnt[e] = 0
        self.ctx = ctx
        self.wr = {}
        self.rd = {}
        self.floor = {e: {} for e in ENGINES}

    def _waits_for(self, eng, reads, writes):
        toks = []
        for r in reads:
            t = self.wr.get(r)
            if t:
                toks.append(t)
        for r in writes:
            t = self.wr.get(r)
            if t:
                toks.append(t)
            toks.extend(self.rd.get(r, ()))
        need = {}
        for s, v in toks:
            if v > need.get(s, -1):
                need[s] = v
        fl = self.floor[eng]
        out = []
        for s, v in sorted(need.items()):
            if fl.get(s, 0) >= v:
                continue
            assert v <= self.cnt[s], (eng, s, v, self.cnt[s])
            out.append((s, v))
            fl[s] = v
        return out

    def dma_ch(self, ch):
        name = "dch_" + ch
        if name not in self.sem:
            self.sem[name] = self.ctx.enter_context(nc_sem(self.nc, name))
            self.cnt[name] = 0
        return name

    def op(self, eng, build, reads=(), writes=(), dma=False, inc=True,
           ch=None):
        semname = self.dma_ch(ch) if dma else eng
        waits = self._waits_for(eng, reads, writes)
        tok = None
        if inc:
            self.cnt[semname] += 16 if dma else 1
            tok = (semname, self.cnt[semname])
        semh = self.sem[semname]
        amount = 16 if dma else 1
        sems = self.sem

        def closure(e, waits=waits, build=build, tok=tok):
            for s, v in waits:
                e.wait_ge(sems[s], v)
            instr = build(e)
            if tok is not None:
                instr.then_inc(semh, amount)
        self.stream[eng].append(closure)
        if tok is not None:
            for r in reads:
                self.rd.setdefault(r, []).append(tok)
            for r in writes:
                self.wr[r] = tok
                self.rd[r] = []
        return tok

    def mm_group(self, builders, reads, writes, reads_each=None):
        n = len(builders)
        for i, b in enumerate(builders[:-1]):
            r = reads + (tuple(reads_each[i]) if reads_each else ())
            self.op("tensor", b, reads=r, writes=writes, inc=False)
        r = reads + (tuple(reads_each[n - 1]) if reads_each else ())
        return self.op("tensor", builders[-1], reads=r, writes=writes)

    def raw_wait(self, eng, semname, value):
        if value <= 0:
            return
        fl = self.floor[eng]
        if fl.get(semname, 0) >= value:
            return
        fl[semname] = value
        sems = self.sem

        def closure(e):
            e.wait_ge(sems[semname], value)
        self.stream[eng].append(closure)

    def realize(self, blk):
        for e in ENGINES:
            ops = self.stream[e]

            def runner(eng, ops=ops):
                for f in ops:
                    f(eng)
            getattr(blk, e)(runner)



def nc_sem(nc, name):
    return nc.semaphore(name)


def build_program(n_seq=2, seq_len=4096, T=512):
    n_tiles = seq_len // T
    NG = T // 128
    nc = bass.Bass("TRN2", target_bir_lowering=False)

    dx = nc.dram_tensor("x", [n_seq * D, seq_len], F32,
                        kind="ExternalInput").ap()
    dout = nc.dram_tensor("out", [n_seq * D, seq_len], F32,
                          kind="ExternalOutput").ap()
    dins = {}
    for name, shape in [
        ("wq", [D, D]), ("wk", [D, D]), ("wv", [D, D]), ("wfc", [D, D]),
        ("gamma", [128, 8]), ("beta", [128, 8]),
        ("wexpT", [HID, D]), ("wcon", [HID, D]),
        ("bexp", [128, 32]), ("upbias", [128, 32]),
        ("bcon", [128, 8]), ("dbias", [128, 8]),
        ("slo_u", [7 * 32 * 128, 128]), ("slo_dT", [7 * 32 * 128, 128]),
        ("dv_u", [128, 5 * 32 * 2]), ("dv_d", [128, 5 * 32 * 2]),
        ("ident", [128, 128]), ("ej", [16, 8 * 128]),
    ]:
        dins[name] = nc.dram_tensor(name, shape, F32, kind="ExternalInput").ap()
    asp = nc.dram_tensor("asp", [D, HID], BF16).ap()
    wqsp = nc.dram_tensor("wqsp", [128, 8 * 1024], BF16).ap()
    wfsp = nc.dram_tensor("wfsp", [128, 8 * 1024], BF16).ap()
    x1sp = nc.dram_tensor("x1sp", [n_seq * D, seq_len], F32).ap()
    xn2hi = nc.dram_tensor("xn2hi", [n_seq * D, seq_len], BF16).ap()
    xn2lo = nc.dram_tensor("xn2lo", [n_seq * D, seq_len], BF16).ap()
    if DEBUG_TAPS:
        dbg1 = nc.dram_tensor("dbg1", [n_seq * D, seq_len], F32).ap()
        dbg2 = nc.dram_tensor("dbg2", [n_seq * D, seq_len], F32).ap()
        dbg3 = nc.dram_tensor("dbg3", [n_seq * D, seq_len], F32).ap()
        dbg4 = nc.dram_tensor("dbg4", [n_seq * D, seq_len], F32).ap()

    ctx = ExitStack()
    with ctx:
        S = lambda name, shape, dt: ctx.enter_context(
            nc.sbuf_tensor(name, shape, dt))
        c_sb = S("c_sb", [128, HCHUNK * 1024], BF16)
        w_a = S("w_a", [128, NCHUNK * 1024], BF16)
        w_b = S("w_b", [128, NCHUNK * 1024], BF16)
        ident_f = S("ident_f", [128, 128], F32)
        ident_b = S("ident_b", [128, 128], BF16)
        onesc = S("onesc", [128, 1], BF16)
        onesr = S("onesr", [1, 512], BF16)
        onesr_f = S("onesr_f", [1, 128], F32)
        epsr = S("epsr", [1, 1], BF16)
        ej_b = S("ej_b", [16, 8 * 128], BF16)
        bfull = S("bfull", [128, 32], F32)
        bvec = S("bvec", [128, 32], BF16)
        upb_f = S("upb_f", [128, 32], F32)
        cch = S("cch", [128, 8], F32)
        gam = S("gam", [128, 8], F32)
        bet = S("bet", [128, 8], F32)
        kvbd = S("kvbd", [128, n_seq * 8 * 128], BF16)
        ksc = S("ksc", [128, n_seq * 8 * 16], BF16)
        x_t = S("x_t", [128, NCHUNK * T], F32)
        s1 = S("s1", [128, NCHUNK * T], BF16)
        s2 = S("s2", [128, NCHUNK * T], BF16)
        s3 = S("s3", [128, NCHUNK * T], BF16)
        g_t = S("g_t", [128, HCHUNK * T], BF16)
        u_t = g_t.bitcast(F32)
        kv_k = S("kv_k", [128, 1024], BF16)
        kv_v = S("kv_v", [128, 8 * 129], BF16)
        a_st = [S("a_st0", [128, 8 * 128], BF16),
                S("a_st1", [128, 8 * 128], BF16),
                S("a_st2", [128, 8 * 128], BF16)]
        stats_sb = S("stats_sb", [33, T], F32)
        small = S("small", [128, 48], F32)
        small_i = small.bitcast(mybir.dt.int32)
        den_sb = S("den_sb", [16, T], F32)
        dsp = S("dsp", [128, 64], F32)
        rdl_sp = S("rdl_sp", [128, 64], F32)
        rdh_sb = S("rdh_sb", [16, T], BF16)
        rdl_sb = S("rdl_sb", [16, T], BF16)
        
        wtmp = S("wtmp", [128, 1024], F32)
        rdb_sb = wtmp
        stg_all = S("stg_all", [128, 2688], BF16)
        stg_fv = stg_all.bitcast(F32)
        kvacc = stg_fv
        stg_f = stg_fv
        STGB0 = 1792
        dvv = S("dvv", [128, 320], F32)
        dgb = S("dgb", [128, 8 * 128], BF16)
        ypp = S("ypp", [128, 2 * 128], BF16)
        trn = S("trn", [128, 2 * 128], BF16)

        P = [ctx.enter_context(nc.psum_tensor("pb%d" % i, [128, 512], F32))
             for i in range(8)]

        blk = ctx.enter_context(nc.Block())
        pr = Prog(nc, ctx)

        tens = lambda b, **kw: pr.op("tensor", b, **kw)
        vec = lambda b, **kw: pr.op("vector", b, **kw)
        sca = lambda b, **kw: pr.op("scalar", b, **kw)

        SERIAL_CH = {"misc"}

        def dma(queue, dst, src, reads=(), writes=(), ch="misc"):
            semname = pr.dma_ch(ch)
            if ch in SERIAL_CH:
                pr.raw_wait(queue, semname, pr.cnt[semname])
            return pr.op(queue, lambda e: e.dma_start(dst, src),
                         reads=reads, writes=writes, dma=True, ch=ch)

        def warm(n):
            for _ in range(n):
                pr.op("tensor", lambda e: e.matmul(
                    P[7][:, 0:512], c_sb[:, 0:128], c_sb[:, 0:512],
                    start=True, stop=True, skip_group_check=True), inc=False)

        ALLC = tuple("C:%d" % k for k in range(HCHUNK))
        ALLWA = tuple("wa:%d" % d for d in range(8))
        ALLWB = tuple("wb:%d" % d for d in range(8))

        dma("sync", ident_f[:, :], dins["ident"][:, :], writes=("idf",), ch="misc")
        vec(lambda e: e.tensor_copy(ident_b[:, :], ident_f[:, :]),
            reads=("idf",), writes=("idb",))
        vec(lambda e: e.memset(onesc[:, :], 1.0), writes=("ones",))
        vec(lambda e: e.memset(onesr[:, :], 1.0), writes=("onesr",))
        vec(lambda e: e.memset(onesr_f[:, :], 1.0), writes=("onesrf",))
        vec(lambda e: e.memset(epsr[:, :], float(D) * LN_EPS), writes=("epsr",))
        for h in range(2):
            dma("sync", wtmp[0:16, 0:512],
                dins["ej"][:, 512 * h:512 * h + 512], writes=("wtmp0",),
                ch="wload0")
            vec(lambda e, h=h: e.tensor_copy(
                ej_b[:, 512 * h:512 * h + 512], wtmp[0:16, 0:512]),
                reads=("wtmp0",), writes=("ej",))
        dma("sync", gam[:, :], dins["gamma"][:, :], writes=("gam",), ch="misc")
        dma("sync", bet[:, :], dins["beta"][:, :], writes=("bet",), ch="misc")
        dma("sync", upb_f[:, :], dins["upbias"][:, :], writes=("upb",), ch="misc")
        dma("sync", cch[:, :], dins["bcon"][:, :], writes=("cch",), ch="misc")
        dma("sync", wtmp[:, 0:8], dins["dbias"][:, :], writes=("wtmp0",), ch="wload0")
        vec(lambda e: e.scalar_tensor_tensor(cch[:, :], wtmp[:, 0:8], 1.0,
                                             cch[:, :], alu.mult, alu.add),
            reads=("wtmp0", "cch"), writes=("cch",))
        vec(lambda e: e.tensor_scalar(cch[:, :], cch[:, :], 0.5, None,
                                      alu.mult, alu.bypass),
            reads=("cch",), writes=("cch",))
        dma("sync", wtmp[:, 0:32], dins["bexp"][:, :], writes=("wtmp0",), ch="wload0")
        vec(lambda e: e.tensor_copy(bvec[:, :], wtmp[:, 0:32]),
            reads=("wtmp0",), writes=tuple("bv:%d" % ch for ch in range(32)))
        vec(lambda e: e.memset(kvbd[:, :], 0.0), writes=("kvbd",))
        vec(lambda e: e.memset(ksc[:, :], 0.0), writes=("ksc",))
        vec(lambda e: e.memset(
            bass.AP(kv_v, 128, [[8 * 129, 128], [129, 8]]), 1.0),
            writes=("kvv1",))

        def emit_x_load(iseq, it):
            for h in range(2):
                xk = tuple("x:%d" % c for c in range(4 * h, 4 * h + 4))
                dma("scalar",
                    bass.AP(x_t, 4 * h * T,
                            [[NCHUNK * T, 128], [T, 4], [1, T]]),
                    bass.AP(dx.tensor,
                            iseq * D * seq_len + 4 * h * 128 * seq_len + it * T,
                            [[seq_len, 128], [128 * seq_len, 4], [1, T]]),
                    reads=xk, writes=xk, ch="x%d" % h)


        def load_w(dsrc, dst_sb, tag):
            for i in range(16):
                ch, h = divmod(i, 2)
                p = i % 2
                dma("sync" if p == 0 else "gpsimd",
                    wtmp[:, 512 * p:512 * p + 512],
                    dsrc[128 * ch:128 * ch + 128, 512 * h:512 * h + 512],
                    writes=("wtmp%d" % p,),
                    ch="wload0" if p == 0 else "wload1g")
                pr.op("gpsimd" if p == 0 else "vector",
                      lambda e, ch=ch, h=h, p=p, dst_sb=dst_sb: e.tensor_copy(
                          dst_sb[:, 1024 * ch + 512 * h:
                                 1024 * ch + 512 * h + 512],
                          wtmp[:, 512 * p:512 * p + 512]),
                      reads=("wtmp%d" % p,), writes=("%s:%d" % (tag, ch),))

        def fill_R_ops(src):
            ops = []
            for i in range(64):
                ch, h = divmod(i, 2)
                p = i % 2

                def one(ch=ch, h=h, p=p):
                    dma("sync" if p == 0 else "gpsimd",
                        wtmp[:, 512 * p:512 * p + 512],
                        src[128 * ch:128 * ch + 128,
                            512 * h:512 * h + 512],
                        writes=("wtmp%d" % p,),
                        ch="wload0" if p == 0 else "wload1g")
                    pr.op("gpsimd" if p == 0 else "vector",
                          lambda e, ch=ch, h=h, p=p: e.tensor_copy(
                              c_sb[:, 1024 * ch + 512 * h:
                                   1024 * ch + 512 * h + 512],
                              wtmp[:, 512 * p:512 * p + 512]),
                          reads=("wtmp%d" % p,), writes=("C:%d" % ch,))
                ops.append(one)
            return ops

        def fill_R_gp(src):
            for i in range(64):
                ch, h = divmod(i, 2)
                p = i % 2
                dma("sync" if p == 0 else "gpsimd",
                    wtmp[:, 512 * p:512 * p + 512],
                    src[128 * ch:128 * ch + 128, 512 * h:512 * h + 512],
                    writes=("wtmp%d" % p,),
                    ch="wload0" if p == 0 else "wload1g")
                pr.op("gpsimd",
                      lambda e, ch=ch, h=h, p=p: e.tensor_copy(
                          c_sb[:, 1024 * ch + 512 * h:
                               1024 * ch + 512 * h + 512],
                          wtmp[:, 512 * p:512 * p + 512]),
                      reads=("wtmp%d" % p,), writes=("C:%d" % ch,))

        def fill_R(src):
            for op in fill_R_ops(src):
                op()

        def load_stage_chunk(src_ap, ch):
            p = ch % 2
            base = p * 1024 + ((ch // 2) % 2) * 2048
            toks = ("gu:%d" % (base // 512), "gu:%d" % (base // 512 + 1))
            dma("scalar" if p == 0 else "sync",
                bass.AP(u_t, base, [[8192, 128], [128, 7], [1, 128]]),
                bass.AP(src_ap.tensor, ch * 128 * 128,
                        [[128, 128], [32 * 128 * 128, 7], [1, 128]]),
                writes=toks, ch="stg" if p == 0 else "stg1")
            return base, toks

        def lo_chain_pair(landA, landB, down, fill):
            holds = {}
            st = {0: 0, 1: 0}
            order = range(0, 7) if down else range(6, -1, -1)
            for k, s in enumerate(order):
                for p, (base, ltoks) in ((0, landA), (1, landB)):
                    lc = base + s * 128
                    lhs = u_t[:, lc:lc + 128]
                    pb = 4 + p
                    if k == 0:
                        rhs, rkeyn = ident_f[:, :], ("idf",)
                    else:
                        cur = 2 * p + st[p]
                        rhs = stg_fv[:, cur * 128:cur * 128 + 128]
                        rkeyn = ("cs:%d" % cur,)
                    tens(lambda e, lhs=lhs, rhs=rhs, pb=pb: e.matmul(
                        P[pb][0:128, 0:128], lhs, rhs, start=True, stop=True),
                        reads=ltoks + rkeyn, writes=("p%d" % pb,))
                    if k == 6:
                        sca(lambda e, p=p, pb=pb: e.activation(
                            kv_k[:, 128 * p:128 * p + 128],
                            P[pb][0:128, 0:128], AF.Identity),
                            reads=("p%d" % pb,), writes=("kvh:%d" % p,))
                        holds[p] = (kv_k[:, 128 * p:128 * p + 128],
                                    "kvh:%d" % p)
                    else:
                        nxt = 1 - st[p]
                        cn = 2 * p + nxt
                        sca(lambda e, cn=cn, pb=pb:
                            e.activation(stg_fv[:, cn * 128:cn * 128 + 128],
                                         P[pb][0:128, 0:128], AF.Identity),
                            reads=("p%d" % pb,),
                            writes=("cs:%d" % cn,))
                        st[p] = nxt
                    if fill:
                        fill.pop(0)()
            return holds

        def cupd_ops(ch, hold, down):
            pc, pkey = hold
            ops = []
            pbs = (0, 1) if ch % 2 == 0 else (2, 3)
            for h in range(2):
                pb = pbs[h]

                def mm(e, pc=pc, ch=ch, h=h, pb=pb):
                    return e.matmul(
                        P[pb][:, :], pc,
                        c_sb[:, 1024 * ch + 512 * h:1024 * ch + 512 * h + 512],
                        start=True, stop=True)
                ops.append((lambda mm=mm, pkey=pkey, ch=ch, pb=pb:
                            tens(mm, reads=(pkey, "C:%d" % ch),
                                 writes=("p%d" % pb,))))
                if down:
                    ops.append((lambda ch=ch, h=h, pb=pb:
                                vec(lambda e: e.tensor_scalar(
                                    c_sb[:, 1024 * ch + 512 * h:
                                         1024 * ch + 512 * h + 512],
                                    P[pb][:, :], 0.5, None,
                                    alu.mult, alu.bypass),
                                    reads=("p%d" % pb,),
                                    writes=("C:%d" % ch,))))
                else:
                    ops.append((lambda ch=ch, h=h, pb=pb:
                                vec(lambda e: e.tensor_copy(
                                    c_sb[:, 1024 * ch + 512 * h:
                                         1024 * ch + 512 * h + 512],
                                    P[pb][:, :]),
                                    reads=("p%d" % pb,),
                                    writes=("C:%d" % ch,))))
            if not down:
                pbb = 6 if ch % 2 == 0 else 7
                ops.append((lambda pc=pc, pkey=pkey, ch=ch, pbb=pbb:
                            tens(lambda e: e.matmul(
                                P[pbb][0:128, 0:1], pc, bvec[:, ch:ch + 1],
                                start=True, stop=True),
                                reads=(pkey, "bv:%d" % ch),
                                writes=("p%d" % pbb,))))
                ops.append((lambda ch=ch, pbb=pbb:
                            vec(lambda e: e.tensor_copy(
                                bvec[:, ch:ch + 1], P[pbb][0:128, 0:1]),
                                reads=("p%d" % pbb,),
                                writes=("bv:%d" % ch,))))
            return ops

        def lo_stages(src_key, down):
            lands = {0: load_stage_chunk(dins[src_key], 0),
                     1: load_stage_chunk(dins[src_key], 1)}
            fill = []
            for base in range(0, 32, 2):
                holds = lo_chain_pair(lands.pop(base), lands.pop(base + 1),
                                      down, fill)
                if base + 2 < 32:
                    lands[base + 2] = load_stage_chunk(dins[src_key],
                                                       base + 2)
                    lands[base + 3] = load_stage_chunk(dins[src_key],
                                                       base + 3)
                while fill:
                    fill.pop(0)()
                fill = (cupd_ops(base, holds[0], down)
                        + cupd_ops(base + 1, holds[1], down))
            while fill:
                fill.pop(0)()

        def hi_stages(dv_key, bias_too):
            npair = 0
            for si in range(5):
                k = 1 << si
                for cbase in range(32):
                    if cbase & k:
                        continue
                    cp = cbase + k
                    db = 512 * (npair % 2)
                    dk = 4 * (npair % 2)
                    npair += 1
                    cols = [(si * 32 + cbase) * 2, (si * 32 + cbase) * 2 + 1,
                            (si * 32 + cp) * 2, (si * 32 + cp) * 2 + 1]
                    for q, col in enumerate(cols):
                        vec(lambda e, q=q, col=col, db=db: e.tensor_scalar(
                            dgb[:, db + 128 * q:db + 128 * q + 128],
                            ident_b[:, :],
                            dvv[:, col:col + 1], None, alu.mult, alu.bypass),
                            reads=("idb", "dvv"), writes=("dgb:%d" % (dk + q),))
                    for h in range(2):
                        a0 = 1024 * cbase + 512 * h
                        a1 = 1024 * cp + 512 * h
                        pr.mm_group(
                            [lambda e, a0=a0, h=h, db=db: e.matmul(
                                P[h][:, :], dgb[:, db:db + 128],
                                c_sb[:, a0:a0 + 512], start=True, stop=False),
                             lambda e, a1=a1, h=h, db=db: e.matmul(
                                P[h][:, :], dgb[:, db + 128:db + 256],
                                c_sb[:, a1:a1 + 512], start=False, stop=True)],
                            reads=("dgb:%d" % dk, "dgb:%d" % (dk + 1),
                                   "C:%d" % cbase, "C:%d" % cp),
                            writes=("p%d" % h,))
                        pr.mm_group(
                            [lambda e, a1=a1, h=h, db=db: e.matmul(
                                P[2 + h][:, :], dgb[:, db + 256:db + 384],
                                c_sb[:, a1:a1 + 512], start=True, stop=False),
                             lambda e, a0=a0, h=h, db=db: e.matmul(
                                P[2 + h][:, :], dgb[:, db + 384:db + 512],
                                c_sb[:, a0:a0 + 512], start=False, stop=True)],
                            reads=("dgb:%d" % (dk + 2), "dgb:%d" % (dk + 3),
                                   "C:%d" % cbase, "C:%d" % cp),
                            writes=("p%d" % (2 + h),))
                    if bias_too:
                        pr.mm_group(
                            [lambda e, db=db: e.matmul(
                                P[4][0:128, 0:1], dgb[:, db:db + 128],
                                bvec[:, cbase:cbase + 1],
                                start=True, stop=False),
                             lambda e, db=db: e.matmul(
                                P[4][0:128, 0:1], dgb[:, db + 128:db + 256],
                                bvec[:, cp:cp + 1], start=False, stop=True)],
                            reads=("dgb:%d" % dk, "dgb:%d" % (dk + 1),
                                   "bv:%d" % cbase, "bv:%d" % cp),
                            writes=("p4",))
                        pr.mm_group(
                            [lambda e, db=db: e.matmul(
                                P[4][0:128, 1:2], dgb[:, db + 256:db + 384],
                                bvec[:, cp:cp + 1], start=True, stop=False),
                             lambda e, db=db: e.matmul(
                                P[4][0:128, 1:2], dgb[:, db + 384:db + 512],
                                bvec[:, cbase:cbase + 1],
                                start=False, stop=True)],
                            reads=("dgb:%d" % (dk + 2), "dgb:%d" % (dk + 3),
                                   "bv:%d" % cbase, "bv:%d" % cp),
                            writes=("p4",))
                    for h in range(2):
                        sca(lambda e, cbase=cbase, h=h: e.activation(
                            c_sb[:, 1024 * cbase + 512 * h:
                                 1024 * cbase + 512 * h + 512], P[h][:, :],
                            AF.Identity),
                            reads=("p%d" % h,), writes=("C:%d" % cbase,))
                        sca(lambda e, cp=cp, h=h: e.activation(
                            c_sb[:, 1024 * cp + 512 * h:
                                 1024 * cp + 512 * h + 512], P[2 + h][:, :],
                            AF.Identity),
                            reads=("p%d" % (2 + h),), writes=("C:%d" % cp,))
                    if bias_too:
                        vec(lambda e, cbase=cbase: e.tensor_copy(
                            bvec[:, cbase:cbase + 1], P[4][0:128, 0:1]),
                            reads=("p4",), writes=("bv:%d" % cbase,))
                        vec(lambda e, cp=cp: e.tensor_copy(
                            bvec[:, cp:cp + 1], P[4][0:128, 1:2]),
                            reads=("p4",), writes=("bv:%d" % cp,))

        dma("scalar", dvv[:, :], dins["dv_u"][:, :], writes=("dvv",), ch="dvv")
        fill_R_gp(dins["wexpT"])
        lo_stages("slo_u", down=False)
        emit_x_load(0, 0)
        hi_stages("dv_u", bias_too=True)
        vec(lambda e: e.tensor_copy(bfull[:, :], bvec[:, :]),
            reads=tuple("bv:%d" % ch for ch in range(32)), writes=("bfull",))
        vec(lambda e: e.scalar_tensor_tensor(bfull[:, :], upb_f[:, :], 1.0,
                                             bfull[:, :], alu.mult, alu.add),
            reads=("upb", "bfull"), writes=("bfull",))

        for ch in range(32):
            for d in range(8):
                par = (ch * 8 + d) % 4
                pb = 4 + par
                tens(lambda e, ch=ch, d=d, pb=pb: e.matmul(
                    P[pb][0:128, 0:128],
                    c_sb[:, 1024 * ch + 128 * d:1024 * ch + 128 * d + 128],
                    ident_b[:, :], start=True, stop=True),
                    reads=("C:%d" % ch, "idb"), writes=("p%d" % pb,))
                if par < 2:
                    slot, skey = trn[:, 128 * par:128 * par + 128], \
                        "trn:%d" % par
                else:
                    slot, skey = ypp[:, 128 * (par - 2):128 * (par - 2) + 128], \
                        "ypp:%d" % (par - 2)
                if par % 2 == 0:
                    sca(lambda e, slot=slot, pb=pb: e.activation(
                        slot, P[pb][0:128, 0:128], AF.Identity),
                        reads=("p%d" % pb,), writes=(skey,))
                else:
                    vec(lambda e, slot=slot, pb=pb: e.tensor_copy(
                        slot, P[pb][0:128, 0:128]),
                        reads=("p%d" % pb,), writes=(skey,))
                dma("gpsimd" if par < 2 else "sync",
                    bass.AP(asp.tensor, (128 * d) * HID + 128 * ch,
                            [[HID, 128], [1, 128]]),
                    slot,
                    reads=(skey,), writes=("asp:%d:%d" % (ch, d),),
                    ch="asp%d" % par)

        fill_R_gp(dins["wcon"])
        dma("scalar", dvv[:, :], dins["dv_d"][:, :], writes=("dvv",), ch="dvv")
        load_w(dins["wk"], w_a, "wa")
        load_w(dins["wv"], w_b, "wb")
        hi_stages("dv_d", bias_too=False)
        lo_stages("slo_dT", down=True)

        for i in range(16):
            for p, (dsrc, dspill) in enumerate(((dins["wq"], wqsp),
                                                (dins["wfc"], wfsp))):
                ch, h = divmod(i, 2)
                ceng = "vector" if p == 0 else "gpsimd"
                ktok = "kvkA" if p == 0 else "kvkB"
                extra = ("kvh:0", "kvh:1") if p == 0 else ()
                dma("sync" if p == 0 else "gpsimd",
                    wtmp[:, 512 * p:512 * p + 512],
                    dsrc[128 * ch:128 * ch + 128, 512 * h:512 * h + 512],
                    writes=("wtmp%d" % p,),
                    ch="wload0" if p == 0 else "wload1g")
                pr.op(ceng, lambda e, p=p: e.tensor_copy(
                    kv_k[:, 512 * p:512 * p + 512],
                    wtmp[:, 512 * p:512 * p + 512]),
                    reads=("wtmp%d" % p,), writes=(ktok,) + extra)
                dma("sync" if p == 0 else "gpsimd",
                    bass.AP(dspill.tensor, 1024 * ch + 512 * h,
                            [[8192, 128], [1, 512]]),
                    kv_k[:, 512 * p:512 * p + 512],
                    reads=(ktok,), writes=("wsp%d" % p,),
                    ch="wspo%d" % p)

        SB = {"s1": s1, "s2": s2, "s3": s3}
        def skey(src_key, c):
            return src_key % c if "%d" in src_key else src_key
        GUALL = tuple("gu:%d" % _c for _c in range(8))
        def wkeys(name, c=None):
            if c is None:
                base = "s1x" if name == "s1" else name
                return (name,) + tuple("%s:%d" % (base, _c) for _c in range(8))
            base = "s1x" if name == "s1" else name
            return ("%s:%d" % (base, _c if False else c),)
        def rkey(name, c):
            base = "s1x" if name == "s1" else name
            return "%s:%d" % (base, c)
        def emit_rows_and_bcast_r(sp_sb, rcol, rkeyn, hilo, ng):
            for g in range(ng):
                tens(lambda e, g=g: e.transpose(
                    P[4][0:1, 128 * g:128 * g + 128],
                    sp_sb[:, rcol + g:rcol + g + 1], ident_f[:, :]),
                    reads=(rkeyn,), writes=("p4",))
            vec(lambda e, ng=ng: e.tensor_copy(stats_sb[0:1, 0:128 * ng],
                                               P[4][0:1, 0:128 * ng]),
                reads=("p4",), writes=("stats",))
            for g in range(ng):
                tens(lambda e, g=g: e.matmul(
                    P[4][0:128, 128 * g:128 * g + 128], onesr_f[0:1, 0:128],
                    stats_sb[0:1, 128 * g:128 * g + 128],
                    start=True, stop=True),
                    reads=("onesrf", "stats"), writes=("p4",))

        def emit_rows_and_bcast_m(sp_sb, mcol, mkeyn, hilo, ng):
            for g in range(ng):
                tens(lambda e, g=g: e.transpose(
                    P[6][0:1, 128 * g:128 * g + 128],
                    sp_sb[:, mcol + g:mcol + g + 1], ident_f[:, :]),
                    reads=(mkeyn,), writes=("p6",))
            vec(lambda e, ng=ng: e.tensor_copy(stats_sb[0:1, 0:128 * ng],
                                               P[6][0:1, 0:128 * ng]),
                reads=("p6", "stats"), writes=("statsm", "stats"))
            for g in range(ng):
                tens(lambda e, g=g: e.matmul(
                    P[6][0:128, 128 * g:128 * g + 128], onesr_f[0:1, 0:128],
                    stats_sb[0:1, 128 * g:128 * g + 128],
                    start=True, stop=True),
                    reads=("onesrf", "statsm"), writes=("p6",))

        def emit_bcast_normalize(sp_sb, rcol, mcol, rkeyn, mkeyn,
                                 src_f32, src_key, outname,
                                 need_f32, hilo, affine, ng, use_bf_src,
                                 cast_n="s1"):
            nrm_eng = lambda c: "vector"
            emit_rows_and_bcast_r(sp_sb, rcol, rkeyn, hilo, ng)
            emit_rows_and_bcast_m(sp_sb, mcol, mkeyn, hilo, ng)
            if use_bf_src:
                ttile, tsrc_t = SB[outname], SB[cast_n]
                for c in range(NCHUNK):
                    pr.op(nrm_eng(c),
                          lambda e, c=c, ttile=ttile, tsrc_t=tsrc_t:
                          e.tensor_tensor(
                              ttile[:, T * c:T * c + T],
                              tsrc_t[:, T * c:T * c + T],
                              P[4][0:128, 0:T], alu.mult),
                          reads=(rkey(cast_n, c), "p4"),
                          writes=wkeys(outname, c))
            else:
                for c in range(NCHUNK):
                    pr.op(nrm_eng(c), lambda e, c=c: e.tensor_tensor(
                        u_t[:, T * c:T * c + T], src_f32[:, T * c:T * c + T],
                        P[4][0:128, 0:T], alu.mult),
                        reads=(skey(src_key, c), "p4", "gu:%d" % c),
                        writes=("gu:%d" % c,))
            if use_bf_src:
                out_bf = SB[outname]
                for c in range(NCHUNK):
                    pr.op(nrm_eng(c),
                          lambda e, c=c, out_bf=out_bf: e.tensor_tensor(
                              out_bf[:, T * c:T * c + T],
                              out_bf[:, T * c:T * c + T],
                              P[6][0:128, 0:T], alu.subtract),
                          reads=(rkey(outname, c), "p6"),
                          writes=wkeys(outname, c))
            else:
                for c in range(NCHUNK):
                    pr.op(nrm_eng(c), lambda e, c=c: e.tensor_tensor(
                        u_t[:, T * c:T * c + T], u_t[:, T * c:T * c + T],
                        P[6][0:128, 0:T], alu.subtract),
                        reads=("gu:%d" % c, "p6"), writes=("gu:%d" % c,))
            if affine:
                for c in range(NCHUNK):
                    sca(lambda e, c=c: e.activation(
                        u_t[:, T * c:T * c + T], u_t[:, T * c:T * c + T],
                        AF.Identity, bias=bet[:, c:c + 1],
                        scale=gam[:, c:c + 1]),
                        reads=("gu:%d" % c, "gam", "bet"),
                        writes=("gu:%d" % c,))
            if need_f32 and outname is not None:
                out_bf = SB[outname]
                for c in range(NCHUNK):
                    sca(lambda e, c=c, out_bf=out_bf: e.activation(
                        out_bf[:, T * c:T * c + T], u_t[:, T * c:T * c + T],
                        AF.Identity),
                        reads=("gu:%d" % c,), writes=wkeys(outname, c))

        def emit_ln_stats_chunk(src_f32, src_key, c, cast_n, sq_n,
                                cast_eng="vector", sq_eng="vector"):
            ct, qt = SB[cast_n], SB[sq_n]
            if cast_eng == "scalar":
                sca(lambda e, c=c, ct=ct: e.activation(
                    ct[:, T * c:T * c + T], src_f32[:, T * c:T * c + T],
                    AF.Identity),
                    reads=(skey(src_key, c),), writes=wkeys(cast_n, c))
            else:
                pr.op(cast_eng, lambda e, c=c, ct=ct: e.tensor_copy(
                    ct[:, T * c:T * c + T], src_f32[:, T * c:T * c + T]),
                    reads=(skey(src_key, c),), writes=wkeys(cast_n, c))
            pr.op(sq_eng, lambda e, c=c, ct=ct, qt=qt: e.tensor_tensor(
                qt[:, T * c:T * c + T], ct[:, T * c:T * c + T],
                ct[:, T * c:T * c + T], alu.mult),
                reads=(rkey(cast_n, c),), writes=wkeys(sq_n, c))

        def emit_ln_stats_mm(c, cast_n, sq_n):
            ct, qt = SB[cast_n], SB[sq_n]
            tens(lambda e, c=c, ct=ct: e.matmul(
                P[4][0:1, 0:T], onesc[:, 0:1], ct[:, T * c:T * c + T],
                start=(c == 0), stop=(c == 7), skip_group_check=True),
                reads=("ones", rkey(cast_n, c)), writes=("p4",))
            tens(lambda e, c=c, qt=qt: e.matmul(
                P[5][0:1, 0:T], onesc[:, 0:1], qt[:, T * c:T * c + T],
                start=(c == 0), stop=False, skip_group_check=True),
                reads=("ones", rkey(sq_n, c)), writes=("p5",))
            if c == 7:
                tens(lambda e: e.matmul(
                    P[5][0:1, 0:T], epsr[0:1, 0:1], onesr[0:1, 0:T],
                    start=False, stop=True, skip_group_check=True),
                    reads=("epsr", "onesr"), writes=("p5",))

        def emit_ln_head():
            ng = NG
            vec(lambda e: e.tensor_copy(stats_sb[0:1, 0:T], P[4][0:1, 0:T]),
                reads=("p4",), writes=("stats",))
            vec(lambda e: e.tensor_copy(stats_sb[32:33, 0:T], P[5][0:1, 0:T]),
                reads=("p5",), writes=("stats",))
            for g in range(ng):
                tens(lambda e, g=g: e.transpose(
                    P[4][0:128, g:g + 1],
                    stats_sb[0:1, 128 * g:128 * g + 128], ident_f[0:1, 0:1]),
                    reads=("stats",), writes=("p4",))
                tens(lambda e, g=g, ng=ng: e.transpose(
                    P[4][0:128, ng + g:ng + g + 1],
                    stats_sb[32:33, 128 * g:128 * g + 128],
                    ident_f[32:33, 32:33]),
                    reads=("stats",), writes=("p4",))
            vec(lambda e, ng=ng: e.tensor_copy(small[:, 0:2 * ng],
                                               P[4][0:128, 0:2 * ng]),
                reads=("p4",), writes=("sm0",))
            mu0, e20, t10, t20, r0, m20 = 8, 12, 16, 20, 24, 28
            vec(lambda e, ng=ng: e.tensor_scalar(
                small[:, mu0:mu0 + 2 * ng], small[:, 0:2 * ng], 1.0 / D, None,
                alu.mult, alu.bypass),
                reads=("sm0",), writes=("smmu", "sme2"))
            vec(lambda e, ng=ng: e.tensor_tensor(
                small[:, t10:t10 + ng], small[:, mu0:mu0 + ng],
                small[:, mu0:mu0 + ng], alu.mult),
                reads=("smmu",), writes=("smt1",))
            vec(lambda e, ng=ng: e.scalar_tensor_tensor(
                small[:, t20:t20 + ng], small[:, t10:t10 + ng], -1.0,
                small[:, e20:e20 + ng], alu.mult, alu.add),
                reads=("smt1", "sme2"), writes=("smv",))
            vec(lambda e, ng=ng: e.tensor_copy(
                small[:, t10:t10 + ng], small_i[:, t20:t20 + ng]),
                reads=("smv",), writes=("smt1",))
            vec(lambda e, ng=ng: e.tensor_scalar(
                small[:, t10:t10 + ng], small[:, t10:t10 + ng], -0.5,
                1597463007.0, alu.mult, alu.add),
                reads=("smt1",), writes=("smt1",))
            vec(lambda e, ng=ng: e.tensor_copy(
                small_i[:, r0:r0 + ng], small[:, t10:t10 + ng]),
                reads=("smt1",), writes=("smr",))
            for _ in range(NEWTON_ITERS):
                vec(lambda e, ng=ng: e.tensor_tensor(
                    small[:, t10:t10 + ng], small[:, r0:r0 + ng],
                    small[:, r0:r0 + ng], alu.mult),
                    reads=("smr",), writes=("smt1",))
                vec(lambda e, ng=ng: e.scalar_tensor_tensor(
                    small[:, m20:m20 + ng], small[:, t10:t10 + ng], -0.5,
                    small[:, t20:t20 + ng], alu.mult, alu.mult),
                    reads=("smt1", "smv"), writes=("smt2",))
                vec(lambda e, ng=ng: e.scalar_tensor_tensor(
                    small[:, r0:r0 + ng], small[:, m20:m20 + ng], 1.5,
                    small[:, r0:r0 + ng], alu.add, alu.mult),
                    reads=("smt2", "smr"), writes=("smr",))
            vec(lambda e, ng=ng: e.tensor_tensor(
                small[:, m20:m20 + ng], small[:, mu0:mu0 + ng],
                small[:, r0:r0 + ng], alu.mult),
                reads=("smmu", "smr"), writes=("smm2",))

        def emit_ln_tail2(src_f32, src_key, outname, need_f32, hilo,
                          affine, cast_n="s1"):
            r0, m20 = 24, 28
            emit_bcast_normalize(small, r0, m20, "smr", "smm2",
                                 src_f32, src_key, outname,
                                 need_f32, hilo, affine, NG,
                                 use_bf_src=not (need_f32 or affine),
                                 cast_n=cast_n)

        def emit_ln_tail(src_f32, src_key, outname, need_f32, hilo,
                         affine, cast_n="s1", sq_n="s2"):
            emit_ln_head()
            emit_ln_tail2(src_f32, src_key, outname, need_f32, hilo,
                          affine, cast_n=cast_n)

        def emit_ln(src_f32, src_key, outname, need_f32, hilo,
                    affine, cast_n="s1", sq_n="s2"):
            for c in range(NCHUNK):
                emit_ln_stats_chunk(src_f32, src_key, c, cast_n, sq_n)
                emit_ln_stats_mm(c, cast_n, sq_n)
            emit_ln_tail(src_f32, src_key, outname, need_f32, hilo,
                         affine, cast_n, sq_n)

        aqueues = ["sync", "gpsimd", "scalar"]

        def emit_ffn(srcname="s3", post_oc=None):
            src = SB[srcname]
            for o in range(HCHUNK):
                par = o % 3
                q = "sync"
                dma(q, bass.AP(a_st[par], 0, [[1024, 128], [128, 8], [1, 128]]),
                    bass.AP(asp.tensor, 128 * o,
                            [[HID, 128], [128 * HID, 8], [1, 128]]),
                    reads=tuple("asp:%d:%d" % (o, d) for d in range(8)),
                    writes=("ast:%d" % par,), ch="ast%d" % par)
                pr.mm_group(
                    [(lambda e, d=d, par=par, src=src: e.matmul(
                        P[par][:, 0:T], a_st[par][:, 128 * d:128 * d + 128],
                        src[:, T * d:T * d + T],
                        start=(d == 0), stop=(d == 7)))
                     for d in range(8)],
                    reads=("ast:%d" % par,), writes=("p%d" % par,),
                    reads_each=[(rkey(srcname, d),) for d in range(8)])
                sca(lambda e, o=o, par=par: e.activation(
                    g_t[:, T * o:T * o + T], P[par][:, 0:T], AF.Gelu,
                    bias=bfull[:, o:o + 1]),
                    reads=("p%d" % par, "bfull"), writes=("gu:%d" % (o // 2),))
            for oc in range(8):
                par = oc % 4
                pr.mm_group(
                    [(lambda e, k=k, oc=oc, par=par: e.matmul(
                        P[par][:, 0:T],
                        c_sb[:, 1024 * k + 128 * oc:1024 * k + 128 * oc + 128],
                        g_t[:, T * k:T * k + T],
                        start=(k == 0), stop=(k == HCHUNK - 1)))
                     for k in range(HCHUNK)],
                    reads=ALLC, writes=("p%d" % par,),
                    reads_each=[("gu:%d" % (k // 2),) for k in range(HCHUNK)])
                vec(lambda e, oc=oc, par=par: e.scalar_tensor_tensor(
                    x_t[:, T * oc:T * oc + T], P[par][:, 0:T],
                    cch[:, oc:oc + 1], x_t[:, T * oc:T * oc + T],
                    alu.add, alu.add),
                    reads=("p%d" % par, "cch", "x:%d" % oc),
                    writes=("x:%d" % oc,))
                if post_oc is not None:
                    post_oc(oc)

        def emit_kv_group(iseq, it, g):
            if True:
                for half in range(2):
                    kp = (2 * g + half) % 2
                    pr.mm_group(
                        [(lambda e, d=d, g=g, half=half, kp=kp: e.matmul(
                            P[kp][:, 0:512],
                            s3[:, T * d + 128 * g:T * d + 128 * g + 128],
                            w_a[:, 1024 * d + 512 * half:
                                1024 * d + 512 * half + 512],
                            start=(d == 0), stop=(d == 7)))
                         for d in range(8)],
                        reads=ALLWA, writes=("p%d" % kp,),
                        reads_each=[("s3:%d" % d,) for d in range(8)])
                    sca(lambda e, half=half, kp=kp: e.activation(
                        u_t[:, 512 * half:512 * half + 512],
                        P[kp][:, 0:512], AF.Tanh, scale=0.5),
                        reads=("p%d" % kp, "gu:%d" % half),
                        writes=("gu:%d" % half,)),
                    vec(lambda e, half=half: e.tensor_scalar(
                        kv_k[:, 512 * half:512 * half + 512],
                        u_t[:, 512 * half:512 * half + 512],
                        0.5, 0.5, alu.mult, alu.add),
                        reads=("gu:%d" % half,),
                        writes=("kvk", "kvkA", "kvkB"))
                    pr.mm_group(
                        [(lambda e, d=d, g=g, half=half, kp=kp: e.matmul(
                            P[2 + kp][:, 0:512],
                            s3[:, T * d + 128 * g:T * d + 128 * g + 128],
                            w_b[:, 1024 * d + 512 * half:
                                1024 * d + 512 * half + 512],
                            start=(d == 0), stop=(d == 7)))
                         for d in range(8)],
                        reads=ALLWB, writes=("p%d" % (2 + kp),),
                        reads_each=[("s3:%d" % d,) for d in range(8)])
                    vec(lambda e, half=half, kp=kp: e.tensor_copy(
                        bass.AP(kv_v, 129 * 4 * half,
                                [[8 * 129, 128], [129, 4], [1, 128]]),
                        bass.AP(P[2 + kp], 0,
                                [[512, 128], [128, 4], [1, 128]])),
                        reads=("p%d" % (2 + kp),), writes=("kvv",))
                first = (it == 0 and g == 0)
                banks = [(0, [0, 1, 2]), (1, [3, 4, 5]), (2, [6, 7])]
                for bank, pairs in banks:
                    pr.mm_group(
                        [(lambda e, j=j, bank=bank, st0=(j == pairs[0]),
                          st1=(j == pairs[-1]): e.matmul(
                              P[5 + bank][:, 129 * (j - 3 * bank):
                                          129 * (j - 3 * bank) + 129],
                              kv_k[:, 128 * j:128 * j + 128],
                              kv_v[:, 129 * j:129 * j + 129],
                              start=st0, stop=st1))
                         for j in pairs],
                        reads=("kvk", "kvv", "kvv1"),
                        writes=("p%d" % (5 + bank),))
                    w0 = 387 * bank
                    wd = 129 * len(pairs)
                    if first:
                        vec(lambda e, bank=bank, w0=w0, wd=wd:
                            e.tensor_copy(kvacc[:, w0:w0 + wd],
                                          P[5 + bank][:, 0:wd]),
                            reads=("p%d" % (5 + bank),),
                            writes=("kvacc", "cs:0", "cs:1", "cs:2", "cs:3"))
                    else:
                        vec(lambda e, bank=bank, w0=w0, wd=wd:
                            e.tensor_tensor(kvacc[:, w0:w0 + wd],
                                            kvacc[:, w0:w0 + wd],
                                            P[5 + bank][:, 0:wd], alu.add),
                            reads=("p%d" % (5 + bank), "kvacc"),
                            writes=("kvacc",))

        def emit_kv(iseq, it):
            for g in range(NG):
                emit_kv_group(iseq, it, g)

        def emit_kv_evacs(iseq):
            for j in range(8):
                base = (iseq * 8 + j) * 128
                vec(lambda e, j=j, base=base: e.tensor_copy(
                    kvbd[0:64, base:base + 64],
                    kvacc[0:64, 129 * j:129 * j + 64]),
                    reads=("kvacc",), writes=("kvbd",))
                vec(lambda e, j=j, base=base: e.tensor_copy(
                    kvbd[64:128, base + 64:base + 128],
                    kvacc[64:128, 129 * j + 64:129 * j + 128]),
                    reads=("kvacc",), writes=("kvbd",))
                kb = (iseq * 8 + j) * 16
                vec(lambda e, kb=kb, j=j: e.tensor_copy(
                    ksc[0:64, kb + 2 * j:kb + 2 * j + 1],
                    kvacc[0:64, 129 * j + 128:129 * j + 129]),
                    reads=("kvacc",), writes=("ksc",))
                vec(lambda e, kb=kb, j=j: e.tensor_copy(
                    ksc[64:128, kb + 2 * j + 1:kb + 2 * j + 2],
                    kvacc[64:128, 129 * j + 128:129 * j + 129]),
                    reads=("kvacc",), writes=("ksc",))

        pending_kv = None
        for iseq in range(n_seq):
            for it in range(n_tiles):
                if (iseq, it) != (0, 0):
                    emit_x_load(iseq, it)
                if pending_kv is not None:
                    ps, pt = pending_kv
                    emit_kv_group(ps, pt, 0)
                    for c in range(4):
                        emit_ln_stats_chunk(x_t, "x:%d", c, "s1", "s2",
                                            cast_eng="gpsimd",
                                            sq_eng="gpsimd")
                    emit_kv_group(ps, pt, 1)
                    for c in range(4, 8):
                        emit_ln_stats_chunk(x_t, "x:%d", c, "s1", "s2",
                                            cast_eng="gpsimd",
                                            sq_eng="gpsimd")
                    emit_kv_group(ps, pt, 2)
                    for c in range(NCHUNK):
                        emit_ln_stats_mm(c, "s1", "s2")
                    emit_ln_head()
                    emit_kv_group(ps, pt, 3)
                    if pt == n_tiles - 1:
                        emit_kv_evacs(ps)
                    emit_ln_tail2(x_t, "x:%d", "s2", False, False, False)
                else:
                    emit_ln(x_t, "x:%d", "s2", False, False, False)

                def ln2_cb(oc):
                    emit_ln_stats_chunk(x_t, "x:%d", oc, "s1", "s2")
                    if oc > 0:
                        emit_ln_stats_mm(oc - 1, "s1", "s2")
                emit_ffn("s2", post_oc=ln2_cb)
                emit_ln_stats_mm(7, "s1", "s2")
                dma("gpsimd",
                    bass.AP(x1sp.tensor, iseq * D * seq_len + it * T,
                            [[seq_len, 128], [128 * seq_len, 8], [1, T]]),
                    bass.AP(x_t, 0, [[NCHUNK * T, 128], [T, 8], [1, T]]),
                    reads=tuple("x:%d" % c for c in range(NCHUNK)),
                    writes=("x1sp:%d:%d" % (iseq, it),),
                    ch="spill")
                emit_ln_tail(x_t, "x:%d", "s3", True, True, False)
                for c in range(NCHUNK):
                    pr.op("vector" if c % 2 == 0 else "gpsimd",
                          lambda e, c=c: e.tensor_tensor(
                              s1[:, T * c:T * c + T], u_t[:, T * c:T * c + T],
                              s3[:, T * c:T * c + T], alu.subtract),
                          reads=("gu:%d" % c, "s3:%d" % c),
                          writes=wkeys("s1", c))
                dma("sync",
                    bass.AP(xn2hi.tensor, iseq * D * seq_len + it * T,
                            [[seq_len, 128], [128 * seq_len, 8], [1, T]]),
                    bass.AP(s3, 0, [[NCHUNK * T, 128], [T, 8], [1, T]]),
                    reads=tuple("s3:%d" % c for c in range(8)),
                    writes=("xh:%d:%d" % (iseq, it),), ch="xh")
                dma("sync",
                    bass.AP(xn2lo.tensor, iseq * D * seq_len + it * T,
                            [[seq_len, 128], [128 * seq_len, 8], [1, T]]),
                    bass.AP(s1, 0, [[NCHUNK * T, 128], [T, 8], [1, T]]),
                    reads=tuple("s1x:%d" % c for c in range(8)),
                    writes=("xl:%d:%d" % (iseq, it),), ch="xl")
                pending_kv = (iseq, it)
        emit_kv(*pending_kv)
        emit_kv_evacs(n_seq - 1)

        for half in range(2):
            wa_toks = tuple("wa:%d" % c for c in range(4 * half, 4 * half + 4))
            wb_toks = tuple("wb:%d" % c for c in range(4 * half, 4 * half + 4))
            dma("sync",
                bass.AP(w_a, 4096 * half, [[8192, 128], [1, 4096]]),
                bass.AP(wqsp.tensor, 4096 * half, [[8192, 128], [1, 4096]]),
                reads=("wsp0",) + wa_toks, writes=wa_toks, ch="wrl0")
            dma("gpsimd",
                bass.AP(w_b, 4096 * half, [[8192, 128], [1, 4096]]),
                bass.AP(wfsp.tensor, 4096 * half, [[8192, 128], [1, 4096]]),
                reads=("wsp1",) + wb_toks, writes=wb_toks, ch="wrl1")

        for iseq in range(n_seq):
            for it in range(n_tiles):
                dma("scalar",
                    bass.AP(x_t, 0, [[NCHUNK * T, 128], [T, 8], [1, T]]),
                    bass.AP(x1sp.tensor, iseq * D * seq_len + it * T,
                            [[seq_len, 128], [128 * seq_len, 8], [1, T]]),
                    reads=("x1sp:%d:%d" % (iseq, it),)
                    + tuple("x:%d" % c for c in range(NCHUNK)),
                    writes=tuple("x:%d" % c for c in range(NCHUNK)),
                    ch="x")
                dma("sync",
                    bass.AP(s3, 0, [[NCHUNK * T, 128], [T, 8], [1, T]]),
                    bass.AP(xn2hi.tensor, iseq * D * seq_len + it * T,
                            [[seq_len, 128], [128 * seq_len, 8], [1, T]]),
                    reads=("s3", "xh:%d:%d" % (iseq, it))
                    + tuple("s3:%d" % c for c in range(8)),
                    writes=wkeys("s3"), ch="xh")
                dma("sync",
                    bass.AP(s1, 0, [[NCHUNK * T, 128], [T, 8], [1, T]]),
                    bass.AP(xn2lo.tensor, iseq * D * seq_len + it * T,
                            [[seq_len, 128], [128 * seq_len, 8], [1, T]]),
                    reads=("s1", "xl:%d:%d" % (iseq, it))
                    + tuple("s1x:%d" % c for c in range(8)),
                    writes=wkeys("s1"), ch="xl")
                for c in range(NCHUNK):
                    pr.op("vector" if c % 2 == 0 else "gpsimd",
                          lambda e, c=c: e.tensor_tensor(
                              u_t[:, T * c:T * c + T], s3[:, T * c:T * c + T],
                              s1[:, T * c:T * c + T], alu.add),
                          reads=("s3:%d" % c, "s1x:%d" % c, "gu:%d" % c),
                          writes=("gu:%d" % c,))
                for oc in range(8):
                    par = oc % 4
                    pr.mm_group(
                        [(lambda e, d=d, oc=oc, par=par: e.matmul(
                            P[par][:, 0:T],
                            w_a[:, 1024 * d + 128 * oc:
                                1024 * d + 128 * oc + 128],
                            s3[:, T * d:T * d + T],
                            start=(d == 0), stop=(d == 7)))
                         for d in range(8)],
                        reads=ALLWA, writes=("p%d" % par,),
                        reads_each=[("s3:%d" % d,) for d in range(8)])
                    sca(lambda e, oc=oc, par=par: e.activation(
                        s2[:, T * oc:T * oc + T], P[par][:, 0:T],
                        AF.Tanh, scale=0.5),
                        reads=("p%d" % par,), writes=("s2",) + wkeys("s2", oc))
                    vec(lambda e, oc=oc: e.tensor_scalar(
                        s2[:, T * oc:T * oc + T], s2[:, T * oc:T * oc + T],
                        0.5, 0.5, alu.mult, alu.add),
                        reads=wkeys("s2", oc), writes=wkeys("s2", oc))
                kb0 = iseq * 8 * 16
                pr.mm_group(
                    [(lambda e, d=d, kb0=kb0: e.matmul(
                        P[4][0:16, 0:T],
                        ksc[:, kb0 + 16 * d:kb0 + 16 * d + 16],
                        s2[:, T * d:T * d + T],
                        start=(d == 0), stop=(d == 7)))
                     for d in range(8)],
                    reads=("ksc",), writes=("p4",),
                    reads_each=[("s2:%d" % d,) for d in range(8)])
                vec(lambda e: e.tensor_scalar(
                    den_sb[0:16, 0:T], P[4][0:16, 0:T], 1e-8, None,
                    alu.add, alu.bypass),
                    reads=("p4",), writes=("den",))
                warm(4)
                for g in range(NG):
                    tens(lambda e, g=g: e.transpose(
                        P[4][0:128, 16 * g:16 * g + 16],
                        den_sb[0:16, 128 * g:128 * g + 128],
                        ident_f[0:16, 0:16]),
                        reads=("den",), writes=("p4",))
                warm(6)
                vec(lambda e: e.tensor_copy(dsp[:, 0:16 * NG],
                                            P[4][0:128, 0:16 * NG]),
                    reads=("p4",), writes=("dsp",))
                vec(lambda e: e.reciprocal(dsp[:, 0:16 * NG],
                                           dsp[:, 0:16 * NG]),
                    reads=("dsp",), writes=("dsp",))
                for g in range(NG):
                    tens(lambda e, g=g: e.transpose(
                        P[4][0:16, 128 * g:128 * g + 128],
                        dsp[:, 16 * g:16 * g + 16], ident_f[:, :]),
                        reads=("dsp",), writes=("p4",))
                vec(lambda e: e.tensor_copy(rdh_sb[0:16, 0:T],
                                            P[4][0:16, 0:T]),
                    reads=("p4",), writes=("rdh",))
                vec(lambda e: e.tensor_tensor(
                    rdl_sb[0:16, 0:T], P[4][0:16, 0:T], rdh_sb[0:16, 0:T],
                    alu.subtract),
                    reads=("p4", "rdh"), writes=("rdl",))
                for j in range(8):
                    pn = P[j % 2]
                    pnk = "p%d" % (j % 2)
                    pe = P[5 + (j % 2)]
                    pek = "p%d" % (5 + (j % 2))
                    kvb = (iseq * 8 + j) * 128
                    tens(lambda e, pn=pn, kvb=kvb, j=j: e.matmul(
                        pn[:, 0:T], kvbd[:, kvb:kvb + 128],
                        s2[:, T * j:T * j + T], start=True, stop=True),
                        reads=("kvbd", "s2:%d" % j), writes=(pnk,))
                    pr.mm_group(
                        [lambda e, pe=pe, j=j: e.matmul(
                            pe[:, 0:T], ej_b[:, 128 * j:128 * j + 128],
                            rdh_sb[0:16, 0:T], start=True, stop=False),
                         lambda e, pe=pe, j=j: e.matmul(
                            pe[:, 0:T], ej_b[:, 128 * j:128 * j + 128],
                            rdl_sb[0:16, 0:T], start=False, stop=True)],
                        reads=("ej", "rdh", "rdl"), writes=(pek,))
                    sca(lambda e, pn=pn: e.activation(
                        rdb_sb[:, 0:T], pn[:, 0:T], AF.Identity),
                        reads=(pnk,), writes=("rdb", "wtmp0", "wtmp1"))
                    vec(lambda e, pe=pe, j=j: e.tensor_tensor(
                        s1[:, T * j:T * j + T], rdb_sb[:, 0:T],
                        pe[:, 0:T], alu.mult),
                        reads=(pek, "rdb"), writes=("s1x:%d" % j,))
                for oc in range(8):
                    par = oc % 4
                    pr.mm_group(
                        [(lambda e, d=d, oc=oc, par=par: e.matmul(
                            P[par][:, 0:T],
                            w_b[:, 1024 * d + 128 * oc:
                                1024 * d + 128 * oc + 128],
                            s1[:, T * d:T * d + T],
                            start=(d == 0), stop=(d == 7)))
                         for d in range(8)],
                        reads=ALLWB, writes=("p%d" % par,),
                        reads_each=[("s1x:%d" % d,) for d in range(8)])
                    vec(lambda e, oc=oc, par=par: e.tensor_tensor(
                        u_t[:, T * oc:T * oc + T], P[par][:, 0:T],
                        u_t[:, T * oc:T * oc + T], alu.add),
                        reads=("p%d" % par, "gu:%d" % oc),
                        writes=("gu:%d" % oc,))
                    emit_ln_stats_chunk(u_t, "gu:%d", oc, "s2", "s3")
                    if oc > 0:
                        emit_ln_stats_mm(oc - 1, "s2", "s3")
                emit_ln_stats_mm(7, "s2", "s3")
                emit_ln_tail(u_t, "gu:%d", None, False, True, True,
                             cast_n="s2", sq_n="s3")
                warm(26)
                for oc in range(8):
                    pr.op("vector" if oc % 2 == 0 else "gpsimd",
                          lambda e, oc=oc: e.tensor_tensor(
                              x_t[:, T * oc:T * oc + T],
                              x_t[:, T * oc:T * oc + T],
                              u_t[:, T * oc:T * oc + T], alu.add),
                          reads=("x:%d" % oc, "gu:%d" % oc),
                          writes=("x:%d" % oc,))
                emit_ln(x_t, "x:%d", "s2", False, False, False,
                        cast_n="s1", sq_n="s3")
                warm(14)
                emit_ffn("s2")
                dma("gpsimd",
                    bass.AP(dout.tensor, iseq * D * seq_len + it * T,
                            [[seq_len, 128], [128 * seq_len, 8], [1, T]]),
                    bass.AP(x_t, 0, [[NCHUNK * T, 128], [T, 8], [1, T]]),
                    reads=tuple("x:%d" % c for c in range(NCHUNK)),
                    writes=("outd",), ch="out")

        pr.raw_wait("gpsimd", "dch_out", pr.cnt["dch_out"])
        pr.realize(blk)
    return nc



def kernel(**inputs):
    x = np.asarray(inputs["L0_attn_inputs"], dtype=np.float32)
    consts = host_constants(inputs)
    nc = build_program(n_seq=2, seq_len=4096, T=512)
    in_maps = []
    for core in range(8):
        xs = x[2 * core:2 * core + 2]
        xfm = np.ascontiguousarray(xs.transpose(0, 2, 1)).reshape(2 * D, 4096)
        m = {"x": xfm}
        m.update(consts)
        in_maps.append(m)
    res = run_bass_kernel_spmd(nc, in_maps, core_ids=list(range(8)))
    outs = []
    for core in range(8):
        o = np.asarray(res.results[core]["out"]).reshape(2, D, 4096)
        outs.append(o.transpose(0, 2, 1))
    return np.ascontiguousarray(np.concatenate(outs, axis=0), dtype=np.float32)


# version 25
# speedup vs baseline: 1.2449x; 1.0265x over previous
import numpy as np
from contextlib import ExitStack

import concourse.bass as bass
import concourse.mybir as mybir
from concourse.bass_utils import run_bass_kernel_spmd
from concourse.alu_op_type import AluOpType as alu

F32 = mybir.dt.float32
BF16 = mybir.dt.bfloat16
AF = mybir.ActivationFunctionType

D = 1024
HID = 4096
NCHUNK = 8
HCHUNK = 32
LN_EPS = 1e-5
NEWTON_ITERS = 2
DEBUG_TAPS = False


def _stage_chunk(w_s, s, c, transpose=False):
    S = 2 ** s
    M = np.zeros((128, 128), dtype=np.float32)
    w = np.asarray(w_s, dtype=np.float32).reshape(HID // (2 * S), S, 2, 2)
    for p in range(128):
        i = 128 * c + p
        if i & S:
            continue
        n1 = i // (2 * S)
        j = i % S
        M[p, p] = w[n1, j, 0, 0]
        M[p, p + S] = w[n1, j, 0, 1] if not transpose else w[n1, j, 1, 0]
        M[p + S, p] = w[n1, j, 1, 0] if not transpose else w[n1, j, 0, 1]
        M[p + S, p + S] = w[n1, j, 1, 1]
    return M


def _diag_vecs(w_s, s, c, transpose=False):
    S = 2 ** s
    w = np.asarray(w_s, dtype=np.float32).reshape(HID // (2 * S), S, 2, 2)
    vs = np.zeros(128, dtype=np.float32)
    vp = np.zeros(128, dtype=np.float32)
    for p in range(128):
        i = 128 * c + p
        if not (i & S):
            n1 = i // (2 * S)
            j = i % S
            vs[p] = w[n1, j, 0, 0]
            vp[p] = w[n1, j, 0, 1] if not transpose else w[n1, j, 1, 0]
        else:
            a = i - S
            n1 = a // (2 * S)
            j = a % S
            vs[p] = w[n1, j, 1, 1]
            vp[p] = w[n1, j, 1, 0] if not transpose else w[n1, j, 0, 1]
    return vs, vp


def host_constants(inp):
    c = {}
    c["wq"] = np.ascontiguousarray(inp["L0_attn_W_Q"], dtype=np.float32)
    c["wk"] = np.ascontiguousarray(inp["L0_attn_W_K"], dtype=np.float32)
    c["wv"] = np.ascontiguousarray(inp["L0_attn_W_V"], dtype=np.float32)
    c["wfc"] = np.ascontiguousarray(inp["L0_attn_W_fc"], dtype=np.float32)
    c["gamma"] = np.ascontiguousarray(
        np.asarray(inp["L0_attn_gamma"], np.float32).reshape(8, 128).T)
    c["beta"] = np.ascontiguousarray(
        np.asarray(inp["L0_attn_beta"], np.float32).reshape(8, 128).T)
    c["wexpT"] = np.ascontiguousarray(
        np.asarray(inp["L0_ffn_W_exp"], np.float32).T)
    c["wcon"] = np.ascontiguousarray(inp["L0_ffn_W_con"], dtype=np.float32)
    c["bexp"] = np.ascontiguousarray(
        np.asarray(inp["L0_ffn_b_exp"], np.float32).reshape(32, 128).T)
    c["upbias"] = np.ascontiguousarray(
        np.asarray(inp["L0_ffn_up_bias"], np.float32).reshape(32, 128).T)
    c["bcon"] = np.ascontiguousarray(
        np.asarray(inp["L0_ffn_b_con"], np.float32).reshape(8, 128).T)
    c["dbias"] = np.ascontiguousarray(
        np.asarray(inp["L0_ffn_down_bias"], np.float32).reshape(8, 128).T)

    upw = np.asarray(inp["L0_ffn_up_weights"], np.float32)
    dnw = np.asarray(inp["L0_ffn_down_weights"], np.float32)
    slo_u = np.zeros((7, 32, 128, 128), dtype=np.float32)
    slo_dT = np.zeros((7, 32, 128, 128), dtype=np.float32)
    for s in range(7):
        for ch in range(32):
            slo_u[s, ch] = _stage_chunk(upw[s], s, ch)
            slo_dT[s, ch] = _stage_chunk(dnw[s], s, ch, transpose=True)
    c["slo_u"] = np.ascontiguousarray(
        slo_u.transpose(1, 2, 0, 3).reshape(32 * 128, 7 * 128))
    c["slo_dT"] = np.ascontiguousarray(
        slo_dT.transpose(1, 2, 0, 3).reshape(32 * 128, 7 * 128))
    dv_u = np.zeros((128, 5 * 32 * 2), dtype=np.float32)
    dv_d = np.zeros((128, 5 * 32 * 2), dtype=np.float32)
    for si in range(5):
        s = si + 7
        for ch in range(32):
            vs_u, vp_u = _diag_vecs(upw[s], s, ch)
            dv_u[:, (si * 32 + ch) * 2 + 0] = vs_u
            dv_u[:, (si * 32 + ch) * 2 + 1] = vp_u
            vs_d, vp_d = _diag_vecs(dnw[s], s, ch, transpose=True)
            dv_d[:, (si * 32 + ch) * 2 + 0] = vs_d
            dv_d[:, (si * 32 + ch) * 2 + 1] = vp_d
    c["dv_u"] = dv_u
    c["dv_d"] = dv_d
    c["ident"] = np.eye(128, dtype=np.float32)
    ej = np.zeros((16, 8 * 128), dtype=np.float32)
    for j in range(8):
        ej[2 * j, 128 * j:128 * j + 64] = 1.0
        ej[2 * j + 1, 128 * j + 64:128 * j + 128] = 1.0
    c["ej"] = ej
    return c



ENGINES = ("tensor", "vector", "scalar", "sync", "gpsimd")


class Prog:
    def __init__(self, nc, ctx):
        self.nc = nc
        self.stream = {e: [] for e in ENGINES}
        self.sem = {}
        self.cnt = {}
        for e in ENGINES:
            self.sem[e] = ctx.enter_context(nc.semaphore("sem_" + e))
            self.cnt[e] = 0
        self.ctx = ctx
        self.wr = {}
        self.rd = {}
        self.floor = {e: {} for e in ENGINES}

    def _waits_for(self, eng, reads, writes):
        toks = []
        for r in reads:
            t = self.wr.get(r)
            if t:
                toks.append(t)
        for r in writes:
            t = self.wr.get(r)
            if t:
                toks.append(t)
            toks.extend(self.rd.get(r, ()))
        need = {}
        for s, v in toks:
            if v > need.get(s, -1):
                need[s] = v
        fl = self.floor[eng]
        out = []
        for s, v in sorted(need.items()):
            if fl.get(s, 0) >= v:
                continue
            assert v <= self.cnt[s], (eng, s, v, self.cnt[s])
            out.append((s, v))
            fl[s] = v
        return out

    def dma_ch(self, ch):
        name = "dch_" + ch
        if name not in self.sem:
            self.sem[name] = self.ctx.enter_context(nc_sem(self.nc, name))
            self.cnt[name] = 0
        return name

    def op(self, eng, build, reads=(), writes=(), dma=False, inc=True,
           ch=None):
        semname = self.dma_ch(ch) if dma else eng
        waits = self._waits_for(eng, reads, writes)
        tok = None
        if inc:
            self.cnt[semname] += 16 if dma else 1
            tok = (semname, self.cnt[semname])
        semh = self.sem[semname]
        amount = 16 if dma else 1
        sems = self.sem

        def closure(e, waits=waits, build=build, tok=tok):
            for s, v in waits:
                e.wait_ge(sems[s], v)
            instr = build(e)
            if tok is not None:
                instr.then_inc(semh, amount)
        self.stream[eng].append(closure)
        if tok is not None:
            for r in reads:
                self.rd.setdefault(r, []).append(tok)
            for r in writes:
                self.wr[r] = tok
                self.rd[r] = []
        return tok

    def mm_group(self, builders, reads, writes, reads_each=None):
        n = len(builders)
        for i, b in enumerate(builders[:-1]):
            r = reads + (tuple(reads_each[i]) if reads_each else ())
            self.op("tensor", b, reads=r, writes=writes, inc=False)
        r = reads + (tuple(reads_each[n - 1]) if reads_each else ())
        return self.op("tensor", builders[-1], reads=r, writes=writes)

    def raw_wait(self, eng, semname, value):
        if value <= 0:
            return
        fl = self.floor[eng]
        if fl.get(semname, 0) >= value:
            return
        fl[semname] = value
        sems = self.sem

        def closure(e):
            e.wait_ge(sems[semname], value)
        self.stream[eng].append(closure)

    def realize(self, blk):
        for e in ENGINES:
            ops = self.stream[e]

            def runner(eng, ops=ops):
                for f in ops:
                    f(eng)
            getattr(blk, e)(runner)



def nc_sem(nc, name):
    return nc.semaphore(name)


def build_program(n_seq=2, seq_len=4096, T=512):
    n_tiles = seq_len // T
    NG = T // 128
    nc = bass.Bass("TRN2", target_bir_lowering=False)

    dx = nc.dram_tensor("x", [n_seq * D, seq_len], F32,
                        kind="ExternalInput").ap()
    dout = nc.dram_tensor("out", [n_seq * D, seq_len], F32,
                          kind="ExternalOutput").ap()
    dins = {}
    for name, shape in [
        ("wq", [D, D]), ("wk", [D, D]), ("wv", [D, D]), ("wfc", [D, D]),
        ("gamma", [128, 8]), ("beta", [128, 8]),
        ("wexpT", [HID, D]), ("wcon", [HID, D]),
        ("bexp", [128, 32]), ("upbias", [128, 32]),
        ("bcon", [128, 8]), ("dbias", [128, 8]),
        ("slo_u", [32 * 128, 7 * 128]), ("slo_dT", [32 * 128, 7 * 128]),
        ("dv_u", [128, 5 * 32 * 2]), ("dv_d", [128, 5 * 32 * 2]),
        ("ident", [128, 128]), ("ej", [16, 8 * 128]),
    ]:
        dins[name] = nc.dram_tensor(name, shape, F32, kind="ExternalInput").ap()
    asp = nc.dram_tensor("asp", [128, 8 * HID], BF16).ap()
    wqsp = nc.dram_tensor("wqsp", [128, 8 * 1024], BF16).ap()
    wfsp = nc.dram_tensor("wfsp", [128, 8 * 1024], BF16).ap()
    x1sp = nc.dram_tensor("x1sp", [n_seq * D, seq_len], F32).ap()
    xn2hi = nc.dram_tensor("xn2hi", [n_seq * D, seq_len], BF16).ap()
    xn2lo = nc.dram_tensor("xn2lo", [n_seq * D, seq_len], BF16).ap()
    if DEBUG_TAPS:
        dbg1 = nc.dram_tensor("dbg1", [n_seq * D, seq_len], F32).ap()
        dbg2 = nc.dram_tensor("dbg2", [n_seq * D, seq_len], F32).ap()
        dbg3 = nc.dram_tensor("dbg3", [n_seq * D, seq_len], F32).ap()
        dbg4 = nc.dram_tensor("dbg4", [n_seq * D, seq_len], F32).ap()

    ctx = ExitStack()
    with ctx:
        S = lambda name, shape, dt: ctx.enter_context(
            nc.sbuf_tensor(name, shape, dt))
        c_sb = S("c_sb", [128, HCHUNK * 1024], BF16)
        w_a = S("w_a", [128, NCHUNK * 1024], BF16)
        w_b = S("w_b", [128, NCHUNK * 1024], BF16)
        ident_f = S("ident_f", [128, 128], F32)
        ident_b = S("ident_b", [128, 128], BF16)
        onesc = S("onesc", [128, 1], BF16)
        onesr = S("onesr", [1, 512], BF16)
        onesr_f = S("onesr_f", [1, 128], F32)
        epsr = S("epsr", [1, 1], BF16)
        ej_b = S("ej_b", [16, 8 * 128], BF16)
        bfull = S("bfull", [128, 32], F32)
        bvec = S("bvec", [128, 32], BF16)
        upb_f = S("upb_f", [128, 32], F32)
        cch = S("cch", [128, 8], F32)
        gam = S("gam", [128, 8], F32)
        bet = S("bet", [128, 8], F32)
        kvbd = S("kvbd", [128, n_seq * 8 * 128], BF16)
        ksc = S("ksc", [128, n_seq * 8 * 16], BF16)
        x_t = S("x_t", [128, NCHUNK * T], F32)
        s1 = S("s1", [128, NCHUNK * T], BF16)
        s2 = S("s2", [128, NCHUNK * T], BF16)
        s3 = S("s3", [128, NCHUNK * T], BF16)
        g_t = S("g_t", [128, HCHUNK * T], BF16)
        u_t = g_t.bitcast(F32)
        kv_k = S("kv_k", [128, 1024], BF16)
        kv_v = S("kv_v", [128, 8 * 129], BF16)
        a_st = [S("a_st0", [128, 8 * 128], BF16),
                S("a_st1", [128, 8 * 128], BF16),
                S("a_st2", [128, 8 * 128], BF16)]
        stats_sb = S("stats_sb", [33, T], F32)
        small = S("small", [128, 48], F32)
        small_i = small.bitcast(mybir.dt.int32)
        den_sb = S("den_sb", [16, T], F32)
        dsp = S("dsp", [128, 64], F32)
        rdl_sp = S("rdl_sp", [128, 64], F32)
        rdh_sb = S("rdh_sb", [16, T], BF16)
        rdl_sb = S("rdl_sb", [16, T], BF16)
        
        wtmp = S("wtmp", [128, 1024], F32)
        rdb_sb = wtmp
        stg_all = S("stg_all", [128, 2688], BF16)
        stg_fv = stg_all.bitcast(F32)
        kvacc = stg_fv
        stg_f = stg_fv
        STGB0 = 1792
        dvv = S("dvv", [128, 320], F32)
        dgb = S("dgb", [128, 8 * 128], BF16)
        ypp = S("ypp", [128, 2 * 128], BF16)
        trn = S("trn", [128, 2 * 128], BF16)

        P = [ctx.enter_context(nc.psum_tensor("pb%d" % i, [128, 512], F32))
             for i in range(8)]

        blk = ctx.enter_context(nc.Block())
        pr = Prog(nc, ctx)

        tens = lambda b, **kw: pr.op("tensor", b, **kw)
        vec = lambda b, **kw: pr.op("vector", b, **kw)
        sca = lambda b, **kw: pr.op("scalar", b, **kw)

        SERIAL_CH = {"misc"}

        def dma(queue, dst, src, reads=(), writes=(), ch="misc"):
            semname = pr.dma_ch(ch)
            if ch in SERIAL_CH:
                pr.raw_wait(queue, semname, pr.cnt[semname])
            return pr.op(queue, lambda e: e.dma_start(dst, src),
                         reads=reads, writes=writes, dma=True, ch=ch)

        def warm(n):
            for _ in range(n):
                pr.op("tensor", lambda e: e.matmul(
                    P[7][:, 0:512], c_sb[:, 0:128], c_sb[:, 0:512],
                    start=True, stop=True, skip_group_check=True), inc=False)

        ALLC = tuple("C:%d" % k for k in range(HCHUNK))
        ALLWA = tuple("wa:%d" % d for d in range(8))
        ALLWB = tuple("wb:%d" % d for d in range(8))

        dma("sync", ident_f[:, :], dins["ident"][:, :], writes=("idf",), ch="misc")
        vec(lambda e: e.tensor_copy(ident_b[:, :], ident_f[:, :]),
            reads=("idf",), writes=("idb",))
        vec(lambda e: e.memset(onesc[:, :], 1.0), writes=("ones",))
        vec(lambda e: e.memset(onesr[:, :], 1.0), writes=("onesr",))
        vec(lambda e: e.memset(onesr_f[:, :], 1.0), writes=("onesrf",))
        vec(lambda e: e.memset(epsr[:, :], float(D) * LN_EPS), writes=("epsr",))
        for h in range(2):
            dma("sync", wtmp[0:16, 0:512],
                dins["ej"][:, 512 * h:512 * h + 512], writes=("wtmp0",),
                ch="wload0")
            vec(lambda e, h=h: e.tensor_copy(
                ej_b[:, 512 * h:512 * h + 512], wtmp[0:16, 0:512]),
                reads=("wtmp0",), writes=("ej",))
        dma("sync", gam[:, :], dins["gamma"][:, :], writes=("gam",), ch="misc")
        dma("sync", bet[:, :], dins["beta"][:, :], writes=("bet",), ch="misc")
        dma("sync", upb_f[:, :], dins["upbias"][:, :], writes=("upb",), ch="misc")
        dma("sync", cch[:, :], dins["bcon"][:, :], writes=("cch",), ch="misc")
        dma("sync", wtmp[:, 0:8], dins["dbias"][:, :], writes=("wtmp0",), ch="wload0")
        vec(lambda e: e.scalar_tensor_tensor(cch[:, :], wtmp[:, 0:8], 1.0,
                                             cch[:, :], alu.mult, alu.add),
            reads=("wtmp0", "cch"), writes=("cch",))
        vec(lambda e: e.tensor_scalar(cch[:, :], cch[:, :], 0.5, None,
                                      alu.mult, alu.bypass),
            reads=("cch",), writes=("cch",))
        dma("sync", wtmp[:, 0:32], dins["bexp"][:, :], writes=("wtmp0",), ch="wload0")
        vec(lambda e: e.tensor_copy(bvec[:, :], wtmp[:, 0:32]),
            reads=("wtmp0",), writes=tuple("bv:%d" % ch for ch in range(32)))
        vec(lambda e: e.memset(kvbd[:, :], 0.0), writes=("kvbd",))
        vec(lambda e: e.memset(ksc[:, :], 0.0), writes=("ksc",))
        vec(lambda e: e.memset(
            bass.AP(kv_v, 128, [[8 * 129, 128], [129, 8]]), 1.0),
            writes=("kvv1",))

        def emit_x_load(iseq, it):
            for h in range(2):
                xk = tuple("x:%d" % c for c in range(4 * h, 4 * h + 4))
                dma("scalar",
                    bass.AP(x_t, 4 * h * T,
                            [[NCHUNK * T, 128], [T, 4], [1, T]]),
                    bass.AP(dx.tensor,
                            iseq * D * seq_len + 4 * h * 128 * seq_len + it * T,
                            [[seq_len, 128], [128 * seq_len, 4], [1, T]]),
                    reads=xk, writes=xk, ch="x%d" % h)


        def load_w(dsrc, dst_sb, tag):
            for i in range(16):
                ch, h = divmod(i, 2)
                p = i % 2
                dma("sync" if p == 0 else "gpsimd",
                    wtmp[:, 512 * p:512 * p + 512],
                    dsrc[128 * ch:128 * ch + 128, 512 * h:512 * h + 512],
                    writes=("wtmp%d" % p,),
                    ch="wload0" if p == 0 else "wload1g")
                vec(lambda e, ch=ch, h=h, p=p, dst_sb=dst_sb: e.tensor_copy(
                    dst_sb[:, 1024 * ch + 512 * h:
                           1024 * ch + 512 * h + 512],
                    wtmp[:, 512 * p:512 * p + 512]),
                    reads=("wtmp%d" % p,), writes=("%s:%d" % (tag, ch),))

        def fill_R_ops(src):
            ops = []
            for i in range(64):
                ch, h = divmod(i, 2)
                p = i % 2

                def one(ch=ch, h=h, p=p):
                    dma("sync" if p == 0 else "gpsimd",
                        wtmp[:, 512 * p:512 * p + 512],
                        src[128 * ch:128 * ch + 128,
                            512 * h:512 * h + 512],
                        writes=("wtmp%d" % p,),
                        ch="wload0" if p == 0 else "wload1g")
                    pr.op("gpsimd" if p == 0 else "vector",
                          lambda e, ch=ch, h=h, p=p: e.tensor_copy(
                              c_sb[:, 1024 * ch + 512 * h:
                                   1024 * ch + 512 * h + 512],
                              wtmp[:, 512 * p:512 * p + 512]),
                          reads=("wtmp%d" % p,), writes=("C:%d" % ch,))
                ops.append(one)
            return ops

        def fill_R_gp(src):
            for i in range(64):
                ch, h = divmod(i, 2)
                p = i % 2
                dma("sync" if p == 0 else "gpsimd",
                    wtmp[:, 512 * p:512 * p + 512],
                    src[128 * ch:128 * ch + 128, 512 * h:512 * h + 512],
                    writes=("wtmp%d" % p,),
                    ch="wload0" if p == 0 else "wload1g")
                vec(lambda e, ch=ch, h=h, p=p: e.tensor_copy(
                    c_sb[:, 1024 * ch + 512 * h:
                         1024 * ch + 512 * h + 512],
                    wtmp[:, 512 * p:512 * p + 512]),
                    reads=("wtmp%d" % p,), writes=("C:%d" % ch,))

        def fill_R(src):
            for op in fill_R_ops(src):
                op()

        def load_stage_chunk(src_ap, ch):
            p = ch % 2
            base = p * 1024 + ((ch // 2) % 2) * 2048
            toks = ("gu:%d" % (base // 512), "gu:%d" % (base // 512 + 1))
            dma("scalar" if p == 0 else "sync",
                bass.AP(u_t, base, [[8192, 128], [1, 896]]),
                bass.AP(src_ap.tensor, ch * 128 * 896,
                        [[896, 128], [1, 896]]),
                writes=toks, ch="stg" if p == 0 else "stg1")
            return base, toks

        def lo_chain_pair(landA, landB, down, fill):
            holds = {}
            st = {0: 0, 1: 0}
            order = range(0, 7) if down else range(6, -1, -1)
            for k, s in enumerate(order):
                for p, (base, ltoks) in ((0, landA), (1, landB)):
                    lc = base + s * 128
                    lhs = u_t[:, lc:lc + 128]
                    pb = 4 + p
                    if k == 0:
                        rhs, rkeyn = ident_f[:, :], ("idf",)
                    else:
                        cur = 2 * p + st[p]
                        rhs = stg_fv[:, cur * 128:cur * 128 + 128]
                        rkeyn = ("cs:%d" % cur,)
                    tens(lambda e, lhs=lhs, rhs=rhs, pb=pb: e.matmul(
                        P[pb][0:128, 0:128], lhs, rhs, start=True, stop=True),
                        reads=ltoks + rkeyn, writes=("p%d" % pb,))
                    if k == 6:
                        sca(lambda e, p=p, pb=pb: e.activation(
                            kv_k[:, 128 * p:128 * p + 128],
                            P[pb][0:128, 0:128], AF.Identity),
                            reads=("p%d" % pb,), writes=("kvh:%d" % p,))
                        holds[p] = (kv_k[:, 128 * p:128 * p + 128],
                                    "kvh:%d" % p)
                    else:
                        nxt = 1 - st[p]
                        cn = 2 * p + nxt
                        sca(lambda e, cn=cn, pb=pb:
                            e.activation(stg_fv[:, cn * 128:cn * 128 + 128],
                                         P[pb][0:128, 0:128], AF.Identity),
                            reads=("p%d" % pb,),
                            writes=("cs:%d" % cn,))
                        st[p] = nxt
                    if fill:
                        fill.pop(0)()
            return holds

        def cupd_ops(ch, hold, down):
            pc, pkey = hold
            ops = []
            pbs = (0, 1) if ch % 2 == 0 else (2, 3)
            for h in range(2):
                pb = pbs[h]

                def mm(e, pc=pc, ch=ch, h=h, pb=pb):
                    return e.matmul(
                        P[pb][:, :], pc,
                        c_sb[:, 1024 * ch + 512 * h:1024 * ch + 512 * h + 512],
                        start=True, stop=True)
                ops.append((lambda mm=mm, pkey=pkey, ch=ch, pb=pb:
                            tens(mm, reads=(pkey, "C:%d" % ch),
                                 writes=("p%d" % pb,))))
                if down:
                    ops.append((lambda ch=ch, h=h, pb=pb:
                                vec(lambda e: e.tensor_scalar(
                                    c_sb[:, 1024 * ch + 512 * h:
                                         1024 * ch + 512 * h + 512],
                                    P[pb][:, :], 0.5, None,
                                    alu.mult, alu.bypass),
                                    reads=("p%d" % pb,),
                                    writes=("C:%d" % ch,))))
                else:
                    ops.append((lambda ch=ch, h=h, pb=pb:
                                vec(lambda e: e.tensor_copy(
                                    c_sb[:, 1024 * ch + 512 * h:
                                         1024 * ch + 512 * h + 512],
                                    P[pb][:, :]),
                                    reads=("p%d" % pb,),
                                    writes=("C:%d" % ch,))))
            if not down:
                pbb = 6 if ch % 2 == 0 else 7
                ops.append((lambda pc=pc, pkey=pkey, ch=ch, pbb=pbb:
                            tens(lambda e: e.matmul(
                                P[pbb][0:128, 0:1], pc, bvec[:, ch:ch + 1],
                                start=True, stop=True),
                                reads=(pkey, "bv:%d" % ch),
                                writes=("p%d" % pbb,))))
                ops.append((lambda ch=ch, pbb=pbb:
                            vec(lambda e: e.tensor_copy(
                                bvec[:, ch:ch + 1], P[pbb][0:128, 0:1]),
                                reads=("p%d" % pbb,),
                                writes=("bv:%d" % ch,))))
            return ops

        def lo_stages(src_key, down):
            lands = {0: load_stage_chunk(dins[src_key], 0),
                     1: load_stage_chunk(dins[src_key], 1)}
            fill = []
            for base in range(0, 32, 2):
                holds = lo_chain_pair(lands.pop(base), lands.pop(base + 1),
                                      down, fill)
                if base + 2 < 32:
                    lands[base + 2] = load_stage_chunk(dins[src_key],
                                                       base + 2)
                    lands[base + 3] = load_stage_chunk(dins[src_key],
                                                       base + 3)
                while fill:
                    fill.pop(0)()
                fill = (cupd_ops(base, holds[0], down)
                        + cupd_ops(base + 1, holds[1], down))
            while fill:
                fill.pop(0)()

        def hi_stages(dv_key, bias_too):
            npair = 0
            for si in range(5):
                k = 1 << si
                for cbase in range(32):
                    if cbase & k:
                        continue
                    cp = cbase + k
                    db = 512 * (npair % 2)
                    dk = 4 * (npair % 2)
                    npair += 1
                    cols = [(si * 32 + cbase) * 2, (si * 32 + cbase) * 2 + 1,
                            (si * 32 + cp) * 2, (si * 32 + cp) * 2 + 1]
                    for q, col in enumerate(cols):
                        pr.op("vector" if q % 2 == 0 else "gpsimd",
                              lambda e, q=q, col=col, db=db: e.tensor_scalar(
                                  dgb[:, db + 128 * q:db + 128 * q + 128],
                                  ident_b[:, :],
                                  dvv[:, col:col + 1], None,
                                  alu.mult, alu.bypass),
                              reads=("idb", "dvv"),
                              writes=("dgb:%d" % (dk + q),))
                    for h in range(2):
                        a0 = 1024 * cbase + 512 * h
                        a1 = 1024 * cp + 512 * h
                        pr.mm_group(
                            [lambda e, a0=a0, h=h, db=db: e.matmul(
                                P[h][:, :], dgb[:, db:db + 128],
                                c_sb[:, a0:a0 + 512], start=True, stop=False),
                             lambda e, a1=a1, h=h, db=db: e.matmul(
                                P[h][:, :], dgb[:, db + 128:db + 256],
                                c_sb[:, a1:a1 + 512], start=False, stop=True)],
                            reads=("dgb:%d" % dk, "dgb:%d" % (dk + 1),
                                   "C:%d" % cbase, "C:%d" % cp),
                            writes=("p%d" % h,))
                        pr.mm_group(
                            [lambda e, a1=a1, h=h, db=db: e.matmul(
                                P[2 + h][:, :], dgb[:, db + 256:db + 384],
                                c_sb[:, a1:a1 + 512], start=True, stop=False),
                             lambda e, a0=a0, h=h, db=db: e.matmul(
                                P[2 + h][:, :], dgb[:, db + 384:db + 512],
                                c_sb[:, a0:a0 + 512], start=False, stop=True)],
                            reads=("dgb:%d" % (dk + 2), "dgb:%d" % (dk + 3),
                                   "C:%d" % cbase, "C:%d" % cp),
                            writes=("p%d" % (2 + h),))
                    if bias_too:
                        pr.mm_group(
                            [lambda e, db=db: e.matmul(
                                P[4][0:128, 0:1], dgb[:, db:db + 128],
                                bvec[:, cbase:cbase + 1],
                                start=True, stop=False),
                             lambda e, db=db: e.matmul(
                                P[4][0:128, 0:1], dgb[:, db + 128:db + 256],
                                bvec[:, cp:cp + 1], start=False, stop=True)],
                            reads=("dgb:%d" % dk, "dgb:%d" % (dk + 1),
                                   "bv:%d" % cbase, "bv:%d" % cp),
                            writes=("p4",))
                        pr.mm_group(
                            [lambda e, db=db: e.matmul(
                                P[4][0:128, 1:2], dgb[:, db + 256:db + 384],
                                bvec[:, cp:cp + 1], start=True, stop=False),
                             lambda e, db=db: e.matmul(
                                P[4][0:128, 1:2], dgb[:, db + 384:db + 512],
                                bvec[:, cbase:cbase + 1],
                                start=False, stop=True)],
                            reads=("dgb:%d" % (dk + 2), "dgb:%d" % (dk + 3),
                                   "bv:%d" % cbase, "bv:%d" % cp),
                            writes=("p4",))
                    for h in range(2):
                        sca(lambda e, cbase=cbase, h=h: e.activation(
                            c_sb[:, 1024 * cbase + 512 * h:
                                 1024 * cbase + 512 * h + 512], P[h][:, :],
                            AF.Identity),
                            reads=("p%d" % h,), writes=("C:%d" % cbase,))
                        sca(lambda e, cp=cp, h=h: e.activation(
                            c_sb[:, 1024 * cp + 512 * h:
                                 1024 * cp + 512 * h + 512], P[2 + h][:, :],
                            AF.Identity),
                            reads=("p%d" % (2 + h),), writes=("C:%d" % cp,))
                    if bias_too:
                        vec(lambda e, cbase=cbase: e.tensor_copy(
                            bvec[:, cbase:cbase + 1], P[4][0:128, 0:1]),
                            reads=("p4",), writes=("bv:%d" % cbase,))
                        vec(lambda e, cp=cp: e.tensor_copy(
                            bvec[:, cp:cp + 1], P[4][0:128, 1:2]),
                            reads=("p4",), writes=("bv:%d" % cp,))

        dma("scalar", dvv[:, :], dins["dv_u"][:, :], writes=("dvv",), ch="dvv")
        emit_x_load(0, 0)
        fill_R_gp(dins["wexpT"])
        lo_stages("slo_u", down=False)
        hi_stages("dv_u", bias_too=True)
        vec(lambda e: e.tensor_copy(bfull[:, :], bvec[:, :]),
            reads=tuple("bv:%d" % ch for ch in range(32)), writes=("bfull",))
        vec(lambda e: e.scalar_tensor_tensor(bfull[:, :], upb_f[:, :], 1.0,
                                             bfull[:, :], alu.mult, alu.add),
            reads=("upb", "bfull"), writes=("bfull",))

        for ch in range(32):
            for dj in range(4):
                par = (ch * 4 + dj) % 4
                pb = 4 + par
                for t_ in range(2):
                    d = 2 * dj + t_
                    tens(lambda e, ch=ch, d=d, pb=pb, t_=t_: e.matmul(
                        P[pb][0:128, 128 * t_:128 * t_ + 128],
                        c_sb[:, 1024 * ch + 128 * d:1024 * ch + 128 * d + 128],
                        ident_b[:, :], start=(t_ == 0), stop=(t_ == 1)),
                        reads=("C:%d" % ch, "idb"), writes=("p%d" % pb,))
                slot = dgb[:, 256 * par:256 * par + 256]
                skeys = ("dgb:%d" % (2 * par), "dgb:%d" % (2 * par + 1))
                if par % 2 == 0:
                    sca(lambda e, slot=slot, pb=pb: e.activation(
                        slot, P[pb][0:128, 0:256], AF.Identity),
                        reads=("p%d" % pb,), writes=skeys)
                else:
                    vec(lambda e, slot=slot, pb=pb: e.tensor_copy(
                        slot, P[pb][0:128, 0:256]),
                        reads=("p%d" % pb,), writes=skeys)
                dma("gpsimd" if par < 2 else "sync",
                    bass.AP(asp.tensor, 1024 * ch + 256 * dj,
                            [[8 * HID, 128], [1, 256]]),
                    slot,
                    reads=skeys, writes=("asp:%d:%d" % (ch, dj),),
                    ch="asp%d" % par)

        fill_R_gp(dins["wcon"])
        dma("scalar", dvv[:, :], dins["dv_d"][:, :], writes=("dvv",), ch="dvv")
        load_w(dins["wk"], w_a, "wa")
        load_w(dins["wv"], w_b, "wb")
        hi_stages("dv_d", bias_too=False)
        lo_stages("slo_dT", down=True)

        for i in range(16):
            for p, (dsrc, dspill) in enumerate(((dins["wq"], wqsp),
                                                (dins["wfc"], wfsp))):
                ch, h = divmod(i, 2)
                ceng = "vector" if p == 0 else "gpsimd"
                ktok = "kvkA" if p == 0 else "kvkB"
                extra = ("kvh:0", "kvh:1") if p == 0 else ()
                dma("sync" if p == 0 else "gpsimd",
                    wtmp[:, 512 * p:512 * p + 512],
                    dsrc[128 * ch:128 * ch + 128, 512 * h:512 * h + 512],
                    writes=("wtmp%d" % p,),
                    ch="wload0" if p == 0 else "wload1g")
                pr.op(ceng, lambda e, p=p: e.tensor_copy(
                    kv_k[:, 512 * p:512 * p + 512],
                    wtmp[:, 512 * p:512 * p + 512]),
                    reads=("wtmp%d" % p,), writes=(ktok,) + extra)
                dma("sync" if p == 0 else "gpsimd",
                    bass.AP(dspill.tensor, 1024 * ch + 512 * h,
                            [[8192, 128], [1, 512]]),
                    kv_k[:, 512 * p:512 * p + 512],
                    reads=(ktok,), writes=("wsp%d" % p,),
                    ch="wspo%d" % p)

        SB = {"s1": s1, "s2": s2, "s3": s3}
        def skey(src_key, c):
            return src_key % c if "%d" in src_key else src_key
        GUALL = tuple("gu:%d" % _c for _c in range(8))
        def wkeys(name, c=None):
            if c is None:
                base = "s1x" if name == "s1" else name
                return (name,) + tuple("%s:%d" % (base, _c) for _c in range(8))
            base = "s1x" if name == "s1" else name
            return ("%s:%d" % (base, _c if False else c),)
        def rkey(name, c):
            base = "s1x" if name == "s1" else name
            return "%s:%d" % (base, c)
        def emit_rows_and_bcast_r(sp_sb, rcol, rkeyn, hilo, ng):
            for g in range(ng):
                tens(lambda e, g=g: e.transpose(
                    P[4][0:1, 128 * g:128 * g + 128],
                    sp_sb[:, rcol + g:rcol + g + 1], ident_f[:, :]),
                    reads=(rkeyn,), writes=("p4",))
            vec(lambda e, ng=ng: e.tensor_copy(stats_sb[0:1, 0:128 * ng],
                                               P[4][0:1, 0:128 * ng]),
                reads=("p4",), writes=("stats",))
            for g in range(ng):
                tens(lambda e, g=g: e.matmul(
                    P[4][0:128, 128 * g:128 * g + 128], onesr_f[0:1, 0:128],
                    stats_sb[0:1, 128 * g:128 * g + 128],
                    start=True, stop=True),
                    reads=("onesrf", "stats"), writes=("p4",))

        def emit_rows_and_bcast_m(sp_sb, mcol, mkeyn, hilo, ng):
            for g in range(ng):
                tens(lambda e, g=g: e.transpose(
                    P[6][0:1, 128 * g:128 * g + 128],
                    sp_sb[:, mcol + g:mcol + g + 1], ident_f[:, :]),
                    reads=(mkeyn,), writes=("p6",))
            vec(lambda e, ng=ng: e.tensor_copy(stats_sb[0:1, 0:128 * ng],
                                               P[6][0:1, 0:128 * ng]),
                reads=("p6", "stats"), writes=("statsm", "stats"))
            for g in range(ng):
                tens(lambda e, g=g: e.matmul(
                    P[6][0:128, 128 * g:128 * g + 128], onesr_f[0:1, 0:128],
                    stats_sb[0:1, 128 * g:128 * g + 128],
                    start=True, stop=True),
                    reads=("onesrf", "statsm"), writes=("p6",))

        def emit_bcast_normalize(sp_sb, rcol, mcol, rkeyn, mkeyn,
                                 src_f32, src_key, outname,
                                 need_f32, hilo, affine, ng, use_bf_src,
                                 cast_n="s1"):
            nrm_eng = lambda c: "vector"
            emit_rows_and_bcast_r(sp_sb, rcol, rkeyn, hilo, ng)
            emit_rows_and_bcast_m(sp_sb, mcol, mkeyn, hilo, ng)
            if use_bf_src:
                ttile, tsrc_t = SB[outname], SB[cast_n]
                for c in range(NCHUNK):
                    pr.op(nrm_eng(c),
                          lambda e, c=c, ttile=ttile, tsrc_t=tsrc_t:
                          e.tensor_tensor(
                              ttile[:, T * c:T * c + T],
                              tsrc_t[:, T * c:T * c + T],
                              P[4][0:128, 0:T], alu.mult),
                          reads=(rkey(cast_n, c), "p4"),
                          writes=wkeys(outname, c))
            else:
                for c in range(NCHUNK):
                    pr.op(nrm_eng(c), lambda e, c=c: e.tensor_tensor(
                        u_t[:, T * c:T * c + T], src_f32[:, T * c:T * c + T],
                        P[4][0:128, 0:T], alu.mult),
                        reads=(skey(src_key, c), "p4", "gu:%d" % c),
                        writes=("gu:%d" % c,))
            if use_bf_src:
                out_bf = SB[outname]
                for c in range(NCHUNK):
                    pr.op(nrm_eng(c),
                          lambda e, c=c, out_bf=out_bf: e.tensor_tensor(
                              out_bf[:, T * c:T * c + T],
                              out_bf[:, T * c:T * c + T],
                              P[6][0:128, 0:T], alu.subtract),
                          reads=(rkey(outname, c), "p6"),
                          writes=wkeys(outname, c))
            else:
                for c in range(NCHUNK):
                    pr.op(nrm_eng(c), lambda e, c=c: e.tensor_tensor(
                        u_t[:, T * c:T * c + T], u_t[:, T * c:T * c + T],
                        P[6][0:128, 0:T], alu.subtract),
                        reads=("gu:%d" % c, "p6"), writes=("gu:%d" % c,))
            if affine:
                for c in range(NCHUNK):
                    sca(lambda e, c=c: e.activation(
                        u_t[:, T * c:T * c + T], u_t[:, T * c:T * c + T],
                        AF.Identity, bias=bet[:, c:c + 1],
                        scale=gam[:, c:c + 1]),
                        reads=("gu:%d" % c, "gam", "bet"),
                        writes=("gu:%d" % c,))
            if need_f32 and outname is not None:
                out_bf = SB[outname]
                for c in range(NCHUNK):
                    sca(lambda e, c=c, out_bf=out_bf: e.activation(
                        out_bf[:, T * c:T * c + T], u_t[:, T * c:T * c + T],
                        AF.Identity),
                        reads=("gu:%d" % c,), writes=wkeys(outname, c))

        def emit_ln_stats_chunk(src_f32, src_key, c, cast_n, sq_n,
                                cast_eng="vector", sq_eng="vector"):
            ct, qt = SB[cast_n], SB[sq_n]
            if cast_eng == "scalar":
                sca(lambda e, c=c, ct=ct: e.activation(
                    ct[:, T * c:T * c + T], src_f32[:, T * c:T * c + T],
                    AF.Identity),
                    reads=(skey(src_key, c),), writes=wkeys(cast_n, c))
            else:
                pr.op(cast_eng, lambda e, c=c, ct=ct: e.tensor_copy(
                    ct[:, T * c:T * c + T], src_f32[:, T * c:T * c + T]),
                    reads=(skey(src_key, c),), writes=wkeys(cast_n, c))
            pr.op(sq_eng, lambda e, c=c, ct=ct, qt=qt: e.tensor_tensor(
                qt[:, T * c:T * c + T], ct[:, T * c:T * c + T],
                ct[:, T * c:T * c + T], alu.mult),
                reads=(rkey(cast_n, c),), writes=wkeys(sq_n, c))

        def emit_ln_stats_mm(c, cast_n, sq_n):
            ct, qt = SB[cast_n], SB[sq_n]
            tens(lambda e, c=c, ct=ct: e.matmul(
                P[4][0:1, 0:T], onesc[:, 0:1], ct[:, T * c:T * c + T],
                start=(c == 0), stop=(c == 7), skip_group_check=True),
                reads=("ones", rkey(cast_n, c)), writes=("p4",))
            tens(lambda e, c=c, qt=qt: e.matmul(
                P[5][0:1, 0:T], onesc[:, 0:1], qt[:, T * c:T * c + T],
                start=(c == 0), stop=False, skip_group_check=True),
                reads=("ones", rkey(sq_n, c)), writes=("p5",))
            if c == 7:
                tens(lambda e: e.matmul(
                    P[5][0:1, 0:T], epsr[0:1, 0:1], onesr[0:1, 0:T],
                    start=False, stop=True, skip_group_check=True),
                    reads=("epsr", "onesr"), writes=("p5",))

        def emit_ln_head():
            ng = NG
            vec(lambda e: e.tensor_copy(stats_sb[0:1, 0:T], P[4][0:1, 0:T]),
                reads=("p4",), writes=("stats",))
            vec(lambda e: e.tensor_copy(stats_sb[32:33, 0:T], P[5][0:1, 0:T]),
                reads=("p5",), writes=("stats",))
            for g in range(ng):
                tens(lambda e, g=g: e.transpose(
                    P[4][0:128, g:g + 1],
                    stats_sb[0:1, 128 * g:128 * g + 128], ident_f[0:1, 0:1]),
                    reads=("stats",), writes=("p4",))
                tens(lambda e, g=g, ng=ng: e.transpose(
                    P[4][0:128, ng + g:ng + g + 1],
                    stats_sb[32:33, 128 * g:128 * g + 128],
                    ident_f[32:33, 32:33]),
                    reads=("stats",), writes=("p4",))
            vec(lambda e, ng=ng: e.tensor_copy(small[:, 0:2 * ng],
                                               P[4][0:128, 0:2 * ng]),
                reads=("p4",), writes=("sm0",))
            mu0, e20, t10, t20, r0, m20 = 8, 12, 16, 20, 24, 28
            vec(lambda e, ng=ng: e.tensor_scalar(
                small[:, mu0:mu0 + 2 * ng], small[:, 0:2 * ng], 1.0 / D, None,
                alu.mult, alu.bypass),
                reads=("sm0",), writes=("smmu", "sme2"))
            vec(lambda e, ng=ng: e.tensor_tensor(
                small[:, t10:t10 + ng], small[:, mu0:mu0 + ng],
                small[:, mu0:mu0 + ng], alu.mult),
                reads=("smmu",), writes=("smt1",))
            vec(lambda e, ng=ng: e.scalar_tensor_tensor(
                small[:, t20:t20 + ng], small[:, t10:t10 + ng], -1.0,
                small[:, e20:e20 + ng], alu.mult, alu.add),
                reads=("smt1", "sme2"), writes=("smv",))
            vec(lambda e, ng=ng: e.tensor_copy(
                small[:, t10:t10 + ng], small_i[:, t20:t20 + ng]),
                reads=("smv",), writes=("smt1",))
            vec(lambda e, ng=ng: e.tensor_scalar(
                small[:, t10:t10 + ng], small[:, t10:t10 + ng], -0.5,
                1597463007.0, alu.mult, alu.add),
                reads=("smt1",), writes=("smt1",))
            vec(lambda e, ng=ng: e.tensor_copy(
                small_i[:, r0:r0 + ng], small[:, t10:t10 + ng]),
                reads=("smt1",), writes=("smr",))
            for _ in range(NEWTON_ITERS):
                vec(lambda e, ng=ng: e.tensor_tensor(
                    small[:, t10:t10 + ng], small[:, r0:r0 + ng],
                    small[:, r0:r0 + ng], alu.mult),
                    reads=("smr",), writes=("smt1",))
                vec(lambda e, ng=ng: e.scalar_tensor_tensor(
                    small[:, m20:m20 + ng], small[:, t10:t10 + ng], -0.5,
                    small[:, t20:t20 + ng], alu.mult, alu.mult),
                    reads=("smt1", "smv"), writes=("smt2",))
                vec(lambda e, ng=ng: e.scalar_tensor_tensor(
                    small[:, r0:r0 + ng], small[:, m20:m20 + ng], 1.5,
                    small[:, r0:r0 + ng], alu.add, alu.mult),
                    reads=("smt2", "smr"), writes=("smr",))
            vec(lambda e, ng=ng: e.tensor_tensor(
                small[:, m20:m20 + ng], small[:, mu0:mu0 + ng],
                small[:, r0:r0 + ng], alu.mult),
                reads=("smmu", "smr"), writes=("smm2",))

        def emit_ln_tail2(src_f32, src_key, outname, need_f32, hilo,
                          affine, cast_n="s1"):
            r0, m20 = 24, 28
            emit_bcast_normalize(small, r0, m20, "smr", "smm2",
                                 src_f32, src_key, outname,
                                 need_f32, hilo, affine, NG,
                                 use_bf_src=not (need_f32 or affine),
                                 cast_n=cast_n)

        def emit_ln_tail(src_f32, src_key, outname, need_f32, hilo,
                         affine, cast_n="s1", sq_n="s2"):
            emit_ln_head()
            emit_ln_tail2(src_f32, src_key, outname, need_f32, hilo,
                          affine, cast_n=cast_n)

        def emit_ln(src_f32, src_key, outname, need_f32, hilo,
                    affine, cast_n="s1", sq_n="s2"):
            for c in range(NCHUNK):
                emit_ln_stats_chunk(src_f32, src_key, c, cast_n, sq_n)
                emit_ln_stats_mm(c, cast_n, sq_n)
            emit_ln_tail(src_f32, src_key, outname, need_f32, hilo,
                         affine, cast_n, sq_n)

        aqueues = ["sync", "gpsimd", "scalar"]

        def emit_ffn(srcname="s3", post_oc=None):
            src = SB[srcname]
            for o in range(HCHUNK):
                par = o % 3
                q = "sync"
                dma(q, bass.AP(a_st[par], 0, [[1024, 128], [1, 1024]]),
                    bass.AP(asp.tensor, 1024 * o,
                            [[8 * HID, 128], [1, 1024]]),
                    reads=tuple("asp:%d:%d" % (o, dj) for dj in range(4)),
                    writes=("ast:%d" % par,), ch="ast%d" % par)
                pr.mm_group(
                    [(lambda e, d=d, par=par, src=src: e.matmul(
                        P[par][:, 0:T], a_st[par][:, 128 * d:128 * d + 128],
                        src[:, T * d:T * d + T],
                        start=(d == 0), stop=(d == 7)))
                     for d in range(8)],
                    reads=("ast:%d" % par,), writes=("p%d" % par,),
                    reads_each=[(rkey(srcname, d),) for d in range(8)])
                sca(lambda e, o=o, par=par: e.activation(
                    g_t[:, T * o:T * o + T], P[par][:, 0:T], AF.Gelu,
                    bias=bfull[:, o:o + 1]),
                    reads=("p%d" % par, "bfull"), writes=("gu:%d" % (o // 2),))
            for oc in range(8):
                par = oc % 4
                pr.mm_group(
                    [(lambda e, k=k, oc=oc, par=par: e.matmul(
                        P[par][:, 0:T],
                        c_sb[:, 1024 * k + 128 * oc:1024 * k + 128 * oc + 128],
                        g_t[:, T * k:T * k + T],
                        start=(k == 0), stop=(k == HCHUNK - 1)))
                     for k in range(HCHUNK)],
                    reads=ALLC, writes=("p%d" % par,),
                    reads_each=[("gu:%d" % (k // 2),) for k in range(HCHUNK)])
                vec(lambda e, oc=oc, par=par: e.scalar_tensor_tensor(
                    x_t[:, T * oc:T * oc + T], P[par][:, 0:T],
                    cch[:, oc:oc + 1], x_t[:, T * oc:T * oc + T],
                    alu.add, alu.add),
                    reads=("p%d" % par, "cch", "x:%d" % oc),
                    writes=("x:%d" % oc,))
                if post_oc is not None:
                    post_oc(oc)

        def emit_kv_group(iseq, it, g):
            if True:
                for half in range(2):
                    kp = (2 * g + half) % 2
                    pr.mm_group(
                        [(lambda e, d=d, g=g, half=half, kp=kp: e.matmul(
                            P[kp][:, 0:512],
                            s3[:, T * d + 128 * g:T * d + 128 * g + 128],
                            w_a[:, 1024 * d + 512 * half:
                                1024 * d + 512 * half + 512],
                            start=(d == 0), stop=(d == 7)))
                         for d in range(8)],
                        reads=ALLWA, writes=("p%d" % kp,),
                        reads_each=[("s3:%d" % d,) for d in range(8)])
                    sca(lambda e, half=half, kp=kp: e.activation(
                        u_t[:, 512 * half:512 * half + 512],
                        P[kp][:, 0:512], AF.Tanh, scale=0.5),
                        reads=("p%d" % kp, "gu:%d" % half),
                        writes=("gu:%d" % half,)),
                    vec(lambda e, half=half: e.tensor_scalar(
                        kv_k[:, 512 * half:512 * half + 512],
                        u_t[:, 512 * half:512 * half + 512],
                        0.5, 0.5, alu.mult, alu.add),
                        reads=("gu:%d" % half,),
                        writes=("kvk", "kvkA", "kvkB"))
                    pr.mm_group(
                        [(lambda e, d=d, g=g, half=half, kp=kp: e.matmul(
                            P[2 + kp][:, 0:512],
                            s3[:, T * d + 128 * g:T * d + 128 * g + 128],
                            w_b[:, 1024 * d + 512 * half:
                                1024 * d + 512 * half + 512],
                            start=(d == 0), stop=(d == 7)))
                         for d in range(8)],
                        reads=ALLWB, writes=("p%d" % (2 + kp),),
                        reads_each=[("s3:%d" % d,) for d in range(8)])
                    vec(lambda e, half=half, kp=kp: e.tensor_copy(
                        bass.AP(kv_v, 129 * 4 * half,
                                [[8 * 129, 128], [129, 4], [1, 128]]),
                        bass.AP(P[2 + kp], 0,
                                [[512, 128], [128, 4], [1, 128]])),
                        reads=("p%d" % (2 + kp),), writes=("kvv",))
                first = (it == 0 and g == 0)
                banks = [(0, [0, 1, 2]), (1, [3, 4, 5]), (2, [6, 7])]
                for bank, pairs in banks:
                    pr.mm_group(
                        [(lambda e, j=j, bank=bank, st0=(j == pairs[0]),
                          st1=(j == pairs[-1]): e.matmul(
                              P[5 + bank][:, 129 * (j - 3 * bank):
                                          129 * (j - 3 * bank) + 129],
                              kv_k[:, 128 * j:128 * j + 128],
                              kv_v[:, 129 * j:129 * j + 129],
                              start=st0, stop=st1))
                         for j in pairs],
                        reads=("kvk", "kvv", "kvv1"),
                        writes=("p%d" % (5 + bank),))
                    w0 = 387 * bank
                    wd = 129 * len(pairs)
                    if first:
                        vec(lambda e, bank=bank, w0=w0, wd=wd:
                            e.tensor_copy(kvacc[:, w0:w0 + wd],
                                          P[5 + bank][:, 0:wd]),
                            reads=("p%d" % (5 + bank),),
                            writes=("kvacc", "cs:0", "cs:1", "cs:2", "cs:3"))
                    else:
                        vec(lambda e, bank=bank, w0=w0, wd=wd:
                            e.tensor_tensor(kvacc[:, w0:w0 + wd],
                                            kvacc[:, w0:w0 + wd],
                                            P[5 + bank][:, 0:wd], alu.add),
                            reads=("p%d" % (5 + bank), "kvacc"),
                            writes=("kvacc",))

        def emit_kv(iseq, it):
            for g in range(NG):
                emit_kv_group(iseq, it, g)

        def emit_kv_evacs(iseq):
            for j in range(8):
                base = (iseq * 8 + j) * 128
                vec(lambda e, j=j, base=base: e.tensor_copy(
                    kvbd[0:64, base:base + 64],
                    kvacc[0:64, 129 * j:129 * j + 64]),
                    reads=("kvacc",), writes=("kvbd",))
                vec(lambda e, j=j, base=base: e.tensor_copy(
                    kvbd[64:128, base + 64:base + 128],
                    kvacc[64:128, 129 * j + 64:129 * j + 128]),
                    reads=("kvacc",), writes=("kvbd",))
                kb = (iseq * 8 + j) * 16
                vec(lambda e, kb=kb, j=j: e.tensor_copy(
                    ksc[0:64, kb + 2 * j:kb + 2 * j + 1],
                    kvacc[0:64, 129 * j + 128:129 * j + 129]),
                    reads=("kvacc",), writes=("ksc",))
                vec(lambda e, kb=kb, j=j: e.tensor_copy(
                    ksc[64:128, kb + 2 * j + 1:kb + 2 * j + 2],
                    kvacc[64:128, 129 * j + 128:129 * j + 129]),
                    reads=("kvacc",), writes=("ksc",))

        pending_kv = None
        for iseq in range(n_seq):
            for it in range(n_tiles):
                if (iseq, it) != (0, 0):
                    emit_x_load(iseq, it)
                if pending_kv is not None:
                    ps, pt = pending_kv
                    emit_kv_group(ps, pt, 0)
                    for c in range(4):
                        emit_ln_stats_chunk(x_t, "x:%d", c, "s1", "s2",
                                            cast_eng="gpsimd",
                                            sq_eng="gpsimd")
                    emit_kv_group(ps, pt, 1)
                    for c in range(4, 8):
                        emit_ln_stats_chunk(x_t, "x:%d", c, "s1", "s2",
                                            cast_eng="gpsimd",
                                            sq_eng="gpsimd")
                    emit_kv_group(ps, pt, 2)
                    for c in range(NCHUNK):
                        emit_ln_stats_mm(c, "s1", "s2")
                    emit_ln_head()
                    emit_kv_group(ps, pt, 3)
                    if pt == n_tiles - 1:
                        emit_kv_evacs(ps)
                    emit_ln_tail2(x_t, "x:%d", "s2", False, False, False)
                else:
                    emit_ln(x_t, "x:%d", "s2", False, False, False)

                def ln2_cb(oc):
                    emit_ln_stats_chunk(x_t, "x:%d", oc, "s1", "s2")
                    if oc > 0:
                        emit_ln_stats_mm(oc - 1, "s1", "s2")
                emit_ffn("s2", post_oc=ln2_cb)
                emit_ln_stats_mm(7, "s1", "s2")
                warm(8)
                dma("gpsimd",
                    bass.AP(x1sp.tensor, iseq * D * seq_len + it * T,
                            [[seq_len, 128], [128 * seq_len, 8], [1, T]]),
                    bass.AP(x_t, 0, [[NCHUNK * T, 128], [T, 8], [1, T]]),
                    reads=tuple("x:%d" % c for c in range(NCHUNK)),
                    writes=("x1sp:%d:%d" % (iseq, it),),
                    ch="spill")
                emit_ln_tail(x_t, "x:%d", "s3", True, True, False)
                warm(12)
                for c in range(NCHUNK):
                    pr.op("vector" if c % 2 == 0 else "gpsimd",
                          lambda e, c=c: e.tensor_tensor(
                              s1[:, T * c:T * c + T], u_t[:, T * c:T * c + T],
                              s3[:, T * c:T * c + T], alu.subtract),
                          reads=("gu:%d" % c, "s3:%d" % c),
                          writes=wkeys("s1", c))
                dma("sync",
                    bass.AP(xn2hi.tensor, iseq * D * seq_len + it * T,
                            [[seq_len, 128], [128 * seq_len, 8], [1, T]]),
                    bass.AP(s3, 0, [[NCHUNK * T, 128], [T, 8], [1, T]]),
                    reads=tuple("s3:%d" % c for c in range(8)),
                    writes=("xh:%d:%d" % (iseq, it),), ch="xh")
                dma("sync",
                    bass.AP(xn2lo.tensor, iseq * D * seq_len + it * T,
                            [[seq_len, 128], [128 * seq_len, 8], [1, T]]),
                    bass.AP(s1, 0, [[NCHUNK * T, 128], [T, 8], [1, T]]),
                    reads=tuple("s1x:%d" % c for c in range(8)),
                    writes=("xl:%d:%d" % (iseq, it),), ch="xl")
                pending_kv = (iseq, it)
        emit_kv(*pending_kv)
        emit_kv_evacs(n_seq - 1)

        for half in range(2):
            wa_toks = tuple("wa:%d" % c for c in range(4 * half, 4 * half + 4))
            wb_toks = tuple("wb:%d" % c for c in range(4 * half, 4 * half + 4))
            dma("sync" if half == 0 else "scalar",
                bass.AP(w_a, 4096 * half, [[8192, 128], [1, 4096]]),
                bass.AP(wqsp.tensor, 4096 * half, [[8192, 128], [1, 4096]]),
                reads=("wsp0",) + wa_toks, writes=wa_toks,
                ch="wrl0" if half == 0 else "wrl2")
            dma("gpsimd",
                bass.AP(w_b, 4096 * half, [[8192, 128], [1, 4096]]),
                bass.AP(wfsp.tensor, 4096 * half, [[8192, 128], [1, 4096]]),
                reads=("wsp1",) + wb_toks, writes=wb_toks, ch="wrl1")

        for iseq in range(n_seq):
            for it in range(n_tiles):
                dma("scalar",
                    bass.AP(x_t, 0, [[NCHUNK * T, 128], [T, 8], [1, T]]),
                    bass.AP(x1sp.tensor, iseq * D * seq_len + it * T,
                            [[seq_len, 128], [128 * seq_len, 8], [1, T]]),
                    reads=("x1sp:%d:%d" % (iseq, it),)
                    + tuple("x:%d" % c for c in range(NCHUNK)),
                    writes=tuple("x:%d" % c for c in range(NCHUNK)),
                    ch="x")
                dma("sync",
                    bass.AP(s3, 0, [[NCHUNK * T, 128], [T, 8], [1, T]]),
                    bass.AP(xn2hi.tensor, iseq * D * seq_len + it * T,
                            [[seq_len, 128], [128 * seq_len, 8], [1, T]]),
                    reads=("s3", "xh:%d:%d" % (iseq, it))
                    + tuple("s3:%d" % c for c in range(8)),
                    writes=wkeys("s3"), ch="xh")
                dma("sync",
                    bass.AP(s1, 0, [[NCHUNK * T, 128], [T, 8], [1, T]]),
                    bass.AP(xn2lo.tensor, iseq * D * seq_len + it * T,
                            [[seq_len, 128], [128 * seq_len, 8], [1, T]]),
                    reads=("s1", "xl:%d:%d" % (iseq, it))
                    + tuple("s1x:%d" % c for c in range(8)),
                    writes=wkeys("s1"), ch="xl")
                for c in range(NCHUNK):
                    pr.op("vector" if c % 2 == 0 else "gpsimd",
                          lambda e, c=c: e.tensor_tensor(
                              u_t[:, T * c:T * c + T], s3[:, T * c:T * c + T],
                              s1[:, T * c:T * c + T], alu.add),
                          reads=("s3:%d" % c, "s1x:%d" % c, "gu:%d" % c),
                          writes=("gu:%d" % c,))
                for oc in range(8):
                    par = oc % 4
                    pr.mm_group(
                        [(lambda e, d=d, oc=oc, par=par: e.matmul(
                            P[par][:, 0:T],
                            w_a[:, 1024 * d + 128 * oc:
                                1024 * d + 128 * oc + 128],
                            s3[:, T * d:T * d + T],
                            start=(d == 0), stop=(d == 7)))
                         for d in range(8)],
                        reads=ALLWA, writes=("p%d" % par,),
                        reads_each=[("s3:%d" % d,) for d in range(8)])
                    sca(lambda e, oc=oc, par=par: e.activation(
                        s2[:, T * oc:T * oc + T], P[par][:, 0:T],
                        AF.Tanh, scale=0.5),
                        reads=("p%d" % par,), writes=("s2",) + wkeys("s2", oc))
                    vec(lambda e, oc=oc: e.tensor_scalar(
                        s2[:, T * oc:T * oc + T], s2[:, T * oc:T * oc + T],
                        0.5, 0.5, alu.mult, alu.add),
                        reads=wkeys("s2", oc), writes=wkeys("s2", oc))
                kb0 = iseq * 8 * 16
                pr.mm_group(
                    [(lambda e, d=d, kb0=kb0: e.matmul(
                        P[4][0:16, 0:T],
                        ksc[:, kb0 + 16 * d:kb0 + 16 * d + 16],
                        s2[:, T * d:T * d + T],
                        start=(d == 0), stop=(d == 7)))
                     for d in range(8)],
                    reads=("ksc",), writes=("p4",),
                    reads_each=[("s2:%d" % d,) for d in range(8)])
                vec(lambda e: e.tensor_scalar(
                    den_sb[0:16, 0:T], P[4][0:16, 0:T], 1e-8, None,
                    alu.add, alu.bypass),
                    reads=("p4",), writes=("den",))
                warm(6)
                for g in range(NG):
                    tens(lambda e, g=g: e.transpose(
                        P[4][0:128, 16 * g:16 * g + 16],
                        den_sb[0:16, 128 * g:128 * g + 128],
                        ident_f[0:16, 0:16]),
                        reads=("den",), writes=("p4",))
                warm(8)
                vec(lambda e: e.tensor_copy(dsp[:, 0:16 * NG],
                                            P[4][0:128, 0:16 * NG]),
                    reads=("p4",), writes=("dsp",))
                vec(lambda e: e.reciprocal(dsp[:, 0:16 * NG],
                                           dsp[:, 0:16 * NG]),
                    reads=("dsp",), writes=("dsp",))
                for g in range(NG):
                    tens(lambda e, g=g: e.transpose(
                        P[4][0:16, 128 * g:128 * g + 128],
                        dsp[:, 16 * g:16 * g + 16], ident_f[:, :]),
                        reads=("dsp",), writes=("p4",))
                vec(lambda e: e.tensor_copy(rdh_sb[0:16, 0:T],
                                            P[4][0:16, 0:T]),
                    reads=("p4",), writes=("rdh",))
                vec(lambda e: e.tensor_tensor(
                    rdl_sb[0:16, 0:T], P[4][0:16, 0:T], rdh_sb[0:16, 0:T],
                    alu.subtract),
                    reads=("p4", "rdh"), writes=("rdl",))
                for j in range(8):
                    pn = P[j % 2]
                    pnk = "p%d" % (j % 2)
                    pe = P[5 + (j % 2)]
                    pek = "p%d" % (5 + (j % 2))
                    kvb = (iseq * 8 + j) * 128
                    tens(lambda e, pn=pn, kvb=kvb, j=j: e.matmul(
                        pn[:, 0:T], kvbd[:, kvb:kvb + 128],
                        s2[:, T * j:T * j + T], start=True, stop=True),
                        reads=("kvbd", "s2:%d" % j), writes=(pnk,))
                    pr.mm_group(
                        [lambda e, pe=pe, j=j: e.matmul(
                            pe[:, 0:T], ej_b[:, 128 * j:128 * j + 128],
                            rdh_sb[0:16, 0:T], start=True, stop=False),
                         lambda e, pe=pe, j=j: e.matmul(
                            pe[:, 0:T], ej_b[:, 128 * j:128 * j + 128],
                            rdl_sb[0:16, 0:T], start=False, stop=True)],
                        reads=("ej", "rdh", "rdl"), writes=(pek,))
                    sca(lambda e, pn=pn: e.activation(
                        rdb_sb[:, 0:T], pn[:, 0:T], AF.Identity),
                        reads=(pnk,), writes=("rdb", "wtmp0", "wtmp1"))
                    vec(lambda e, pe=pe, j=j: e.tensor_tensor(
                        s1[:, T * j:T * j + T], rdb_sb[:, 0:T],
                        pe[:, 0:T], alu.mult),
                        reads=(pek, "rdb"), writes=("s1x:%d" % j,))
                for oc in range(8):
                    par = oc % 4
                    pr.mm_group(
                        [(lambda e, d=d, oc=oc, par=par: e.matmul(
                            P[par][:, 0:T],
                            w_b[:, 1024 * d + 128 * oc:
                                1024 * d + 128 * oc + 128],
                            s1[:, T * d:T * d + T],
                            start=(d == 0), stop=(d == 7)))
                         for d in range(8)],
                        reads=ALLWB, writes=("p%d" % par,),
                        reads_each=[("s1x:%d" % d,) for d in range(8)])
                    vec(lambda e, oc=oc, par=par: e.tensor_tensor(
                        u_t[:, T * oc:T * oc + T], P[par][:, 0:T],
                        u_t[:, T * oc:T * oc + T], alu.add),
                        reads=("p%d" % par, "gu:%d" % oc),
                        writes=("gu:%d" % oc,))
                    emit_ln_stats_chunk(u_t, "gu:%d", oc, "s2", "s3")
                    if oc > 0:
                        emit_ln_stats_mm(oc - 1, "s2", "s3")
                emit_ln_stats_mm(7, "s2", "s3")
                emit_ln_tail(u_t, "gu:%d", None, False, True, True,
                             cast_n="s2", sq_n="s3")
                warm(36)
                for oc in range(8):
                    pr.op("vector" if oc % 2 == 0 else "gpsimd",
                          lambda e, oc=oc: e.tensor_tensor(
                              x_t[:, T * oc:T * oc + T],
                              x_t[:, T * oc:T * oc + T],
                              u_t[:, T * oc:T * oc + T], alu.add),
                          reads=("x:%d" % oc, "gu:%d" % oc),
                          writes=("x:%d" % oc,))
                emit_ln(x_t, "x:%d", "s2", False, False, False,
                        cast_n="s1", sq_n="s3")
                warm(20)
                emit_ffn("s2")
                dma("gpsimd",
                    bass.AP(dout.tensor, iseq * D * seq_len + it * T,
                            [[seq_len, 128], [128 * seq_len, 8], [1, T]]),
                    bass.AP(x_t, 0, [[NCHUNK * T, 128], [T, 8], [1, T]]),
                    reads=tuple("x:%d" % c for c in range(NCHUNK)),
                    writes=("outd",), ch="out")

        pr.raw_wait("gpsimd", "dch_out", pr.cnt["dch_out"])
        pr.realize(blk)
    return nc



def kernel(**inputs):
    x = np.asarray(inputs["L0_attn_inputs"], dtype=np.float32)
    consts = host_constants(inputs)
    nc = build_program(n_seq=2, seq_len=4096, T=512)
    in_maps = []
    for core in range(8):
        xs = x[2 * core:2 * core + 2]
        xfm = np.ascontiguousarray(xs.transpose(0, 2, 1)).reshape(2 * D, 4096)
        m = {"x": xfm}
        m.update(consts)
        in_maps.append(m)
    res = run_bass_kernel_spmd(nc, in_maps, core_ids=list(range(8)))
    outs = []
    for core in range(8):
        o = np.asarray(res.results[core]["out"]).reshape(2, D, 4096)
        outs.append(o.transpose(0, 2, 1))
    return np.ascontiguousarray(np.concatenate(outs, axis=0), dtype=np.float32)


# version 26 (final)
# speedup vs baseline: 1.2871x; 1.0338x over previous
import numpy as np
from contextlib import ExitStack

import concourse.bass as bass
import concourse.mybir as mybir
from concourse.bass_utils import run_bass_kernel_spmd
from concourse.alu_op_type import AluOpType as alu

F32 = mybir.dt.float32
BF16 = mybir.dt.bfloat16
AF = mybir.ActivationFunctionType

D = 1024
HID = 4096
NCHUNK = 8
HCHUNK = 32
LN_EPS = 1e-5
NEWTON_ITERS = 2
DEBUG_TAPS = False


def _stage_chunk(w_s, s, c, transpose=False):
    S = 2 ** s
    M = np.zeros((128, 128), dtype=np.float32)
    w = np.asarray(w_s, dtype=np.float32).reshape(HID // (2 * S), S, 2, 2)
    for p in range(128):
        i = 128 * c + p
        if i & S:
            continue
        n1 = i // (2 * S)
        j = i % S
        M[p, p] = w[n1, j, 0, 0]
        M[p, p + S] = w[n1, j, 0, 1] if not transpose else w[n1, j, 1, 0]
        M[p + S, p] = w[n1, j, 1, 0] if not transpose else w[n1, j, 0, 1]
        M[p + S, p + S] = w[n1, j, 1, 1]
    return M


def _diag_vecs(w_s, s, c, transpose=False):
    S = 2 ** s
    w = np.asarray(w_s, dtype=np.float32).reshape(HID // (2 * S), S, 2, 2)
    vs = np.zeros(128, dtype=np.float32)
    vp = np.zeros(128, dtype=np.float32)
    for p in range(128):
        i = 128 * c + p
        if not (i & S):
            n1 = i // (2 * S)
            j = i % S
            vs[p] = w[n1, j, 0, 0]
            vp[p] = w[n1, j, 0, 1] if not transpose else w[n1, j, 1, 0]
        else:
            a = i - S
            n1 = a // (2 * S)
            j = a % S
            vs[p] = w[n1, j, 1, 1]
            vp[p] = w[n1, j, 1, 0] if not transpose else w[n1, j, 0, 1]
    return vs, vp


def host_constants(inp):
    c = {}
    c["wq"] = np.ascontiguousarray(inp["L0_attn_W_Q"], dtype=np.float32)
    c["wk"] = np.ascontiguousarray(inp["L0_attn_W_K"], dtype=np.float32)
    c["wv"] = np.ascontiguousarray(inp["L0_attn_W_V"], dtype=np.float32)
    c["wfc"] = np.ascontiguousarray(inp["L0_attn_W_fc"], dtype=np.float32)
    c["gamma"] = np.ascontiguousarray(
        np.asarray(inp["L0_attn_gamma"], np.float32).reshape(8, 128).T)
    c["beta"] = np.ascontiguousarray(
        np.asarray(inp["L0_attn_beta"], np.float32).reshape(8, 128).T)
    c["wexpT"] = np.ascontiguousarray(
        np.asarray(inp["L0_ffn_W_exp"], np.float32).T)
    c["wcon"] = np.ascontiguousarray(inp["L0_ffn_W_con"], dtype=np.float32)
    c["bexp"] = np.ascontiguousarray(
        np.asarray(inp["L0_ffn_b_exp"], np.float32).reshape(32, 128).T)
    c["upbias"] = np.ascontiguousarray(
        np.asarray(inp["L0_ffn_up_bias"], np.float32).reshape(32, 128).T)
    c["bcon"] = np.ascontiguousarray(
        np.asarray(inp["L0_ffn_b_con"], np.float32).reshape(8, 128).T)
    c["dbias"] = np.ascontiguousarray(
        np.asarray(inp["L0_ffn_down_bias"], np.float32).reshape(8, 128).T)

    upw = np.asarray(inp["L0_ffn_up_weights"], np.float32)
    dnw = np.asarray(inp["L0_ffn_down_weights"], np.float32)
    slo_u = np.zeros((7, 32, 128, 128), dtype=np.float32)
    slo_dT = np.zeros((7, 32, 128, 128), dtype=np.float32)
    for s in range(7):
        for ch in range(32):
            slo_u[s, ch] = _stage_chunk(upw[s], s, ch)
            slo_dT[s, ch] = _stage_chunk(dnw[s], s, ch, transpose=True)
    c["slo_u"] = np.ascontiguousarray(
        slo_u.transpose(1, 2, 0, 3).reshape(32 * 128, 7 * 128))
    c["slo_dT"] = np.ascontiguousarray(
        slo_dT.transpose(1, 2, 0, 3).reshape(32 * 128, 7 * 128))
    dv_u = np.zeros((128, 5 * 32 * 2), dtype=np.float32)
    dv_d = np.zeros((128, 5 * 32 * 2), dtype=np.float32)
    for si in range(5):
        s = si + 7
        for ch in range(32):
            vs_u, vp_u = _diag_vecs(upw[s], s, ch)
            dv_u[:, (si * 32 + ch) * 2 + 0] = vs_u
            dv_u[:, (si * 32 + ch) * 2 + 1] = vp_u
            vs_d, vp_d = _diag_vecs(dnw[s], s, ch, transpose=True)
            dv_d[:, (si * 32 + ch) * 2 + 0] = vs_d
            dv_d[:, (si * 32 + ch) * 2 + 1] = vp_d
    c["dv_u"] = dv_u
    c["dv_d"] = dv_d
    c["ident"] = np.eye(128, dtype=np.float32)
    ej = np.zeros((16, 8 * 128), dtype=np.float32)
    for j in range(8):
        ej[2 * j, 128 * j:128 * j + 64] = 1.0
        ej[2 * j + 1, 128 * j + 64:128 * j + 128] = 1.0
    c["ej"] = ej
    return c



ENGINES = ("tensor", "vector", "scalar", "sync", "gpsimd")


class Prog:
    def __init__(self, nc, ctx):
        self.nc = nc
        self.stream = {e: [] for e in ENGINES}
        self.sem = {}
        self.cnt = {}
        for e in ENGINES:
            self.sem[e] = ctx.enter_context(nc.semaphore("sem_" + e))
            self.cnt[e] = 0
        self.ctx = ctx
        self.wr = {}
        self.rd = {}
        self.floor = {e: {} for e in ENGINES}

    def _waits_for(self, eng, reads, writes):
        toks = []
        for r in reads:
            t = self.wr.get(r)
            if t:
                toks.append(t)
        for r in writes:
            t = self.wr.get(r)
            if t:
                toks.append(t)
            toks.extend(self.rd.get(r, ()))
        need = {}
        for s, v in toks:
            if v > need.get(s, -1):
                need[s] = v
        fl = self.floor[eng]
        out = []
        for s, v in sorted(need.items()):
            if fl.get(s, 0) >= v:
                continue
            assert v <= self.cnt[s], (eng, s, v, self.cnt[s])
            out.append((s, v))
            fl[s] = v
        return out

    def dma_ch(self, ch):
        name = "dch_" + ch
        if name not in self.sem:
            self.sem[name] = self.ctx.enter_context(nc_sem(self.nc, name))
            self.cnt[name] = 0
        return name

    def op(self, eng, build, reads=(), writes=(), dma=False, inc=True,
           ch=None):
        semname = self.dma_ch(ch) if dma else eng
        waits = self._waits_for(eng, reads, writes)
        tok = None
        if inc:
            self.cnt[semname] += 16 if dma else 1
            tok = (semname, self.cnt[semname])
        semh = self.sem[semname]
        amount = 16 if dma else 1
        sems = self.sem

        def closure(e, waits=waits, build=build, tok=tok):
            for s, v in waits:
                e.wait_ge(sems[s], v)
            instr = build(e)
            if tok is not None:
                instr.then_inc(semh, amount)
        self.stream[eng].append(closure)
        if tok is not None:
            for r in reads:
                self.rd.setdefault(r, []).append(tok)
            for r in writes:
                self.wr[r] = tok
                self.rd[r] = []
        return tok

    def mm_group(self, builders, reads, writes, reads_each=None):
        n = len(builders)
        for i, b in enumerate(builders[:-1]):
            r = reads + (tuple(reads_each[i]) if reads_each else ())
            self.op("tensor", b, reads=r, writes=writes, inc=False)
        r = reads + (tuple(reads_each[n - 1]) if reads_each else ())
        return self.op("tensor", builders[-1], reads=r, writes=writes)

    def raw_wait(self, eng, semname, value):
        if value <= 0:
            return
        fl = self.floor[eng]
        if fl.get(semname, 0) >= value:
            return
        fl[semname] = value
        sems = self.sem

        def closure(e):
            e.wait_ge(sems[semname], value)
        self.stream[eng].append(closure)

    def realize(self, blk):
        for e in ENGINES:
            ops = self.stream[e]

            def runner(eng, ops=ops):
                for f in ops:
                    f(eng)
            getattr(blk, e)(runner)



def nc_sem(nc, name):
    return nc.semaphore(name)


def build_program(n_seq=2, seq_len=4096, T=512):
    n_tiles = seq_len // T
    NG = T // 128
    nc = bass.Bass("TRN2", target_bir_lowering=False)

    dx = nc.dram_tensor("x", [n_seq * D, seq_len], F32,
                        kind="ExternalInput").ap()
    dout = nc.dram_tensor("out", [n_seq * D, seq_len], F32,
                          kind="ExternalOutput").ap()
    dins = {}
    for name, shape in [
        ("wq", [D, D]), ("wk", [D, D]), ("wv", [D, D]), ("wfc", [D, D]),
        ("gamma", [128, 8]), ("beta", [128, 8]),
        ("wexpT", [HID, D]), ("wcon", [HID, D]),
        ("bexp", [128, 32]), ("upbias", [128, 32]),
        ("bcon", [128, 8]), ("dbias", [128, 8]),
        ("slo_u", [32 * 128, 7 * 128]), ("slo_dT", [32 * 128, 7 * 128]),
        ("dv_u", [128, 5 * 32 * 2]), ("dv_d", [128, 5 * 32 * 2]),
        ("ident", [128, 128]), ("ej", [16, 8 * 128]),
    ]:
        dins[name] = nc.dram_tensor(name, shape, F32, kind="ExternalInput").ap()
    asp = nc.dram_tensor("asp", [128, 8 * HID], BF16).ap()
    wqsp = nc.dram_tensor("wqsp", [128, 8 * 1024], BF16).ap()
    wfsp = nc.dram_tensor("wfsp", [128, 8 * 1024], BF16).ap()
    x1sp = nc.dram_tensor("x1sp", [n_seq * D, seq_len], F32).ap()
    xn2hi = nc.dram_tensor("xn2hi", [n_seq * D, seq_len], BF16).ap()
    xn2lo = nc.dram_tensor("xn2lo", [n_seq * D, seq_len], BF16).ap()
    if DEBUG_TAPS:
        dbg1 = nc.dram_tensor("dbg1", [n_seq * D, seq_len], F32).ap()
        dbg2 = nc.dram_tensor("dbg2", [n_seq * D, seq_len], F32).ap()
        dbg3 = nc.dram_tensor("dbg3", [n_seq * D, seq_len], F32).ap()
        dbg4 = nc.dram_tensor("dbg4", [n_seq * D, seq_len], F32).ap()

    ctx = ExitStack()
    with ctx:
        S = lambda name, shape, dt: ctx.enter_context(
            nc.sbuf_tensor(name, shape, dt))
        c_sb = S("c_sb", [128, HCHUNK * 1024], BF16)
        w_a = S("w_a", [128, NCHUNK * 1024], BF16)
        w_b = S("w_b", [128, NCHUNK * 1024], BF16)
        ident_f = S("ident_f", [128, 128], F32)
        ident_b = S("ident_b", [128, 128], BF16)
        onesc = S("onesc", [128, 1], BF16)
        onesr = S("onesr", [1, 512], BF16)
        onesr_f = S("onesr_f", [1, 128], F32)
        epsr = S("epsr", [1, 1], BF16)
        ej_b = S("ej_b", [16, 8 * 128], BF16)
        bfull = S("bfull", [128, 32], F32)
        bvec = S("bvec", [128, 32], BF16)
        upb_f = S("upb_f", [128, 32], F32)
        cch = S("cch", [128, 8], F32)
        gam = S("gam", [128, 8], F32)
        bet = S("bet", [128, 8], F32)
        kvbd = S("kvbd", [128, n_seq * 8 * 128], BF16)
        ksc = S("ksc", [128, n_seq * 8 * 16], BF16)
        x_t = S("x_t", [128, NCHUNK * T], F32)
        s1 = S("s1", [128, NCHUNK * T], BF16)
        s2 = S("s2", [128, NCHUNK * T], BF16)
        s3 = S("s3", [128, NCHUNK * T], BF16)
        g_t = S("g_t", [128, HCHUNK * T], BF16)
        u_t = g_t.bitcast(F32)
        kv_k = S("kv_k", [128, 1024], BF16)
        kv_v = S("kv_v", [128, 8 * 129], BF16)
        a_st = [S("a_st0", [128, 8 * 128], BF16),
                S("a_st1", [128, 8 * 128], BF16),
                S("a_st2", [128, 8 * 128], BF16)]
        stats_sb = S("stats_sb", [33, T], F32)
        small = S("small", [128, 48], F32)
        small_i = small.bitcast(mybir.dt.int32)
        den_sb = S("den_sb", [16, T], F32)
        dsp = S("dsp", [128, 64], F32)
        rdl_sp = S("rdl_sp", [128, 64], F32)
        rdh_sb = S("rdh_sb", [16, T], BF16)
        rdl_sb = S("rdl_sb", [16, T], BF16)
        
        wtmp = S("wtmp", [128, 1024], F32)
        rdb_sb = wtmp
        stg_all = S("stg_all", [128, 2688], BF16)
        stg_fv = stg_all.bitcast(F32)
        kvacc = stg_fv
        stg_f = stg_fv
        STGB0 = 1792
        dvv = S("dvv", [128, 320], F32)
        dgb = S("dgb", [128, 8 * 128], BF16)
        ypp = S("ypp", [128, 2 * 128], BF16)
        trn = S("trn", [128, 2 * 128], BF16)

        P = [ctx.enter_context(nc.psum_tensor("pb%d" % i, [128, 512], F32))
             for i in range(8)]

        blk = ctx.enter_context(nc.Block())
        pr = Prog(nc, ctx)

        tens = lambda b, **kw: pr.op("tensor", b, **kw)
        vec = lambda b, **kw: pr.op("vector", b, **kw)
        sca = lambda b, **kw: pr.op("scalar", b, **kw)

        SERIAL_CH = {"misc"}

        def dma(queue, dst, src, reads=(), writes=(), ch="misc"):
            semname = pr.dma_ch(ch)
            if ch in SERIAL_CH:
                pr.raw_wait(queue, semname, pr.cnt[semname])
            return pr.op(queue, lambda e: e.dma_start(dst, src),
                         reads=reads, writes=writes, dma=True, ch=ch)

        def warm(n):
            for _ in range(n):
                pr.op("tensor", lambda e: e.matmul(
                    P[7][:, 0:512], c_sb[:, 0:128], c_sb[:, 0:512],
                    start=True, stop=True, skip_group_check=True), inc=False)

        ALLC = tuple("C:%d" % k for k in range(HCHUNK))
        ALLWA = tuple("wa:%d" % d for d in range(8))
        ALLWB = tuple("wb:%d" % d for d in range(8))

        dma("sync", ident_f[:, :], dins["ident"][:, :], writes=("idf",), ch="misc")
        vec(lambda e: e.tensor_copy(ident_b[:, :], ident_f[:, :]),
            reads=("idf",), writes=("idb",))
        vec(lambda e: e.memset(onesc[:, :], 1.0), writes=("ones",))
        vec(lambda e: e.memset(onesr[:, :], 1.0), writes=("onesr",))
        vec(lambda e: e.memset(onesr_f[:, :], 1.0), writes=("onesrf",))
        vec(lambda e: e.memset(epsr[:, :], float(D) * LN_EPS), writes=("epsr",))
        for h in range(2):
            dma("sync", wtmp[0:16, 0:512],
                dins["ej"][:, 512 * h:512 * h + 512], writes=("wtmp0",),
                ch="wload0")
            vec(lambda e, h=h: e.tensor_copy(
                ej_b[:, 512 * h:512 * h + 512], wtmp[0:16, 0:512]),
                reads=("wtmp0",), writes=("ej",))
        dma("sync", gam[:, :], dins["gamma"][:, :], writes=("gam",), ch="misc")
        dma("sync", bet[:, :], dins["beta"][:, :], writes=("bet",), ch="misc")
        dma("sync", upb_f[:, :], dins["upbias"][:, :], writes=("upb",), ch="misc")
        dma("sync", cch[:, :], dins["bcon"][:, :], writes=("cch",), ch="misc")
        dma("sync", wtmp[:, 0:8], dins["dbias"][:, :], writes=("wtmp0",), ch="wload0")
        vec(lambda e: e.scalar_tensor_tensor(cch[:, :], wtmp[:, 0:8], 1.0,
                                             cch[:, :], alu.mult, alu.add),
            reads=("wtmp0", "cch"), writes=("cch",))
        vec(lambda e: e.tensor_scalar(cch[:, :], cch[:, :], 0.5, None,
                                      alu.mult, alu.bypass),
            reads=("cch",), writes=("cch",))
        dma("sync", wtmp[:, 0:32], dins["bexp"][:, :], writes=("wtmp0",), ch="wload0")
        vec(lambda e: e.tensor_copy(bvec[:, :], wtmp[:, 0:32]),
            reads=("wtmp0",), writes=tuple("bv:%d" % ch for ch in range(32)))
        vec(lambda e: e.memset(kvbd[:, :], 0.0), writes=("kvbd",))
        vec(lambda e: e.memset(ksc[:, :], 0.0), writes=("ksc",))
        vec(lambda e: e.memset(
            bass.AP(kv_v, 128, [[8 * 129, 128], [129, 8]]), 1.0),
            writes=("kvv1",))

        def emit_x_load(iseq, it):
            for h in range(2):
                xk = tuple("x:%d" % c for c in range(4 * h, 4 * h + 4))
                dma("scalar",
                    bass.AP(x_t, 4 * h * T,
                            [[NCHUNK * T, 128], [T, 4], [1, T]]),
                    bass.AP(dx.tensor,
                            iseq * D * seq_len + 4 * h * 128 * seq_len + it * T,
                            [[seq_len, 128], [128 * seq_len, 4], [1, T]]),
                    reads=xk, writes=xk, ch="x%d" % h)


        def load_w(dsrc, dst_sb, tag):
            for i in range(16):
                ch, h = divmod(i, 2)
                p = i % 2
                dma("sync" if p == 0 else "gpsimd",
                    wtmp[:, 512 * p:512 * p + 512],
                    dsrc[128 * ch:128 * ch + 128, 512 * h:512 * h + 512],
                    writes=("wtmp%d" % p,),
                    ch="wload0" if p == 0 else "wload1g")
                vec(lambda e, ch=ch, h=h, p=p, dst_sb=dst_sb: e.tensor_copy(
                    dst_sb[:, 1024 * ch + 512 * h:
                           1024 * ch + 512 * h + 512],
                    wtmp[:, 512 * p:512 * p + 512]),
                    reads=("wtmp%d" % p,), writes=("%s:%d" % (tag, ch),))

        def fill_R_ops(src):
            ops = []
            for i in range(64):
                ch, h = divmod(i, 2)
                p = i % 2

                def one(ch=ch, h=h, p=p):
                    dma("sync" if p == 0 else "gpsimd",
                        wtmp[:, 512 * p:512 * p + 512],
                        src[128 * ch:128 * ch + 128,
                            512 * h:512 * h + 512],
                        writes=("wtmp%d" % p,),
                        ch="wload0" if p == 0 else "wload1g")
                    pr.op("gpsimd" if p == 0 else "vector",
                          lambda e, ch=ch, h=h, p=p: e.tensor_copy(
                              c_sb[:, 1024 * ch + 512 * h:
                                   1024 * ch + 512 * h + 512],
                              wtmp[:, 512 * p:512 * p + 512]),
                          reads=("wtmp%d" % p,), writes=("C:%d" % ch,))
                ops.append(one)
            return ops

        def fill_R_gp(src):
            for i in range(64):
                ch, h = divmod(i, 2)
                p = i % 2
                dma("sync" if p == 0 else "gpsimd",
                    wtmp[:, 512 * p:512 * p + 512],
                    src[128 * ch:128 * ch + 128, 512 * h:512 * h + 512],
                    writes=("wtmp%d" % p,),
                    ch="wload0" if p == 0 else "wload1g")
                vec(lambda e, ch=ch, h=h, p=p: e.tensor_copy(
                    c_sb[:, 1024 * ch + 512 * h:
                         1024 * ch + 512 * h + 512],
                    wtmp[:, 512 * p:512 * p + 512]),
                    reads=("wtmp%d" % p,), writes=("C:%d" % ch,))

        def fill_R(src):
            for op in fill_R_ops(src):
                op()

        def load_stage_chunk(src_ap, ch):
            p = ch % 2
            base = p * 1024 + ((ch // 2) % 2) * 2048
            toks = ("gu:%d" % (base // 512), "gu:%d" % (base // 512 + 1))
            dma("scalar" if p == 0 else "sync",
                bass.AP(u_t, base, [[8192, 128], [1, 896]]),
                bass.AP(src_ap.tensor, ch * 128 * 896,
                        [[896, 128], [1, 896]]),
                writes=toks, ch="stg" if p == 0 else "stg1")
            return base, toks

        def lo_chain_pair(landA, landB, down, fill):
            holds = {}
            st = {0: 0, 1: 0}
            order = range(0, 7) if down else range(6, -1, -1)
            for k, s in enumerate(order):
                for p, (base, ltoks) in ((0, landA), (1, landB)):
                    lc = base + s * 128
                    lhs = u_t[:, lc:lc + 128]
                    pb = 4 + p
                    if k == 0:
                        rhs, rkeyn = ident_f[:, :], ("idf",)
                    else:
                        cur = 2 * p + st[p]
                        rhs = stg_fv[:, cur * 128:cur * 128 + 128]
                        rkeyn = ("cs:%d" % cur,)
                    tens(lambda e, lhs=lhs, rhs=rhs, pb=pb: e.matmul(
                        P[pb][0:128, 0:128], lhs, rhs, start=True, stop=True),
                        reads=ltoks + rkeyn, writes=("p%d" % pb,))
                    if k == 6:
                        sca(lambda e, p=p, pb=pb: e.activation(
                            kv_k[:, 128 * p:128 * p + 128],
                            P[pb][0:128, 0:128], AF.Identity),
                            reads=("p%d" % pb,), writes=("kvh:%d" % p,))
                        holds[p] = (kv_k[:, 128 * p:128 * p + 128],
                                    "kvh:%d" % p)
                    else:
                        nxt = 1 - st[p]
                        cn = 2 * p + nxt
                        sca(lambda e, cn=cn, pb=pb:
                            e.activation(stg_fv[:, cn * 128:cn * 128 + 128],
                                         P[pb][0:128, 0:128], AF.Identity),
                            reads=("p%d" % pb,),
                            writes=("cs:%d" % cn,))
                        st[p] = nxt
                    if fill:
                        fill.pop(0)()
            return holds

        def cupd_ops(ch, hold, down):
            pc, pkey = hold
            ops = []
            pbs = (0, 1) if ch % 2 == 0 else (2, 3)
            for h in range(2):
                pb = pbs[h]

                def mm(e, pc=pc, ch=ch, h=h, pb=pb):
                    return e.matmul(
                        P[pb][:, :], pc,
                        c_sb[:, 1024 * ch + 512 * h:1024 * ch + 512 * h + 512],
                        start=True, stop=True)
                ops.append((lambda mm=mm, pkey=pkey, ch=ch, pb=pb:
                            tens(mm, reads=(pkey, "C:%d" % ch),
                                 writes=("p%d" % pb,))))
                if down:
                    ops.append((lambda ch=ch, h=h, pb=pb:
                                vec(lambda e: e.tensor_scalar(
                                    c_sb[:, 1024 * ch + 512 * h:
                                         1024 * ch + 512 * h + 512],
                                    P[pb][:, :], 0.5, None,
                                    alu.mult, alu.bypass),
                                    reads=("p%d" % pb,),
                                    writes=("C:%d" % ch,))))
                else:
                    ops.append((lambda ch=ch, h=h, pb=pb:
                                vec(lambda e: e.tensor_copy(
                                    c_sb[:, 1024 * ch + 512 * h:
                                         1024 * ch + 512 * h + 512],
                                    P[pb][:, :]),
                                    reads=("p%d" % pb,),
                                    writes=("C:%d" % ch,))))
            if not down:
                pbb = 6 if ch % 2 == 0 else 7
                ops.append((lambda pc=pc, pkey=pkey, ch=ch, pbb=pbb:
                            tens(lambda e: e.matmul(
                                P[pbb][0:128, 0:1], pc, bvec[:, ch:ch + 1],
                                start=True, stop=True),
                                reads=(pkey, "bv:%d" % ch),
                                writes=("p%d" % pbb,))))
                ops.append((lambda ch=ch, pbb=pbb:
                            vec(lambda e: e.tensor_copy(
                                bvec[:, ch:ch + 1], P[pbb][0:128, 0:1]),
                                reads=("p%d" % pbb,),
                                writes=("bv:%d" % ch,))))
            return ops

        def lo_stages(src_key, down):
            lands = {0: load_stage_chunk(dins[src_key], 0),
                     1: load_stage_chunk(dins[src_key], 1)}
            fill = []
            for base in range(0, 32, 2):
                holds = lo_chain_pair(lands.pop(base), lands.pop(base + 1),
                                      down, fill)
                if base + 2 < 32:
                    lands[base + 2] = load_stage_chunk(dins[src_key],
                                                       base + 2)
                    lands[base + 3] = load_stage_chunk(dins[src_key],
                                                       base + 3)
                while fill:
                    fill.pop(0)()
                fill = (cupd_ops(base, holds[0], down)
                        + cupd_ops(base + 1, holds[1], down))
            while fill:
                fill.pop(0)()

        def hi_stages(dv_key, bias_too):
            npair = 0
            for si in range(5):
                k = 1 << si
                for cbase in range(32):
                    if cbase & k:
                        continue
                    cp = cbase + k
                    db = 512 * (npair % 2)
                    dk = 4 * (npair % 2)
                    npair += 1
                    cols = [(si * 32 + cbase) * 2, (si * 32 + cbase) * 2 + 1,
                            (si * 32 + cp) * 2, (si * 32 + cp) * 2 + 1]
                    for q, col in enumerate(cols):
                        vec(lambda e, q=q, col=col, db=db: e.tensor_scalar(
                            dgb[:, db + 128 * q:db + 128 * q + 128],
                            ident_b[:, :],
                            dvv[:, col:col + 1], None, alu.mult, alu.bypass),
                            reads=("idb", "dvv"), writes=("dgb:%d" % (dk + q),))
                    for h in range(2):
                        a0 = 1024 * cbase + 512 * h
                        a1 = 1024 * cp + 512 * h
                        pr.mm_group(
                            [lambda e, a0=a0, h=h, db=db: e.matmul(
                                P[h][:, :], dgb[:, db:db + 128],
                                c_sb[:, a0:a0 + 512], start=True, stop=False),
                             lambda e, a1=a1, h=h, db=db: e.matmul(
                                P[h][:, :], dgb[:, db + 128:db + 256],
                                c_sb[:, a1:a1 + 512], start=False, stop=True)],
                            reads=("dgb:%d" % dk, "dgb:%d" % (dk + 1),
                                   "C:%d" % cbase, "C:%d" % cp),
                            writes=("p%d" % h,))
                        pr.mm_group(
                            [lambda e, a1=a1, h=h, db=db: e.matmul(
                                P[2 + h][:, :], dgb[:, db + 256:db + 384],
                                c_sb[:, a1:a1 + 512], start=True, stop=False),
                             lambda e, a0=a0, h=h, db=db: e.matmul(
                                P[2 + h][:, :], dgb[:, db + 384:db + 512],
                                c_sb[:, a0:a0 + 512], start=False, stop=True)],
                            reads=("dgb:%d" % (dk + 2), "dgb:%d" % (dk + 3),
                                   "C:%d" % cbase, "C:%d" % cp),
                            writes=("p%d" % (2 + h),))
                    if bias_too:
                        pr.mm_group(
                            [lambda e, db=db: e.matmul(
                                P[4][0:128, 0:1], dgb[:, db:db + 128],
                                bvec[:, cbase:cbase + 1],
                                start=True, stop=False),
                             lambda e, db=db: e.matmul(
                                P[4][0:128, 0:1], dgb[:, db + 128:db + 256],
                                bvec[:, cp:cp + 1], start=False, stop=True)],
                            reads=("dgb:%d" % dk, "dgb:%d" % (dk + 1),
                                   "bv:%d" % cbase, "bv:%d" % cp),
                            writes=("p4",))
                        pr.mm_group(
                            [lambda e, db=db: e.matmul(
                                P[4][0:128, 1:2], dgb[:, db + 256:db + 384],
                                bvec[:, cp:cp + 1], start=True, stop=False),
                             lambda e, db=db: e.matmul(
                                P[4][0:128, 1:2], dgb[:, db + 384:db + 512],
                                bvec[:, cbase:cbase + 1],
                                start=False, stop=True)],
                            reads=("dgb:%d" % (dk + 2), "dgb:%d" % (dk + 3),
                                   "bv:%d" % cbase, "bv:%d" % cp),
                            writes=("p4",))
                    for h in range(2):
                        sca(lambda e, cbase=cbase, h=h: e.activation(
                            c_sb[:, 1024 * cbase + 512 * h:
                                 1024 * cbase + 512 * h + 512], P[h][:, :],
                            AF.Identity),
                            reads=("p%d" % h,), writes=("C:%d" % cbase,))
                        sca(lambda e, cp=cp, h=h: e.activation(
                            c_sb[:, 1024 * cp + 512 * h:
                                 1024 * cp + 512 * h + 512], P[2 + h][:, :],
                            AF.Identity),
                            reads=("p%d" % (2 + h),), writes=("C:%d" % cp,))
                    if bias_too:
                        vec(lambda e, cbase=cbase: e.tensor_copy(
                            bvec[:, cbase:cbase + 1], P[4][0:128, 0:1]),
                            reads=("p4",), writes=("bv:%d" % cbase,))
                        vec(lambda e, cp=cp: e.tensor_copy(
                            bvec[:, cp:cp + 1], P[4][0:128, 1:2]),
                            reads=("p4",), writes=("bv:%d" % cp,))

        dma("scalar", dvv[:, :], dins["dv_u"][:, :], writes=("dvv",), ch="dvv")
        emit_x_load(0, 0)
        fill_R_gp(dins["wexpT"])
        lo_stages("slo_u", down=False)
        hi_stages("dv_u", bias_too=True)
        vec(lambda e: e.tensor_copy(bfull[:, :], bvec[:, :]),
            reads=tuple("bv:%d" % ch for ch in range(32)), writes=("bfull",))
        vec(lambda e: e.scalar_tensor_tensor(bfull[:, :], upb_f[:, :], 1.0,
                                             bfull[:, :], alu.mult, alu.add),
            reads=("upb", "bfull"), writes=("bfull",))

        for ch in range(32):
            for dj in range(4):
                par = (ch * 4 + dj) % 4
                pb = 4 + par
                for t_ in range(2):
                    d = 2 * dj + t_
                    tens(lambda e, ch=ch, d=d, pb=pb, t_=t_: e.matmul(
                        P[pb][0:128, 128 * t_:128 * t_ + 128],
                        c_sb[:, 1024 * ch + 128 * d:1024 * ch + 128 * d + 128],
                        ident_b[:, :], start=(t_ == 0), stop=(t_ == 1)),
                        reads=("C:%d" % ch, "idb"), writes=("p%d" % pb,))
                slot = dgb[:, 256 * par:256 * par + 256]
                skeys = ("dgb:%d" % (2 * par), "dgb:%d" % (2 * par + 1))
                if par % 2 == 0:
                    sca(lambda e, slot=slot, pb=pb: e.activation(
                        slot, P[pb][0:128, 0:256], AF.Identity),
                        reads=("p%d" % pb,), writes=skeys)
                else:
                    vec(lambda e, slot=slot, pb=pb: e.tensor_copy(
                        slot, P[pb][0:128, 0:256]),
                        reads=("p%d" % pb,), writes=skeys)
                dma("gpsimd" if par < 2 else "sync",
                    bass.AP(asp.tensor, 1024 * ch + 256 * dj,
                            [[8 * HID, 128], [1, 256]]),
                    slot,
                    reads=skeys, writes=("asp:%d:%d" % (ch, dj),),
                    ch="asp%d" % par)

        fill_R_gp(dins["wcon"])
        dma("scalar", dvv[:, :], dins["dv_d"][:, :], writes=("dvv",), ch="dvv")
        load_w(dins["wk"], w_a, "wa")
        load_w(dins["wv"], w_b, "wb")
        hi_stages("dv_d", bias_too=False)
        lo_stages("slo_dT", down=True)

        for i in range(16):
            for p, (dsrc, dspill) in enumerate(((dins["wq"], wqsp),
                                                (dins["wfc"], wfsp))):
                ch, h = divmod(i, 2)
                ceng = "vector" if p == 0 else "gpsimd"
                ktok = "kvkA" if p == 0 else "kvkB"
                extra = ("kvh:0", "kvh:1") if p == 0 else ()
                dma("sync" if p == 0 else "gpsimd",
                    wtmp[:, 512 * p:512 * p + 512],
                    dsrc[128 * ch:128 * ch + 128, 512 * h:512 * h + 512],
                    writes=("wtmp%d" % p,),
                    ch="wload0" if p == 0 else "wload1g")
                pr.op(ceng, lambda e, p=p: e.tensor_copy(
                    kv_k[:, 512 * p:512 * p + 512],
                    wtmp[:, 512 * p:512 * p + 512]),
                    reads=("wtmp%d" % p,), writes=(ktok,) + extra)
                dma("sync" if p == 0 else "gpsimd",
                    bass.AP(dspill.tensor, 1024 * ch + 512 * h,
                            [[8192, 128], [1, 512]]),
                    kv_k[:, 512 * p:512 * p + 512],
                    reads=(ktok,), writes=("wsp%d" % p,),
                    ch="wspo%d" % p)

        SB = {"s1": s1, "s2": s2, "s3": s3}
        def skey(src_key, c):
            return src_key % c if "%d" in src_key else src_key
        GUALL = tuple("gu:%d" % _c for _c in range(8))
        def wkeys(name, c=None):
            if c is None:
                base = "s1x" if name == "s1" else name
                return (name,) + tuple("%s:%d" % (base, _c) for _c in range(8))
            base = "s1x" if name == "s1" else name
            return ("%s:%d" % (base, _c if False else c),)
        def rkey(name, c):
            base = "s1x" if name == "s1" else name
            return "%s:%d" % (base, c)
        def emit_rows_and_bcast_r(sp_sb, rcol, rkeyn, hilo, ng):
            for g in range(ng):
                tens(lambda e, g=g: e.transpose(
                    P[4][0:1, 128 * g:128 * g + 128],
                    sp_sb[:, rcol + g:rcol + g + 1], ident_f[:, :]),
                    reads=(rkeyn,), writes=("p4",))
            vec(lambda e, ng=ng: e.tensor_copy(stats_sb[0:1, 0:128 * ng],
                                               P[4][0:1, 0:128 * ng]),
                reads=("p4",), writes=("stats",))
            for g in range(ng):
                tens(lambda e, g=g: e.matmul(
                    P[4][0:128, 128 * g:128 * g + 128], onesr_f[0:1, 0:128],
                    stats_sb[0:1, 128 * g:128 * g + 128],
                    start=True, stop=True),
                    reads=("onesrf", "stats"), writes=("p4",))

        def emit_rows_and_bcast_m(sp_sb, mcol, mkeyn, hilo, ng):
            for g in range(ng):
                tens(lambda e, g=g: e.transpose(
                    P[6][0:1, 128 * g:128 * g + 128],
                    sp_sb[:, mcol + g:mcol + g + 1], ident_f[:, :]),
                    reads=(mkeyn,), writes=("p6",))
            vec(lambda e, ng=ng: e.tensor_copy(stats_sb[0:1, 0:128 * ng],
                                               P[6][0:1, 0:128 * ng]),
                reads=("p6", "stats"), writes=("statsm", "stats"))
            for g in range(ng):
                tens(lambda e, g=g: e.matmul(
                    P[6][0:128, 128 * g:128 * g + 128], onesr_f[0:1, 0:128],
                    stats_sb[0:1, 128 * g:128 * g + 128],
                    start=True, stop=True),
                    reads=("onesrf", "statsm"), writes=("p6",))

        def emit_bcast_normalize(sp_sb, rcol, mcol, rkeyn, mkeyn,
                                 src_f32, src_key, outname,
                                 need_f32, hilo, affine, ng, use_bf_src,
                                 cast_n="s1"):
            nrm_eng = lambda c: "vector"
            emit_rows_and_bcast_r(sp_sb, rcol, rkeyn, hilo, ng)
            emit_rows_and_bcast_m(sp_sb, mcol, mkeyn, hilo, ng)
            if use_bf_src:
                ttile, tsrc_t = SB[outname], SB[cast_n]
                for c in range(NCHUNK):
                    pr.op(nrm_eng(c),
                          lambda e, c=c, ttile=ttile, tsrc_t=tsrc_t:
                          e.tensor_tensor(
                              ttile[:, T * c:T * c + T],
                              tsrc_t[:, T * c:T * c + T],
                              P[4][0:128, 0:T], alu.mult),
                          reads=(rkey(cast_n, c), "p4"),
                          writes=wkeys(outname, c))
            else:
                for c in range(NCHUNK):
                    pr.op(nrm_eng(c), lambda e, c=c: e.tensor_tensor(
                        u_t[:, T * c:T * c + T], src_f32[:, T * c:T * c + T],
                        P[4][0:128, 0:T], alu.mult),
                        reads=(skey(src_key, c), "p4", "gu:%d" % c),
                        writes=("gu:%d" % c,))
            if use_bf_src:
                out_bf = SB[outname]
                for c in range(NCHUNK):
                    pr.op(nrm_eng(c),
                          lambda e, c=c, out_bf=out_bf: e.tensor_tensor(
                              out_bf[:, T * c:T * c + T],
                              out_bf[:, T * c:T * c + T],
                              P[6][0:128, 0:T], alu.subtract),
                          reads=(rkey(outname, c), "p6"),
                          writes=wkeys(outname, c))
            else:
                for c in range(NCHUNK):
                    pr.op(nrm_eng(c), lambda e, c=c: e.tensor_tensor(
                        u_t[:, T * c:T * c + T], u_t[:, T * c:T * c + T],
                        P[6][0:128, 0:T], alu.subtract),
                        reads=("gu:%d" % c, "p6"), writes=("gu:%d" % c,))
            if affine:
                for c in range(NCHUNK):
                    sca(lambda e, c=c: e.activation(
                        u_t[:, T * c:T * c + T], u_t[:, T * c:T * c + T],
                        AF.Identity, bias=bet[:, c:c + 1],
                        scale=gam[:, c:c + 1]),
                        reads=("gu:%d" % c, "gam", "bet"),
                        writes=("gu:%d" % c,))
            if need_f32 and outname is not None:
                out_bf = SB[outname]
                for c in range(NCHUNK):
                    sca(lambda e, c=c, out_bf=out_bf: e.activation(
                        out_bf[:, T * c:T * c + T], u_t[:, T * c:T * c + T],
                        AF.Identity),
                        reads=("gu:%d" % c,), writes=wkeys(outname, c))

        def emit_ln_stats_chunk(src_f32, src_key, c, cast_n, sq_n,
                                cast_eng="vector", sq_eng="vector"):
            ct, qt = SB[cast_n], SB[sq_n]
            if cast_eng == "scalar":
                sca(lambda e, c=c, ct=ct: e.activation(
                    ct[:, T * c:T * c + T], src_f32[:, T * c:T * c + T],
                    AF.Identity),
                    reads=(skey(src_key, c),), writes=wkeys(cast_n, c))
            else:
                pr.op(cast_eng, lambda e, c=c, ct=ct: e.tensor_copy(
                    ct[:, T * c:T * c + T], src_f32[:, T * c:T * c + T]),
                    reads=(skey(src_key, c),), writes=wkeys(cast_n, c))
            pr.op(sq_eng, lambda e, c=c, ct=ct, qt=qt: e.tensor_tensor(
                qt[:, T * c:T * c + T], ct[:, T * c:T * c + T],
                ct[:, T * c:T * c + T], alu.mult),
                reads=(rkey(cast_n, c),), writes=wkeys(sq_n, c))

        def emit_ln_stats_mm(c, cast_n, sq_n):
            ct, qt = SB[cast_n], SB[sq_n]
            tens(lambda e, c=c, ct=ct: e.matmul(
                P[4][0:1, 0:T], onesc[:, 0:1], ct[:, T * c:T * c + T],
                start=(c == 0), stop=(c == 7), skip_group_check=True),
                reads=("ones", rkey(cast_n, c)), writes=("p4",))
            tens(lambda e, c=c, qt=qt: e.matmul(
                P[5][0:1, 0:T], onesc[:, 0:1], qt[:, T * c:T * c + T],
                start=(c == 0), stop=False, skip_group_check=True),
                reads=("ones", rkey(sq_n, c)), writes=("p5",))
            if c == 7:
                tens(lambda e: e.matmul(
                    P[5][0:1, 0:T], epsr[0:1, 0:1], onesr[0:1, 0:T],
                    start=False, stop=True, skip_group_check=True),
                    reads=("epsr", "onesr"), writes=("p5",))

        def emit_ln_head():
            ng = NG
            vec(lambda e: e.tensor_copy(stats_sb[0:1, 0:T], P[4][0:1, 0:T]),
                reads=("p4",), writes=("stats",))
            vec(lambda e: e.tensor_copy(stats_sb[32:33, 0:T], P[5][0:1, 0:T]),
                reads=("p5",), writes=("stats",))
            for g in range(ng):
                tens(lambda e, g=g: e.transpose(
                    P[4][0:128, g:g + 1],
                    stats_sb[0:1, 128 * g:128 * g + 128], ident_f[0:1, 0:1]),
                    reads=("stats",), writes=("p4",))
                tens(lambda e, g=g, ng=ng: e.transpose(
                    P[4][0:128, ng + g:ng + g + 1],
                    stats_sb[32:33, 128 * g:128 * g + 128],
                    ident_f[32:33, 32:33]),
                    reads=("stats",), writes=("p4",))
            vec(lambda e, ng=ng: e.tensor_copy(small[:, 0:2 * ng],
                                               P[4][0:128, 0:2 * ng]),
                reads=("p4",), writes=("sm0",))
            mu0, e20, t10, t20, r0, m20 = 8, 12, 16, 20, 24, 28
            vec(lambda e, ng=ng: e.tensor_scalar(
                small[:, mu0:mu0 + 2 * ng], small[:, 0:2 * ng], 1.0 / D, None,
                alu.mult, alu.bypass),
                reads=("sm0",), writes=("smmu", "sme2"))
            vec(lambda e, ng=ng: e.tensor_tensor(
                small[:, t10:t10 + ng], small[:, mu0:mu0 + ng],
                small[:, mu0:mu0 + ng], alu.mult),
                reads=("smmu",), writes=("smt1",))
            vec(lambda e, ng=ng: e.scalar_tensor_tensor(
                small[:, t20:t20 + ng], small[:, t10:t10 + ng], -1.0,
                small[:, e20:e20 + ng], alu.mult, alu.add),
                reads=("smt1", "sme2"), writes=("smv",))
            vec(lambda e, ng=ng: e.tensor_copy(
                small[:, t10:t10 + ng], small_i[:, t20:t20 + ng]),
                reads=("smv",), writes=("smt1",))
            vec(lambda e, ng=ng: e.tensor_scalar(
                small[:, t10:t10 + ng], small[:, t10:t10 + ng], -0.5,
                1597463007.0, alu.mult, alu.add),
                reads=("smt1",), writes=("smt1",))
            vec(lambda e, ng=ng: e.tensor_copy(
                small_i[:, r0:r0 + ng], small[:, t10:t10 + ng]),
                reads=("smt1",), writes=("smr",))
            for _ in range(NEWTON_ITERS):
                vec(lambda e, ng=ng: e.tensor_tensor(
                    small[:, t10:t10 + ng], small[:, r0:r0 + ng],
                    small[:, r0:r0 + ng], alu.mult),
                    reads=("smr",), writes=("smt1",))
                vec(lambda e, ng=ng: e.scalar_tensor_tensor(
                    small[:, m20:m20 + ng], small[:, t10:t10 + ng], -0.5,
                    small[:, t20:t20 + ng], alu.mult, alu.mult),
                    reads=("smt1", "smv"), writes=("smt2",))
                vec(lambda e, ng=ng: e.scalar_tensor_tensor(
                    small[:, r0:r0 + ng], small[:, m20:m20 + ng], 1.5,
                    small[:, r0:r0 + ng], alu.add, alu.mult),
                    reads=("smt2", "smr"), writes=("smr",))
            vec(lambda e, ng=ng: e.tensor_tensor(
                small[:, m20:m20 + ng], small[:, mu0:mu0 + ng],
                small[:, r0:r0 + ng], alu.mult),
                reads=("smmu", "smr"), writes=("smm2",))

        def emit_ln_tail2(src_f32, src_key, outname, need_f32, hilo,
                          affine, cast_n="s1"):
            r0, m20 = 24, 28
            emit_bcast_normalize(small, r0, m20, "smr", "smm2",
                                 src_f32, src_key, outname,
                                 need_f32, hilo, affine, NG,
                                 use_bf_src=not (need_f32 or affine),
                                 cast_n=cast_n)

        def emit_ln_tail(src_f32, src_key, outname, need_f32, hilo,
                         affine, cast_n="s1", sq_n="s2"):
            emit_ln_head()
            emit_ln_tail2(src_f32, src_key, outname, need_f32, hilo,
                          affine, cast_n=cast_n)

        def emit_ln(src_f32, src_key, outname, need_f32, hilo,
                    affine, cast_n="s1", sq_n="s2"):
            for c in range(NCHUNK):
                emit_ln_stats_chunk(src_f32, src_key, c, cast_n, sq_n)
                emit_ln_stats_mm(c, cast_n, sq_n)
            emit_ln_tail(src_f32, src_key, outname, need_f32, hilo,
                         affine, cast_n, sq_n)

        aqueues = ["sync", "gpsimd", "scalar"]

        def emit_ffn(srcname="s3", post_oc=None):
            src = SB[srcname]
            for o in range(HCHUNK):
                par = o % 3
                q = "sync"
                dma(q, bass.AP(a_st[par], 0, [[1024, 128], [1, 1024]]),
                    bass.AP(asp.tensor, 1024 * o,
                            [[8 * HID, 128], [1, 1024]]),
                    reads=tuple("asp:%d:%d" % (o, dj) for dj in range(4)),
                    writes=("ast:%d" % par,), ch="ast%d" % par)
                pr.mm_group(
                    [(lambda e, d=d, par=par, src=src: e.matmul(
                        P[par][:, 0:T], a_st[par][:, 128 * d:128 * d + 128],
                        src[:, T * d:T * d + T],
                        start=(d == 0), stop=(d == 7)))
                     for d in range(8)],
                    reads=("ast:%d" % par,), writes=("p%d" % par,),
                    reads_each=[(rkey(srcname, d),) for d in range(8)])
                sca(lambda e, o=o, par=par: e.activation(
                    g_t[:, T * o:T * o + T], P[par][:, 0:T], AF.Gelu,
                    bias=bfull[:, o:o + 1]),
                    reads=("p%d" % par, "bfull"), writes=("gu:%d" % (o // 2),))
            for oc in range(8):
                par = oc % 4
                pr.mm_group(
                    [(lambda e, k=k, oc=oc, par=par: e.matmul(
                        P[par][:, 0:T],
                        c_sb[:, 1024 * k + 128 * oc:1024 * k + 128 * oc + 128],
                        g_t[:, T * k:T * k + T],
                        start=(k == 0), stop=(k == HCHUNK - 1)))
                     for k in range(HCHUNK)],
                    reads=ALLC, writes=("p%d" % par,),
                    reads_each=[("gu:%d" % (k // 2),) for k in range(HCHUNK)])
                vec(lambda e, oc=oc, par=par: e.scalar_tensor_tensor(
                    x_t[:, T * oc:T * oc + T], P[par][:, 0:T],
                    cch[:, oc:oc + 1], x_t[:, T * oc:T * oc + T],
                    alu.add, alu.add),
                    reads=("p%d" % par, "cch", "x:%d" % oc),
                    writes=("x:%d" % oc,))
                if post_oc is not None:
                    post_oc(oc)

        def emit_kv_group(iseq, it, g):
            if True:
                for half in range(2):
                    kp = (2 * g + half) % 2
                    pr.mm_group(
                        [(lambda e, d=d, g=g, half=half, kp=kp: e.matmul(
                            P[kp][:, 0:512],
                            s3[:, T * d + 128 * g:T * d + 128 * g + 128],
                            w_a[:, 1024 * d + 512 * half:
                                1024 * d + 512 * half + 512],
                            start=(d == 0), stop=(d == 7)))
                         for d in range(8)],
                        reads=ALLWA, writes=("p%d" % kp,),
                        reads_each=[("s3:%d" % d,) for d in range(8)])
                    sca(lambda e, half=half, kp=kp: e.activation(
                        u_t[:, 512 * half:512 * half + 512],
                        P[kp][:, 0:512], AF.Tanh, scale=0.5),
                        reads=("p%d" % kp, "gu:%d" % half),
                        writes=("gu:%d" % half,)),
                    vec(lambda e, half=half: e.tensor_scalar(
                        kv_k[:, 512 * half:512 * half + 512],
                        u_t[:, 512 * half:512 * half + 512],
                        0.5, 0.5, alu.mult, alu.add),
                        reads=("gu:%d" % half,),
                        writes=("kvk", "kvkA", "kvkB"))
                    pr.mm_group(
                        [(lambda e, d=d, g=g, half=half, kp=kp: e.matmul(
                            P[2 + kp][:, 0:512],
                            s3[:, T * d + 128 * g:T * d + 128 * g + 128],
                            w_b[:, 1024 * d + 512 * half:
                                1024 * d + 512 * half + 512],
                            start=(d == 0), stop=(d == 7)))
                         for d in range(8)],
                        reads=ALLWB, writes=("p%d" % (2 + kp),),
                        reads_each=[("s3:%d" % d,) for d in range(8)])
                    vec(lambda e, half=half, kp=kp: e.tensor_copy(
                        bass.AP(kv_v, 129 * 4 * half,
                                [[8 * 129, 128], [129, 4], [1, 128]]),
                        bass.AP(P[2 + kp], 0,
                                [[512, 128], [128, 4], [1, 128]])),
                        reads=("p%d" % (2 + kp),), writes=("kvv",))
                first = (it == 0 and g == 0)
                banks = [(0, [0, 1, 2]), (1, [3, 4, 5]), (2, [6, 7])]
                for bank, pairs in banks:
                    pr.mm_group(
                        [(lambda e, j=j, bank=bank, st0=(j == pairs[0]),
                          st1=(j == pairs[-1]): e.matmul(
                              P[5 + bank][:, 129 * (j - 3 * bank):
                                          129 * (j - 3 * bank) + 129],
                              kv_k[:, 128 * j:128 * j + 128],
                              kv_v[:, 129 * j:129 * j + 129],
                              start=st0, stop=st1))
                         for j in pairs],
                        reads=("kvk", "kvv", "kvv1"),
                        writes=("p%d" % (5 + bank),))
                    w0 = 387 * bank
                    wd = 129 * len(pairs)
                    if first:
                        vec(lambda e, bank=bank, w0=w0, wd=wd:
                            e.tensor_copy(kvacc[:, w0:w0 + wd],
                                          P[5 + bank][:, 0:wd]),
                            reads=("p%d" % (5 + bank),),
                            writes=("kvacc", "cs:0", "cs:1", "cs:2", "cs:3"))
                    else:
                        vec(lambda e, bank=bank, w0=w0, wd=wd:
                            e.tensor_tensor(kvacc[:, w0:w0 + wd],
                                            kvacc[:, w0:w0 + wd],
                                            P[5 + bank][:, 0:wd], alu.add),
                            reads=("p%d" % (5 + bank), "kvacc"),
                            writes=("kvacc",))

        def emit_kv(iseq, it):
            for g in range(NG):
                emit_kv_group(iseq, it, g)

        def emit_kv_evacs(iseq):
            for j in range(8):
                base = (iseq * 8 + j) * 128
                vec(lambda e, j=j, base=base: e.tensor_copy(
                    kvbd[0:64, base:base + 64],
                    kvacc[0:64, 129 * j:129 * j + 64]),
                    reads=("kvacc",), writes=("kvbd",))
                vec(lambda e, j=j, base=base: e.tensor_copy(
                    kvbd[64:128, base + 64:base + 128],
                    kvacc[64:128, 129 * j + 64:129 * j + 128]),
                    reads=("kvacc",), writes=("kvbd",))
                kb = (iseq * 8 + j) * 16
                vec(lambda e, kb=kb, j=j: e.tensor_copy(
                    ksc[0:64, kb + 2 * j:kb + 2 * j + 1],
                    kvacc[0:64, 129 * j + 128:129 * j + 129]),
                    reads=("kvacc",), writes=("ksc",))
                vec(lambda e, kb=kb, j=j: e.tensor_copy(
                    ksc[64:128, kb + 2 * j + 1:kb + 2 * j + 2],
                    kvacc[64:128, 129 * j + 128:129 * j + 129]),
                    reads=("kvacc",), writes=("ksc",))

        pending_kv = None
        for iseq in range(n_seq):
            for it in range(n_tiles):
                if (iseq, it) != (0, 0):
                    emit_x_load(iseq, it)
                if pending_kv is not None:
                    ps, pt = pending_kv
                    emit_kv_group(ps, pt, 0)
                    for c in range(4):
                        emit_ln_stats_chunk(x_t, "x:%d", c, "s1", "s2",
                                            cast_eng="gpsimd",
                                            sq_eng="gpsimd")
                    emit_kv_group(ps, pt, 1)
                    for c in range(4, 8):
                        emit_ln_stats_chunk(x_t, "x:%d", c, "s1", "s2",
                                            cast_eng="gpsimd",
                                            sq_eng="gpsimd")
                    emit_kv_group(ps, pt, 2)
                    for c in range(NCHUNK):
                        emit_ln_stats_mm(c, "s1", "s2")
                    emit_ln_head()
                    emit_kv_group(ps, pt, 3)
                    if pt == n_tiles - 1:
                        emit_kv_evacs(ps)
                    emit_ln_tail2(x_t, "x:%d", "s2", False, False, False)
                else:
                    emit_ln(x_t, "x:%d", "s2", False, False, False)

                def ln2_cb(oc):
                    emit_ln_stats_chunk(x_t, "x:%d", oc, "s1", "s2")
                    if oc > 0:
                        emit_ln_stats_mm(oc - 1, "s1", "s2")
                emit_ffn("s2", post_oc=ln2_cb)
                emit_ln_stats_mm(7, "s1", "s2")
                dma("gpsimd",
                    bass.AP(x1sp.tensor, iseq * D * seq_len + it * T,
                            [[seq_len, 128], [128 * seq_len, 8], [1, T]]),
                    bass.AP(x_t, 0, [[NCHUNK * T, 128], [T, 8], [1, T]]),
                    reads=tuple("x:%d" % c for c in range(NCHUNK)),
                    writes=("x1sp:%d:%d" % (iseq, it),),
                    ch="spill")
                emit_ln_tail(x_t, "x:%d", "s3", True, True, False)
                for c in range(NCHUNK):
                    pr.op("vector" if c % 2 == 0 else "gpsimd",
                          lambda e, c=c: e.tensor_tensor(
                              s1[:, T * c:T * c + T], u_t[:, T * c:T * c + T],
                              s3[:, T * c:T * c + T], alu.subtract),
                          reads=("gu:%d" % c, "s3:%d" % c),
                          writes=wkeys("s1", c))
                dma("sync",
                    bass.AP(xn2hi.tensor, iseq * D * seq_len + it * T,
                            [[seq_len, 128], [128 * seq_len, 8], [1, T]]),
                    bass.AP(s3, 0, [[NCHUNK * T, 128], [T, 8], [1, T]]),
                    reads=tuple("s3:%d" % c for c in range(8)),
                    writes=("xh:%d:%d" % (iseq, it),), ch="xh")
                dma("sync",
                    bass.AP(xn2lo.tensor, iseq * D * seq_len + it * T,
                            [[seq_len, 128], [128 * seq_len, 8], [1, T]]),
                    bass.AP(s1, 0, [[NCHUNK * T, 128], [T, 8], [1, T]]),
                    reads=tuple("s1x:%d" % c for c in range(8)),
                    writes=("xl:%d:%d" % (iseq, it),), ch="xl")
                pending_kv = (iseq, it)
        emit_kv(*pending_kv)
        emit_kv_evacs(n_seq - 1)

        for half in range(2):
            wa_toks = tuple("wa:%d" % c for c in range(4 * half, 4 * half + 4))
            wb_toks = tuple("wb:%d" % c for c in range(4 * half, 4 * half + 4))
            dma("sync" if half == 0 else "scalar",
                bass.AP(w_a, 4096 * half, [[8192, 128], [1, 4096]]),
                bass.AP(wqsp.tensor, 4096 * half, [[8192, 128], [1, 4096]]),
                reads=("wsp0",) + wa_toks, writes=wa_toks,
                ch="wrl0" if half == 0 else "wrl2")
            dma("gpsimd",
                bass.AP(w_b, 4096 * half, [[8192, 128], [1, 4096]]),
                bass.AP(wfsp.tensor, 4096 * half, [[8192, 128], [1, 4096]]),
                reads=("wsp1",) + wb_toks, writes=wb_toks, ch="wrl1")

        for iseq in range(n_seq):
            for it in range(n_tiles):
                dma("scalar",
                    bass.AP(x_t, 0, [[NCHUNK * T, 128], [T, 8], [1, T]]),
                    bass.AP(x1sp.tensor, iseq * D * seq_len + it * T,
                            [[seq_len, 128], [128 * seq_len, 8], [1, T]]),
                    reads=("x1sp:%d:%d" % (iseq, it),)
                    + tuple("x:%d" % c for c in range(NCHUNK)),
                    writes=tuple("x:%d" % c for c in range(NCHUNK)),
                    ch="x")
                dma("sync",
                    bass.AP(s3, 0, [[NCHUNK * T, 128], [T, 8], [1, T]]),
                    bass.AP(xn2hi.tensor, iseq * D * seq_len + it * T,
                            [[seq_len, 128], [128 * seq_len, 8], [1, T]]),
                    reads=("s3", "xh:%d:%d" % (iseq, it))
                    + tuple("s3:%d" % c for c in range(8)),
                    writes=wkeys("s3"), ch="xh")
                dma("sync",
                    bass.AP(s1, 0, [[NCHUNK * T, 128], [T, 8], [1, T]]),
                    bass.AP(xn2lo.tensor, iseq * D * seq_len + it * T,
                            [[seq_len, 128], [128 * seq_len, 8], [1, T]]),
                    reads=("s1", "xl:%d:%d" % (iseq, it))
                    + tuple("s1x:%d" % c for c in range(8)),
                    writes=wkeys("s1"), ch="xl")
                for c in range(NCHUNK):
                    pr.op("vector" if c % 2 == 0 else "gpsimd",
                          lambda e, c=c: e.tensor_tensor(
                              u_t[:, T * c:T * c + T], s3[:, T * c:T * c + T],
                              s1[:, T * c:T * c + T], alu.add),
                          reads=("s3:%d" % c, "s1x:%d" % c, "gu:%d" % c),
                          writes=("gu:%d" % c,))
                for oc in range(8):
                    par = oc % 4
                    pr.mm_group(
                        [(lambda e, d=d, oc=oc, par=par: e.matmul(
                            P[par][:, 0:T],
                            w_a[:, 1024 * d + 128 * oc:
                                1024 * d + 128 * oc + 128],
                            s3[:, T * d:T * d + T],
                            start=(d == 0), stop=(d == 7)))
                         for d in range(8)],
                        reads=ALLWA, writes=("p%d" % par,),
                        reads_each=[("s3:%d" % d,) for d in range(8)])
                    sca(lambda e, oc=oc, par=par: e.activation(
                        s2[:, T * oc:T * oc + T], P[par][:, 0:T],
                        AF.Tanh, scale=0.5),
                        reads=("p%d" % par,), writes=("s2",) + wkeys("s2", oc))
                    vec(lambda e, oc=oc: e.tensor_scalar(
                        s2[:, T * oc:T * oc + T], s2[:, T * oc:T * oc + T],
                        0.5, 0.5, alu.mult, alu.add),
                        reads=wkeys("s2", oc), writes=wkeys("s2", oc))
                kb0 = iseq * 8 * 16
                pr.mm_group(
                    [(lambda e, d=d, kb0=kb0: e.matmul(
                        P[4][0:16, 0:T],
                        ksc[:, kb0 + 16 * d:kb0 + 16 * d + 16],
                        s2[:, T * d:T * d + T],
                        start=(d == 0), stop=(d == 7)))
                     for d in range(8)],
                    reads=("ksc",), writes=("p4",),
                    reads_each=[("s2:%d" % d,) for d in range(8)])
                vec(lambda e: e.tensor_scalar(
                    den_sb[0:16, 0:T], P[4][0:16, 0:T], 1e-8, None,
                    alu.add, alu.bypass),
                    reads=("p4",), writes=("den",))
                warm(4)
                for g in range(NG):
                    tens(lambda e, g=g: e.transpose(
                        P[4][0:128, 16 * g:16 * g + 16],
                        den_sb[0:16, 128 * g:128 * g + 128],
                        ident_f[0:16, 0:16]),
                        reads=("den",), writes=("p4",))
                warm(6)
                vec(lambda e: e.tensor_copy(dsp[:, 0:16 * NG],
                                            P[4][0:128, 0:16 * NG]),
                    reads=("p4",), writes=("dsp",))
                vec(lambda e: e.reciprocal(dsp[:, 0:16 * NG],
                                           dsp[:, 0:16 * NG]),
                    reads=("dsp",), writes=("dsp",))
                for g in range(NG):
                    tens(lambda e, g=g: e.transpose(
                        P[4][0:16, 128 * g:128 * g + 128],
                        dsp[:, 16 * g:16 * g + 16], ident_f[:, :]),
                        reads=("dsp",), writes=("p4",))
                vec(lambda e: e.tensor_copy(rdh_sb[0:16, 0:T],
                                            P[4][0:16, 0:T]),
                    reads=("p4",), writes=("rdh",))
                vec(lambda e: e.tensor_tensor(
                    rdl_sb[0:16, 0:T], P[4][0:16, 0:T], rdh_sb[0:16, 0:T],
                    alu.subtract),
                    reads=("p4", "rdh"), writes=("rdl",))
                for j in range(8):
                    pn = P[j % 2]
                    pnk = "p%d" % (j % 2)
                    pe = P[5 + (j % 2)]
                    pek = "p%d" % (5 + (j % 2))
                    kvb = (iseq * 8 + j) * 128
                    tens(lambda e, pn=pn, kvb=kvb, j=j: e.matmul(
                        pn[:, 0:T], kvbd[:, kvb:kvb + 128],
                        s2[:, T * j:T * j + T], start=True, stop=True),
                        reads=("kvbd", "s2:%d" % j), writes=(pnk,))
                    pr.mm_group(
                        [lambda e, pe=pe, j=j: e.matmul(
                            pe[:, 0:T], ej_b[:, 128 * j:128 * j + 128],
                            rdh_sb[0:16, 0:T], start=True, stop=False),
                         lambda e, pe=pe, j=j: e.matmul(
                            pe[:, 0:T], ej_b[:, 128 * j:128 * j + 128],
                            rdl_sb[0:16, 0:T], start=False, stop=True)],
                        reads=("ej", "rdh", "rdl"), writes=(pek,))
                    sca(lambda e, pn=pn: e.activation(
                        rdb_sb[:, 0:T], pn[:, 0:T], AF.Identity),
                        reads=(pnk,), writes=("rdb", "wtmp0", "wtmp1"))
                    vec(lambda e, pe=pe, j=j: e.tensor_tensor(
                        s1[:, T * j:T * j + T], rdb_sb[:, 0:T],
                        pe[:, 0:T], alu.mult),
                        reads=(pek, "rdb"), writes=("s1x:%d" % j,))
                for oc in range(8):
                    par = oc % 4
                    pr.mm_group(
                        [(lambda e, d=d, oc=oc, par=par: e.matmul(
                            P[par][:, 0:T],
                            w_b[:, 1024 * d + 128 * oc:
                                1024 * d + 128 * oc + 128],
                            s1[:, T * d:T * d + T],
                            start=(d == 0), stop=(d == 7)))
                         for d in range(8)],
                        reads=ALLWB, writes=("p%d" % par,),
                        reads_each=[("s1x:%d" % d,) for d in range(8)])
                    vec(lambda e, oc=oc, par=par: e.tensor_tensor(
                        u_t[:, T * oc:T * oc + T], P[par][:, 0:T],
                        u_t[:, T * oc:T * oc + T], alu.add),
                        reads=("p%d" % par, "gu:%d" % oc),
                        writes=("gu:%d" % oc,))
                    emit_ln_stats_chunk(u_t, "gu:%d", oc, "s2", "s3")
                    if oc > 0:
                        emit_ln_stats_mm(oc - 1, "s2", "s3")
                emit_ln_stats_mm(7, "s2", "s3")
                emit_ln_tail(u_t, "gu:%d", None, False, True, True,
                             cast_n="s2", sq_n="s3")
                warm(26)
                for oc in range(8):
                    pr.op("vector" if oc % 2 == 0 else "gpsimd",
                          lambda e, oc=oc: e.tensor_tensor(
                              x_t[:, T * oc:T * oc + T],
                              x_t[:, T * oc:T * oc + T],
                              u_t[:, T * oc:T * oc + T], alu.add),
                          reads=("x:%d" % oc, "gu:%d" % oc),
                          writes=("x:%d" % oc,))
                emit_ln(x_t, "x:%d", "s2", False, False, False,
                        cast_n="s1", sq_n="s3")
                warm(14)
                emit_ffn("s2")
                dma("gpsimd",
                    bass.AP(dout.tensor, iseq * D * seq_len + it * T,
                            [[seq_len, 128], [128 * seq_len, 8], [1, T]]),
                    bass.AP(x_t, 0, [[NCHUNK * T, 128], [T, 8], [1, T]]),
                    reads=tuple("x:%d" % c for c in range(NCHUNK)),
                    writes=("outd",), ch="out")

        pr.raw_wait("gpsimd", "dch_out", pr.cnt["dch_out"])
        pr.realize(blk)
    return nc



def kernel(**inputs):
    x = np.asarray(inputs["L0_attn_inputs"], dtype=np.float32)
    consts = host_constants(inputs)
    nc = build_program(n_seq=2, seq_len=4096, T=512)
    in_maps = []
    for core in range(8):
        xs = x[2 * core:2 * core + 2]
        xfm = np.ascontiguousarray(xs.transpose(0, 2, 1)).reshape(2 * D, 4096)
        m = {"x": xfm}
        m.update(consts)
        in_maps.append(m)
    res = run_bass_kernel_spmd(nc, in_maps, core_ids=list(range(8)))
    outs = []
    for core in range(8):
        o = np.asarray(res.results[core]["out"]).reshape(2, D, 4096)
        outs.append(o.transpose(0, 2, 1))
    return np.ascontiguousarray(np.concatenate(outs, axis=0), dtype=np.float32)


# version 29
# speedup vs baseline: 1.2875x; 1.0003x over previous
import numpy as np
from contextlib import ExitStack

import concourse.bass as bass
import concourse.mybir as mybir
from concourse.bass_utils import run_bass_kernel_spmd
from concourse.alu_op_type import AluOpType as alu

F32 = mybir.dt.float32
BF16 = mybir.dt.bfloat16
AF = mybir.ActivationFunctionType

D = 1024
HID = 4096
NCHUNK = 8
HCHUNK = 32
LN_EPS = 1e-5
NEWTON_ITERS = 2
DEBUG_TAPS = False


def _stage_chunk(w_s, s, c, transpose=False):
    S = 2 ** s
    M = np.zeros((128, 128), dtype=np.float32)
    w = np.asarray(w_s, dtype=np.float32).reshape(HID // (2 * S), S, 2, 2)
    for p in range(128):
        i = 128 * c + p
        if i & S:
            continue
        n1 = i // (2 * S)
        j = i % S
        M[p, p] = w[n1, j, 0, 0]
        M[p, p + S] = w[n1, j, 0, 1] if not transpose else w[n1, j, 1, 0]
        M[p + S, p] = w[n1, j, 1, 0] if not transpose else w[n1, j, 0, 1]
        M[p + S, p + S] = w[n1, j, 1, 1]
    return M


def _diag_vecs(w_s, s, c, transpose=False):
    S = 2 ** s
    w = np.asarray(w_s, dtype=np.float32).reshape(HID // (2 * S), S, 2, 2)
    vs = np.zeros(128, dtype=np.float32)
    vp = np.zeros(128, dtype=np.float32)
    for p in range(128):
        i = 128 * c + p
        if not (i & S):
            n1 = i // (2 * S)
            j = i % S
            vs[p] = w[n1, j, 0, 0]
            vp[p] = w[n1, j, 0, 1] if not transpose else w[n1, j, 1, 0]
        else:
            a = i - S
            n1 = a // (2 * S)
            j = a % S
            vs[p] = w[n1, j, 1, 1]
            vp[p] = w[n1, j, 1, 0] if not transpose else w[n1, j, 0, 1]
    return vs, vp


def host_constants(inp):
    c = {}
    c["wq"] = np.ascontiguousarray(inp["L0_attn_W_Q"], dtype=np.float32)
    c["wk"] = np.ascontiguousarray(inp["L0_attn_W_K"], dtype=np.float32)
    c["wv"] = np.ascontiguousarray(inp["L0_attn_W_V"], dtype=np.float32)
    c["wfc"] = np.ascontiguousarray(inp["L0_attn_W_fc"], dtype=np.float32)
    c["gamma"] = np.ascontiguousarray(
        np.asarray(inp["L0_attn_gamma"], np.float32).reshape(8, 128).T)
    c["beta"] = np.ascontiguousarray(
        np.asarray(inp["L0_attn_beta"], np.float32).reshape(8, 128).T)
    c["wexpT"] = np.ascontiguousarray(
        np.asarray(inp["L0_ffn_W_exp"], np.float32).T)
    c["wcon"] = np.ascontiguousarray(inp["L0_ffn_W_con"], dtype=np.float32)
    c["bexp"] = np.ascontiguousarray(
        np.asarray(inp["L0_ffn_b_exp"], np.float32).reshape(32, 128).T)
    c["upbias"] = np.ascontiguousarray(
        np.asarray(inp["L0_ffn_up_bias"], np.float32).reshape(32, 128).T)
    c["bcon"] = np.ascontiguousarray(
        np.asarray(inp["L0_ffn_b_con"], np.float32).reshape(8, 128).T)
    c["dbias"] = np.ascontiguousarray(
        np.asarray(inp["L0_ffn_down_bias"], np.float32).reshape(8, 128).T)

    upw = np.asarray(inp["L0_ffn_up_weights"], np.float32)
    dnw = np.asarray(inp["L0_ffn_down_weights"], np.float32)
    slo_u = np.zeros((7, 32, 128, 128), dtype=np.float32)
    slo_dT = np.zeros((7, 32, 128, 128), dtype=np.float32)
    for s in range(7):
        for ch in range(32):
            slo_u[s, ch] = _stage_chunk(upw[s], s, ch)
            slo_dT[s, ch] = _stage_chunk(dnw[s], s, ch, transpose=True)
    c["slo_u"] = np.ascontiguousarray(
        slo_u.transpose(1, 2, 0, 3).reshape(32 * 128, 7 * 128))
    c["slo_dT"] = np.ascontiguousarray(
        slo_dT.transpose(1, 2, 0, 3).reshape(32 * 128, 7 * 128))
    dv_u = np.zeros((128, 5 * 32 * 2), dtype=np.float32)
    dv_d = np.zeros((128, 5 * 32 * 2), dtype=np.float32)
    for si in range(5):
        s = si + 7
        for ch in range(32):
            vs_u, vp_u = _diag_vecs(upw[s], s, ch)
            dv_u[:, (si * 32 + ch) * 2 + 0] = vs_u
            dv_u[:, (si * 32 + ch) * 2 + 1] = vp_u
            vs_d, vp_d = _diag_vecs(dnw[s], s, ch, transpose=True)
            dv_d[:, (si * 32 + ch) * 2 + 0] = vs_d
            dv_d[:, (si * 32 + ch) * 2 + 1] = vp_d
    c["dv_u"] = dv_u
    c["dv_d"] = dv_d
    c["ident"] = np.eye(128, dtype=np.float32)
    ej = np.zeros((16, 8 * 128), dtype=np.float32)
    for j in range(8):
        ej[2 * j, 128 * j:128 * j + 64] = 1.0
        ej[2 * j + 1, 128 * j + 64:128 * j + 128] = 1.0
    c["ej"] = ej
    return c



ENGINES = ("tensor", "vector", "scalar", "sync", "gpsimd")


class Prog:
    def __init__(self, nc, ctx):
        self.nc = nc
        self.stream = {e: [] for e in ENGINES}
        self.sem = {}
        self.cnt = {}
        for e in ENGINES:
            self.sem[e] = ctx.enter_context(nc.semaphore("sem_" + e))
            self.cnt[e] = 0
        self.ctx = ctx
        self.wr = {}
        self.rd = {}
        self.floor = {e: {} for e in ENGINES}

    def _waits_for(self, eng, reads, writes):
        toks = []
        for r in reads:
            t = self.wr.get(r)
            if t:
                toks.append(t)
        for r in writes:
            t = self.wr.get(r)
            if t:
                toks.append(t)
            toks.extend(self.rd.get(r, ()))
        need = {}
        for s, v in toks:
            if v > need.get(s, -1):
                need[s] = v
        fl = self.floor[eng]
        out = []
        for s, v in sorted(need.items()):
            if fl.get(s, 0) >= v:
                continue
            assert v <= self.cnt[s], (eng, s, v, self.cnt[s])
            out.append((s, v))
            fl[s] = v
        return out

    def dma_ch(self, ch):
        name = "dch_" + ch
        if name not in self.sem:
            self.sem[name] = self.ctx.enter_context(nc_sem(self.nc, name))
            self.cnt[name] = 0
        return name

    def op(self, eng, build, reads=(), writes=(), dma=False, inc=True,
           ch=None):
        semname = self.dma_ch(ch) if dma else eng
        waits = self._waits_for(eng, reads, writes)
        tok = None
        if inc:
            self.cnt[semname] += 16 if dma else 1
            tok = (semname, self.cnt[semname])
        semh = self.sem[semname]
        amount = 16 if dma else 1
        sems = self.sem

        def closure(e, waits=waits, build=build, tok=tok):
            for s, v in waits:
                e.wait_ge(sems[s], v)
            instr = build(e)
            if tok is not None:
                instr.then_inc(semh, amount)
        self.stream[eng].append(closure)
        if tok is not None:
            for r in reads:
                self.rd.setdefault(r, []).append(tok)
            for r in writes:
                self.wr[r] = tok
                self.rd[r] = []
        return tok

    def mm_group(self, builders, reads, writes, reads_each=None):
        n = len(builders)
        for i, b in enumerate(builders[:-1]):
            r = reads + (tuple(reads_each[i]) if reads_each else ())
            self.op("tensor", b, reads=r, writes=writes, inc=False)
        r = reads + (tuple(reads_each[n - 1]) if reads_each else ())
        return self.op("tensor", builders[-1], reads=r, writes=writes)

    def raw_wait(self, eng, semname, value):
        if value <= 0:
            return
        fl = self.floor[eng]
        if fl.get(semname, 0) >= value:
            return
        fl[semname] = value
        sems = self.sem

        def closure(e):
            e.wait_ge(sems[semname], value)
        self.stream[eng].append(closure)

    def realize(self, blk):
        for e in ENGINES:
            ops = self.stream[e]

            def runner(eng, ops=ops):
                for f in ops:
                    f(eng)
            getattr(blk, e)(runner)



def nc_sem(nc, name):
    return nc.semaphore(name)


def build_program(n_seq=2, seq_len=4096, T=512):
    n_tiles = seq_len // T
    NG = T // 128
    nc = bass.Bass("TRN2", target_bir_lowering=False)

    dx = nc.dram_tensor("x", [n_seq * D, seq_len], F32,
                        kind="ExternalInput").ap()
    dout = nc.dram_tensor("out", [n_seq * D, seq_len], F32,
                          kind="ExternalOutput").ap()
    dins = {}
    for name, shape in [
        ("wq", [D, D]), ("wk", [D, D]), ("wv", [D, D]), ("wfc", [D, D]),
        ("gamma", [128, 8]), ("beta", [128, 8]),
        ("wexpT", [HID, D]), ("wcon", [HID, D]),
        ("bexp", [128, 32]), ("upbias", [128, 32]),
        ("bcon", [128, 8]), ("dbias", [128, 8]),
        ("slo_u", [32 * 128, 7 * 128]), ("slo_dT", [32 * 128, 7 * 128]),
        ("dv_u", [128, 5 * 32 * 2]), ("dv_d", [128, 5 * 32 * 2]),
        ("ident", [128, 128]), ("ej", [16, 8 * 128]),
    ]:
        dins[name] = nc.dram_tensor(name, shape, F32, kind="ExternalInput").ap()
    asp = nc.dram_tensor("asp", [128, 8 * HID], BF16).ap()
    wqsp = nc.dram_tensor("wqsp", [128, 8 * 1024], BF16).ap()
    wfsp = nc.dram_tensor("wfsp", [128, 8 * 1024], BF16).ap()
    x1sp = nc.dram_tensor("x1sp", [n_seq * D, seq_len], F32).ap()
    xn2hi = nc.dram_tensor("xn2hi", [n_seq * D, seq_len], BF16).ap()
    xn2lo = nc.dram_tensor("xn2lo", [n_seq * D, seq_len], BF16).ap()
    if DEBUG_TAPS:
        dbg1 = nc.dram_tensor("dbg1", [n_seq * D, seq_len], F32).ap()
        dbg2 = nc.dram_tensor("dbg2", [n_seq * D, seq_len], F32).ap()
        dbg3 = nc.dram_tensor("dbg3", [n_seq * D, seq_len], F32).ap()
        dbg4 = nc.dram_tensor("dbg4", [n_seq * D, seq_len], F32).ap()

    ctx = ExitStack()
    with ctx:
        S = lambda name, shape, dt: ctx.enter_context(
            nc.sbuf_tensor(name, shape, dt))
        c_sb = S("c_sb", [128, HCHUNK * 1024], BF16)
        w_a = S("w_a", [128, NCHUNK * 1024], BF16)
        w_b = S("w_b", [128, NCHUNK * 1024], BF16)
        ident_f = S("ident_f", [128, 128], F32)
        ident_b = S("ident_b", [128, 128], BF16)
        onesc = S("onesc", [128, 1], BF16)
        onesr = S("onesr", [1, 512], BF16)
        onesr_f = S("onesr_f", [1, 128], F32)
        epsr = S("epsr", [1, 1], BF16)
        ej_b = S("ej_b", [16, 8 * 128], BF16)
        bfull = S("bfull", [128, 32], F32)
        bvec = S("bvec", [128, 32], BF16)
        upb_f = S("upb_f", [128, 32], F32)
        cch = S("cch", [128, 8], F32)
        gam = S("gam", [128, 8], F32)
        bet = S("bet", [128, 8], F32)
        kvbd = S("kvbd", [128, n_seq * 8 * 128], BF16)
        ksc = S("ksc", [128, n_seq * 8 * 16], BF16)
        x_t = S("x_t", [128, NCHUNK * T], F32)
        s1 = S("s1", [128, NCHUNK * T], BF16)
        s2 = S("s2", [128, NCHUNK * T], BF16)
        s3 = S("s3", [128, NCHUNK * T], BF16)
        g_t = S("g_t", [128, HCHUNK * T], BF16)
        u_t = g_t.bitcast(F32)
        kv_k = S("kv_k", [128, 1024], BF16)
        kv_v = S("kv_v", [128, 8 * 129], BF16)
        a_st = [S("a_st0", [128, 8 * 128], BF16),
                S("a_st1", [128, 8 * 128], BF16),
                S("a_st2", [128, 8 * 128], BF16)]
        stats_sb = S("stats_sb", [33, T], F32)
        small = S("small", [128, 48], F32)
        small_i = small.bitcast(mybir.dt.int32)
        den_sb = S("den_sb", [16, T], F32)
        dsp = S("dsp", [128, 64], F32)
        rdl_sp = S("rdl_sp", [128, 64], F32)
        rdh_sb = S("rdh_sb", [16, T], BF16)
        rdl_sb = S("rdl_sb", [16, T], BF16)
        
        wtmp = S("wtmp", [128, 1024], F32)
        rdb_sb = wtmp
        stg_all = S("stg_all", [128, 2688], BF16)
        stg_fv = stg_all.bitcast(F32)
        kvacc = stg_fv
        stg_f = stg_fv
        STGB0 = 1792
        dvv = S("dvv", [128, 320], F32)
        dgb = S("dgb", [128, 8 * 128], BF16)
        ypp = S("ypp", [128, 2 * 128], BF16)
        trn = S("trn", [128, 2 * 128], BF16)

        P = [ctx.enter_context(nc.psum_tensor("pb%d" % i, [128, 512], F32))
             for i in range(8)]

        blk = ctx.enter_context(nc.Block())
        pr = Prog(nc, ctx)

        tens = lambda b, **kw: pr.op("tensor", b, **kw)
        vec = lambda b, **kw: pr.op("vector", b, **kw)
        sca = lambda b, **kw: pr.op("scalar", b, **kw)

        SERIAL_CH = {"misc"}

        def dma(queue, dst, src, reads=(), writes=(), ch="misc"):
            semname = pr.dma_ch(ch)
            if ch in SERIAL_CH:
                pr.raw_wait(queue, semname, pr.cnt[semname])
            return pr.op(queue, lambda e: e.dma_start(dst, src),
                         reads=reads, writes=writes, dma=True, ch=ch)

        def warm(n):
            for _ in range(n):
                pr.op("tensor", lambda e: e.matmul(
                    P[7][:, 0:512], c_sb[:, 0:128], c_sb[:, 0:512],
                    start=True, stop=True, skip_group_check=True), inc=False)

        ALLC = tuple("C:%d" % k for k in range(HCHUNK))
        ALLWA = tuple("wa:%d" % d for d in range(8))
        ALLWB = tuple("wb:%d" % d for d in range(8))

        dma("sync", ident_f[:, :], dins["ident"][:, :], writes=("idf",), ch="misc")
        vec(lambda e: e.tensor_copy(ident_b[:, :], ident_f[:, :]),
            reads=("idf",), writes=("idb",))
        vec(lambda e: e.memset(onesc[:, :], 1.0), writes=("ones",))
        vec(lambda e: e.memset(onesr[:, :], 1.0), writes=("onesr",))
        vec(lambda e: e.memset(onesr_f[:, :], 1.0), writes=("onesrf",))
        vec(lambda e: e.memset(epsr[:, :], float(D) * LN_EPS), writes=("epsr",))
        for h in range(2):
            dma("sync", wtmp[0:16, 0:512],
                dins["ej"][:, 512 * h:512 * h + 512], writes=("wtmp0",),
                ch="wload0")
            vec(lambda e, h=h: e.tensor_copy(
                ej_b[:, 512 * h:512 * h + 512], wtmp[0:16, 0:512]),
                reads=("wtmp0",), writes=("ej",))
        dma("sync", gam[:, :], dins["gamma"][:, :], writes=("gam",), ch="misc")
        dma("sync", bet[:, :], dins["beta"][:, :], writes=("bet",), ch="misc")
        dma("sync", upb_f[:, :], dins["upbias"][:, :], writes=("upb",), ch="misc")
        dma("sync", cch[:, :], dins["bcon"][:, :], writes=("cch",), ch="misc")
        dma("sync", wtmp[:, 0:8], dins["dbias"][:, :], writes=("wtmp0",), ch="wload0")
        vec(lambda e: e.scalar_tensor_tensor(cch[:, :], wtmp[:, 0:8], 1.0,
                                             cch[:, :], alu.mult, alu.add),
            reads=("wtmp0", "cch"), writes=("cch",))
        vec(lambda e: e.tensor_scalar(cch[:, :], cch[:, :], 0.5, None,
                                      alu.mult, alu.bypass),
            reads=("cch",), writes=("cch",))
        dma("sync", wtmp[:, 0:32], dins["bexp"][:, :], writes=("wtmp0",), ch="wload0")
        vec(lambda e: e.tensor_copy(bvec[:, :], wtmp[:, 0:32]),
            reads=("wtmp0",), writes=tuple("bv:%d" % ch for ch in range(32)))
        vec(lambda e: e.memset(kvbd[:, :], 0.0), writes=("kvbd",))
        vec(lambda e: e.memset(ksc[:, :], 0.0), writes=("ksc",))
        vec(lambda e: e.memset(
            bass.AP(kv_v, 128, [[8 * 129, 128], [129, 8]]), 1.0),
            writes=("kvv1",))

        def emit_x_load(iseq, it):
            for h in range(2):
                xk = tuple("x:%d" % c for c in range(4 * h, 4 * h + 4))
                dma("scalar",
                    bass.AP(x_t, 4 * h * T,
                            [[NCHUNK * T, 128], [T, 4], [1, T]]),
                    bass.AP(dx.tensor,
                            iseq * D * seq_len + 4 * h * 128 * seq_len + it * T,
                            [[seq_len, 128], [128 * seq_len, 4], [1, T]]),
                    reads=xk, writes=xk, ch="x%d" % h)


        def load_w(dsrc, dst_sb, tag):
            for i in range(16):
                ch, h = divmod(i, 2)
                p = i % 2
                dma("sync" if p == 0 else "gpsimd",
                    wtmp[:, 512 * p:512 * p + 512],
                    dsrc[128 * ch:128 * ch + 128, 512 * h:512 * h + 512],
                    writes=("wtmp%d" % p,),
                    ch="wload0" if p == 0 else "wload1g")
                vec(lambda e, ch=ch, h=h, p=p, dst_sb=dst_sb: e.tensor_copy(
                    dst_sb[:, 1024 * ch + 512 * h:
                           1024 * ch + 512 * h + 512],
                    wtmp[:, 512 * p:512 * p + 512]),
                    reads=("wtmp%d" % p,), writes=("%s:%d" % (tag, ch),))

        def fill_R_ops(src):
            ops = []
            for i in range(64):
                ch, h = divmod(i, 2)
                p = i % 2

                def one(ch=ch, h=h, p=p):
                    dma("sync" if p == 0 else "gpsimd",
                        wtmp[:, 512 * p:512 * p + 512],
                        src[128 * ch:128 * ch + 128,
                            512 * h:512 * h + 512],
                        writes=("wtmp%d" % p,),
                        ch="wload0" if p == 0 else "wload1g")
                    pr.op("gpsimd" if p == 0 else "vector",
                          lambda e, ch=ch, h=h, p=p: e.tensor_copy(
                              c_sb[:, 1024 * ch + 512 * h:
                                   1024 * ch + 512 * h + 512],
                              wtmp[:, 512 * p:512 * p + 512]),
                          reads=("wtmp%d" % p,), writes=("C:%d" % ch,))
                ops.append(one)
            return ops

        def fill_R_gp(src, eng="vector"):
            for i in range(64):
                ch, h = divmod(i, 2)
                p = i % 2
                dma("sync" if p == 0 else "gpsimd",
                    wtmp[:, 512 * p:512 * p + 512],
                    src[128 * ch:128 * ch + 128, 512 * h:512 * h + 512],
                    writes=("wtmp%d" % p,),
                    ch="wload0" if p == 0 else "wload1g")
                if eng == "vector":
                    vec(lambda e, ch=ch, h=h, p=p: e.tensor_copy(
                        c_sb[:, 1024 * ch + 512 * h:
                             1024 * ch + 512 * h + 512],
                        wtmp[:, 512 * p:512 * p + 512]),
                        reads=("wtmp%d" % p,), writes=("C:%d" % ch,))
                else:
                    sca(lambda e, ch=ch, h=h, p=p: e.activation(
                        c_sb[:, 1024 * ch + 512 * h:
                             1024 * ch + 512 * h + 512],
                        wtmp[:, 512 * p:512 * p + 512], AF.Identity),
                        reads=("wtmp%d" % p,), writes=("C:%d" % ch,))

        def fill_R(src):
            for op in fill_R_ops(src):
                op()

        def load_stage_chunk(src_ap, ch):
            p = ch % 2
            base = p * 1024 + ((ch // 2) % 2) * 2048
            toks = ("gu:%d" % (base // 512), "gu:%d" % (base // 512 + 1))
            dma("scalar" if p == 0 else "sync",
                bass.AP(u_t, base, [[8192, 128], [1, 896]]),
                bass.AP(src_ap.tensor, ch * 128 * 896,
                        [[896, 128], [1, 896]]),
                writes=toks, ch="stg" if p == 0 else "stg1")
            return base, toks

        def lo_chain_pair(landA, landB, down, fill):
            holds = {}
            st = {0: 0, 1: 0}
            order = range(0, 7) if down else range(6, -1, -1)
            for k, s in enumerate(order):
                for p, (base, ltoks) in ((0, landA), (1, landB)):
                    lc = base + s * 128
                    lhs = u_t[:, lc:lc + 128]
                    pb = 4 + p
                    if k == 0:
                        rhs, rkeyn = ident_f[:, :], ("idf",)
                    else:
                        cur = 2 * p + st[p]
                        rhs = stg_fv[:, cur * 128:cur * 128 + 128]
                        rkeyn = ("cs:%d" % cur,)
                    tens(lambda e, lhs=lhs, rhs=rhs, pb=pb: e.matmul(
                        P[pb][0:128, 0:128], lhs, rhs, start=True, stop=True),
                        reads=ltoks + rkeyn, writes=("p%d" % pb,))
                    if k == 6:
                        sca(lambda e, p=p, pb=pb: e.activation(
                            kv_k[:, 128 * p:128 * p + 128],
                            P[pb][0:128, 0:128], AF.Identity),
                            reads=("p%d" % pb,), writes=("kvh:%d" % p,))
                        holds[p] = (kv_k[:, 128 * p:128 * p + 128],
                                    "kvh:%d" % p)
                    else:
                        nxt = 1 - st[p]
                        cn = 2 * p + nxt
                        sca(lambda e, cn=cn, pb=pb:
                            e.activation(stg_fv[:, cn * 128:cn * 128 + 128],
                                         P[pb][0:128, 0:128], AF.Identity),
                            reads=("p%d" % pb,),
                            writes=("cs:%d" % cn,))
                        st[p] = nxt
                    if fill:
                        fill.pop(0)()
            return holds

        def cupd_ops(ch, hold, down):
            pc, pkey = hold
            ops = []
            pbs = (0, 1) if ch % 2 == 0 else (2, 3)
            for h in range(2):
                pb = pbs[h]

                def mm(e, pc=pc, ch=ch, h=h, pb=pb):
                    return e.matmul(
                        P[pb][:, :], pc,
                        c_sb[:, 1024 * ch + 512 * h:1024 * ch + 512 * h + 512],
                        start=True, stop=True)
                ops.append((lambda mm=mm, pkey=pkey, ch=ch, pb=pb:
                            tens(mm, reads=(pkey, "C:%d" % ch),
                                 writes=("p%d" % pb,))))
                if down:
                    ops.append((lambda ch=ch, h=h, pb=pb:
                                vec(lambda e: e.tensor_scalar(
                                    c_sb[:, 1024 * ch + 512 * h:
                                         1024 * ch + 512 * h + 512],
                                    P[pb][:, :], 0.5, None,
                                    alu.mult, alu.bypass),
                                    reads=("p%d" % pb,),
                                    writes=("C:%d" % ch,))))
                else:
                    ops.append((lambda ch=ch, h=h, pb=pb:
                                vec(lambda e: e.tensor_copy(
                                    c_sb[:, 1024 * ch + 512 * h:
                                         1024 * ch + 512 * h + 512],
                                    P[pb][:, :]),
                                    reads=("p%d" % pb,),
                                    writes=("C:%d" % ch,))))
            if not down:
                pbb = 6 if ch % 2 == 0 else 7
                ops.append((lambda pc=pc, pkey=pkey, ch=ch, pbb=pbb:
                            tens(lambda e: e.matmul(
                                P[pbb][0:128, 0:1], pc, bvec[:, ch:ch + 1],
                                start=True, stop=True),
                                reads=(pkey, "bv:%d" % ch),
                                writes=("p%d" % pbb,))))
                ops.append((lambda ch=ch, pbb=pbb:
                            vec(lambda e: e.tensor_copy(
                                bvec[:, ch:ch + 1], P[pbb][0:128, 0:1]),
                                reads=("p%d" % pbb,),
                                writes=("bv:%d" % ch,))))
            return ops

        def lo_stages(src_key, down):
            lands = {0: load_stage_chunk(dins[src_key], 0),
                     1: load_stage_chunk(dins[src_key], 1)}
            fill = []
            for base in range(0, 32, 2):
                holds = lo_chain_pair(lands.pop(base), lands.pop(base + 1),
                                      down, fill)
                if base + 2 < 32:
                    lands[base + 2] = load_stage_chunk(dins[src_key],
                                                       base + 2)
                    lands[base + 3] = load_stage_chunk(dins[src_key],
                                                       base + 3)
                while fill:
                    fill.pop(0)()
                fill = (cupd_ops(base, holds[0], down)
                        + cupd_ops(base + 1, holds[1], down))
            while fill:
                fill.pop(0)()

        def hi_stages(dv_key, bias_too):
            npair = 0
            for si in range(5):
                k = 1 << si
                for cbase in range(32):
                    if cbase & k:
                        continue
                    cp = cbase + k
                    db = 512 * (npair % 2)
                    dk = 4 * (npair % 2)
                    npair += 1
                    cols = [(si * 32 + cbase) * 2, (si * 32 + cbase) * 2 + 1,
                            (si * 32 + cp) * 2, (si * 32 + cp) * 2 + 1]
                    for q, col in enumerate(cols):
                        vec(lambda e, q=q, col=col, db=db: e.tensor_scalar(
                            dgb[:, db + 128 * q:db + 128 * q + 128],
                            ident_b[:, :],
                            dvv[:, col:col + 1], None, alu.mult, alu.bypass),
                            reads=("idb", "dvv"), writes=("dgb:%d" % (dk + q),))
                    for h in range(2):
                        a0 = 1024 * cbase + 512 * h
                        a1 = 1024 * cp + 512 * h
                        pr.mm_group(
                            [lambda e, a0=a0, h=h, db=db: e.matmul(
                                P[h][:, :], dgb[:, db:db + 128],
                                c_sb[:, a0:a0 + 512], start=True, stop=False),
                             lambda e, a1=a1, h=h, db=db: e.matmul(
                                P[h][:, :], dgb[:, db + 128:db + 256],
                                c_sb[:, a1:a1 + 512], start=False, stop=True)],
                            reads=("dgb:%d" % dk, "dgb:%d" % (dk + 1),
                                   "C:%d" % cbase, "C:%d" % cp),
                            writes=("p%d" % h,))
                        pr.mm_group(
                            [lambda e, a1=a1, h=h, db=db: e.matmul(
                                P[2 + h][:, :], dgb[:, db + 256:db + 384],
                                c_sb[:, a1:a1 + 512], start=True, stop=False),
                             lambda e, a0=a0, h=h, db=db: e.matmul(
                                P[2 + h][:, :], dgb[:, db + 384:db + 512],
                                c_sb[:, a0:a0 + 512], start=False, stop=True)],
                            reads=("dgb:%d" % (dk + 2), "dgb:%d" % (dk + 3),
                                   "C:%d" % cbase, "C:%d" % cp),
                            writes=("p%d" % (2 + h),))
                    if bias_too:
                        pr.mm_group(
                            [lambda e, db=db: e.matmul(
                                P[4][0:128, 0:1], dgb[:, db:db + 128],
                                bvec[:, cbase:cbase + 1],
                                start=True, stop=False),
                             lambda e, db=db: e.matmul(
                                P[4][0:128, 0:1], dgb[:, db + 128:db + 256],
                                bvec[:, cp:cp + 1], start=False, stop=True)],
                            reads=("dgb:%d" % dk, "dgb:%d" % (dk + 1),
                                   "bv:%d" % cbase, "bv:%d" % cp),
                            writes=("p4",))
                        pr.mm_group(
                            [lambda e, db=db: e.matmul(
                                P[4][0:128, 1:2], dgb[:, db + 256:db + 384],
                                bvec[:, cp:cp + 1], start=True, stop=False),
                             lambda e, db=db: e.matmul(
                                P[4][0:128, 1:2], dgb[:, db + 384:db + 512],
                                bvec[:, cbase:cbase + 1],
                                start=False, stop=True)],
                            reads=("dgb:%d" % (dk + 2), "dgb:%d" % (dk + 3),
                                   "bv:%d" % cbase, "bv:%d" % cp),
                            writes=("p4",))
                    for h in range(2):
                        sca(lambda e, cbase=cbase, h=h: e.activation(
                            c_sb[:, 1024 * cbase + 512 * h:
                                 1024 * cbase + 512 * h + 512], P[h][:, :],
                            AF.Identity),
                            reads=("p%d" % h,), writes=("C:%d" % cbase,))
                        sca(lambda e, cp=cp, h=h: e.activation(
                            c_sb[:, 1024 * cp + 512 * h:
                                 1024 * cp + 512 * h + 512], P[2 + h][:, :],
                            AF.Identity),
                            reads=("p%d" % (2 + h),), writes=("C:%d" % cp,))
                    if bias_too:
                        vec(lambda e, cbase=cbase: e.tensor_copy(
                            bvec[:, cbase:cbase + 1], P[4][0:128, 0:1]),
                            reads=("p4",), writes=("bv:%d" % cbase,))
                        vec(lambda e, cp=cp: e.tensor_copy(
                            bvec[:, cp:cp + 1], P[4][0:128, 1:2]),
                            reads=("p4",), writes=("bv:%d" % cp,))

        dma("scalar", dvv[:, :], dins["dv_u"][:, :], writes=("dvv",), ch="dvv")
        emit_x_load(0, 0)
        fill_R_gp(dins["wexpT"])
        lo_stages("slo_u", down=False)
        hi_stages("dv_u", bias_too=True)
        vec(lambda e: e.tensor_copy(bfull[:, :], bvec[:, :]),
            reads=tuple("bv:%d" % ch for ch in range(32)), writes=("bfull",))
        vec(lambda e: e.scalar_tensor_tensor(bfull[:, :], upb_f[:, :], 1.0,
                                             bfull[:, :], alu.mult, alu.add),
            reads=("upb", "bfull"), writes=("bfull",))

        for ch in range(32):
            for dj in range(4):
                par = (ch * 4 + dj) % 4
                pb = 4 + par
                for t_ in range(2):
                    d = 2 * dj + t_
                    tens(lambda e, ch=ch, d=d, pb=pb, t_=t_: e.matmul(
                        P[pb][0:128, 128 * t_:128 * t_ + 128],
                        c_sb[:, 1024 * ch + 128 * d:1024 * ch + 128 * d + 128],
                        ident_b[:, :], start=(t_ == 0), stop=(t_ == 1)),
                        reads=("C:%d" % ch, "idb"), writes=("p%d" % pb,))
                slot = dgb[:, 256 * par:256 * par + 256]
                skeys = ("dgb:%d" % (2 * par), "dgb:%d" % (2 * par + 1))
                if par % 2 == 0:
                    sca(lambda e, slot=slot, pb=pb: e.activation(
                        slot, P[pb][0:128, 0:256], AF.Identity),
                        reads=("p%d" % pb,), writes=skeys)
                else:
                    vec(lambda e, slot=slot, pb=pb: e.tensor_copy(
                        slot, P[pb][0:128, 0:256]),
                        reads=("p%d" % pb,), writes=skeys)
                dma("gpsimd" if par < 2 else "sync",
                    bass.AP(asp.tensor, 1024 * ch + 256 * dj,
                            [[8 * HID, 128], [1, 256]]),
                    slot,
                    reads=skeys, writes=("asp:%d:%d" % (ch, dj),),
                    ch="asp%d" % par)

        fill_R_gp(dins["wcon"], eng="scalar")
        dma("scalar", dvv[:, :], dins["dv_d"][:, :], writes=("dvv",), ch="dvv")
        load_w(dins["wk"], w_a, "wa")
        load_w(dins["wv"], w_b, "wb")
        hi_stages("dv_d", bias_too=False)
        lo_stages("slo_dT", down=True)

        for i in range(16):
            for p, (dsrc, dspill) in enumerate(((dins["wq"], wqsp),
                                                (dins["wfc"], wfsp))):
                ch, h = divmod(i, 2)
                ceng = "vector" if p == 0 else "gpsimd"
                ktok = "kvkA" if p == 0 else "kvkB"
                extra = ("kvh:0", "kvh:1") if p == 0 else ()
                dma("sync" if p == 0 else "gpsimd",
                    wtmp[:, 512 * p:512 * p + 512],
                    dsrc[128 * ch:128 * ch + 128, 512 * h:512 * h + 512],
                    writes=("wtmp%d" % p,),
                    ch="wload0" if p == 0 else "wload1g")
                pr.op(ceng, lambda e, p=p: e.tensor_copy(
                    kv_k[:, 512 * p:512 * p + 512],
                    wtmp[:, 512 * p:512 * p + 512]),
                    reads=("wtmp%d" % p,), writes=(ktok,) + extra)
                dma("sync" if p == 0 else "gpsimd",
                    bass.AP(dspill.tensor, 1024 * ch + 512 * h,
                            [[8192, 128], [1, 512]]),
                    kv_k[:, 512 * p:512 * p + 512],
                    reads=(ktok,), writes=("wsp%d" % p,),
                    ch="wspo%d" % p)

        SB = {"s1": s1, "s2": s2, "s3": s3}
        def skey(src_key, c):
            return src_key % c if "%d" in src_key else src_key
        GUALL = tuple("gu:%d" % _c for _c in range(8))
        def wkeys(name, c=None):
            if c is None:
                base = "s1x" if name == "s1" else name
                return (name,) + tuple("%s:%d" % (base, _c) for _c in range(8))
            base = "s1x" if name == "s1" else name
            return ("%s:%d" % (base, _c if False else c),)
        def rkey(name, c):
            base = "s1x" if name == "s1" else name
            return "%s:%d" % (base, c)
        def emit_rows_and_bcast_r(sp_sb, rcol, rkeyn, hilo, ng):
            for g in range(ng):
                tens(lambda e, g=g: e.transpose(
                    P[4][0:1, 128 * g:128 * g + 128],
                    sp_sb[:, rcol + g:rcol + g + 1], ident_f[:, :]),
                    reads=(rkeyn,), writes=("p4",))
            vec(lambda e, ng=ng: e.tensor_copy(stats_sb[0:1, 0:128 * ng],
                                               P[4][0:1, 0:128 * ng]),
                reads=("p4",), writes=("stats",))
            for g in range(ng):
                tens(lambda e, g=g: e.matmul(
                    P[4][0:128, 128 * g:128 * g + 128], onesr_f[0:1, 0:128],
                    stats_sb[0:1, 128 * g:128 * g + 128],
                    start=True, stop=True),
                    reads=("onesrf", "stats"), writes=("p4",))

        def emit_rows_and_bcast_m(sp_sb, mcol, mkeyn, hilo, ng):
            for g in range(ng):
                tens(lambda e, g=g: e.transpose(
                    P[6][0:1, 128 * g:128 * g + 128],
                    sp_sb[:, mcol + g:mcol + g + 1], ident_f[:, :]),
                    reads=(mkeyn,), writes=("p6",))
            vec(lambda e, ng=ng: e.tensor_copy(stats_sb[0:1, 0:128 * ng],
                                               P[6][0:1, 0:128 * ng]),
                reads=("p6", "stats"), writes=("statsm", "stats"))
            for g in range(ng):
                tens(lambda e, g=g: e.matmul(
                    P[6][0:128, 128 * g:128 * g + 128], onesr_f[0:1, 0:128],
                    stats_sb[0:1, 128 * g:128 * g + 128],
                    start=True, stop=True),
                    reads=("onesrf", "statsm"), writes=("p6",))

        def emit_bcast_normalize(sp_sb, rcol, mcol, rkeyn, mkeyn,
                                 src_f32, src_key, outname,
                                 need_f32, hilo, affine, ng, use_bf_src,
                                 cast_n="s1"):
            nrm_eng = lambda c: "vector"
            emit_rows_and_bcast_r(sp_sb, rcol, rkeyn, hilo, ng)
            emit_rows_and_bcast_m(sp_sb, mcol, mkeyn, hilo, ng)
            if use_bf_src:
                ttile, tsrc_t = SB[outname], SB[cast_n]
                for c in range(NCHUNK):
                    pr.op(nrm_eng(c),
                          lambda e, c=c, ttile=ttile, tsrc_t=tsrc_t:
                          e.tensor_tensor(
                              ttile[:, T * c:T * c + T],
                              tsrc_t[:, T * c:T * c + T],
                              P[4][0:128, 0:T], alu.mult),
                          reads=(rkey(cast_n, c), "p4"),
                          writes=wkeys(outname, c))
            else:
                for c in range(NCHUNK):
                    pr.op(nrm_eng(c), lambda e, c=c: e.tensor_tensor(
                        u_t[:, T * c:T * c + T], src_f32[:, T * c:T * c + T],
                        P[4][0:128, 0:T], alu.mult),
                        reads=(skey(src_key, c), "p4", "gu:%d" % c),
                        writes=("gu:%d" % c,))
            if use_bf_src:
                out_bf = SB[outname]
                for c in range(NCHUNK):
                    pr.op(nrm_eng(c),
                          lambda e, c=c, out_bf=out_bf: e.tensor_tensor(
                              out_bf[:, T * c:T * c + T],
                              out_bf[:, T * c:T * c + T],
                              P[6][0:128, 0:T], alu.subtract),
                          reads=(rkey(outname, c), "p6"),
                          writes=wkeys(outname, c))
            else:
                for c in range(NCHUNK):
                    pr.op(nrm_eng(c), lambda e, c=c: e.tensor_tensor(
                        u_t[:, T * c:T * c + T], u_t[:, T * c:T * c + T],
                        P[6][0:128, 0:T], alu.subtract),
                        reads=("gu:%d" % c, "p6"), writes=("gu:%d" % c,))
            if affine:
                for c in range(NCHUNK):
                    sca(lambda e, c=c: e.activation(
                        u_t[:, T * c:T * c + T], u_t[:, T * c:T * c + T],
                        AF.Identity, bias=bet[:, c:c + 1],
                        scale=gam[:, c:c + 1]),
                        reads=("gu:%d" % c, "gam", "bet"),
                        writes=("gu:%d" % c,))
            if need_f32 and outname is not None:
                out_bf = SB[outname]
                for c in range(NCHUNK):
                    sca(lambda e, c=c, out_bf=out_bf: e.activation(
                        out_bf[:, T * c:T * c + T], u_t[:, T * c:T * c + T],
                        AF.Identity),
                        reads=("gu:%d" % c,), writes=wkeys(outname, c))

        def emit_ln_stats_chunk(src_f32, src_key, c, cast_n, sq_n,
                                cast_eng="vector", sq_eng="vector"):
            ct, qt = SB[cast_n], SB[sq_n]
            if cast_eng == "scalar":
                sca(lambda e, c=c, ct=ct: e.activation(
                    ct[:, T * c:T * c + T], src_f32[:, T * c:T * c + T],
                    AF.Identity),
                    reads=(skey(src_key, c),), writes=wkeys(cast_n, c))
            else:
                pr.op(cast_eng, lambda e, c=c, ct=ct: e.tensor_copy(
                    ct[:, T * c:T * c + T], src_f32[:, T * c:T * c + T]),
                    reads=(skey(src_key, c),), writes=wkeys(cast_n, c))
            pr.op(sq_eng, lambda e, c=c, ct=ct, qt=qt: e.tensor_tensor(
                qt[:, T * c:T * c + T], ct[:, T * c:T * c + T],
                ct[:, T * c:T * c + T], alu.mult),
                reads=(rkey(cast_n, c),), writes=wkeys(sq_n, c))

        def emit_ln_stats_mm(c, cast_n, sq_n):
            ct, qt = SB[cast_n], SB[sq_n]
            tens(lambda e, c=c, ct=ct: e.matmul(
                P[4][0:1, 0:T], onesc[:, 0:1], ct[:, T * c:T * c + T],
                start=(c == 0), stop=(c == 7), skip_group_check=True),
                reads=("ones", rkey(cast_n, c)), writes=("p4",))
            tens(lambda e, c=c, qt=qt: e.matmul(
                P[5][0:1, 0:T], onesc[:, 0:1], qt[:, T * c:T * c + T],
                start=(c == 0), stop=False, skip_group_check=True),
                reads=("ones", rkey(sq_n, c)), writes=("p5",))
            if c == 7:
                tens(lambda e: e.matmul(
                    P[5][0:1, 0:T], epsr[0:1, 0:1], onesr[0:1, 0:T],
                    start=False, stop=True, skip_group_check=True),
                    reads=("epsr", "onesr"), writes=("p5",))

        def emit_ln_head():
            ng = NG
            vec(lambda e: e.tensor_copy(stats_sb[0:1, 0:T], P[4][0:1, 0:T]),
                reads=("p4",), writes=("stats",))
            vec(lambda e: e.tensor_copy(stats_sb[32:33, 0:T], P[5][0:1, 0:T]),
                reads=("p5",), writes=("stats",))
            for g in range(ng):
                tens(lambda e, g=g: e.transpose(
                    P[4][0:128, g:g + 1],
                    stats_sb[0:1, 128 * g:128 * g + 128], ident_f[0:1, 0:1]),
                    reads=("stats",), writes=("p4",))
                tens(lambda e, g=g, ng=ng: e.transpose(
                    P[4][0:128, ng + g:ng + g + 1],
                    stats_sb[32:33, 128 * g:128 * g + 128],
                    ident_f[32:33, 32:33]),
                    reads=("stats",), writes=("p4",))
            vec(lambda e, ng=ng: e.tensor_copy(small[:, 0:2 * ng],
                                               P[4][0:128, 0:2 * ng]),
                reads=("p4",), writes=("sm0",))
            mu0, e20, t10, t20, r0, m20 = 8, 12, 16, 20, 24, 28
            vec(lambda e, ng=ng: e.tensor_scalar(
                small[:, mu0:mu0 + 2 * ng], small[:, 0:2 * ng], 1.0 / D, None,
                alu.mult, alu.bypass),
                reads=("sm0",), writes=("smmu", "sme2"))
            vec(lambda e, ng=ng: e.tensor_tensor(
                small[:, t10:t10 + ng], small[:, mu0:mu0 + ng],
                small[:, mu0:mu0 + ng], alu.mult),
                reads=("smmu",), writes=("smt1",))
            vec(lambda e, ng=ng: e.scalar_tensor_tensor(
                small[:, t20:t20 + ng], small[:, t10:t10 + ng], -1.0,
                small[:, e20:e20 + ng], alu.mult, alu.add),
                reads=("smt1", "sme2"), writes=("smv",))
            vec(lambda e, ng=ng: e.tensor_copy(
                small[:, t10:t10 + ng], small_i[:, t20:t20 + ng]),
                reads=("smv",), writes=("smt1",))
            vec(lambda e, ng=ng: e.tensor_scalar(
                small[:, t10:t10 + ng], small[:, t10:t10 + ng], -0.5,
                1597463007.0, alu.mult, alu.add),
                reads=("smt1",), writes=("smt1",))
            vec(lambda e, ng=ng: e.tensor_copy(
                small_i[:, r0:r0 + ng], small[:, t10:t10 + ng]),
                reads=("smt1",), writes=("smr",))
            for _ in range(NEWTON_ITERS):
                vec(lambda e, ng=ng: e.tensor_tensor(
                    small[:, t10:t10 + ng], small[:, r0:r0 + ng],
                    small[:, r0:r0 + ng], alu.mult),
                    reads=("smr",), writes=("smt1",))
                vec(lambda e, ng=ng: e.scalar_tensor_tensor(
                    small[:, m20:m20 + ng], small[:, t10:t10 + ng], -0.5,
                    small[:, t20:t20 + ng], alu.mult, alu.mult),
                    reads=("smt1", "smv"), writes=("smt2",))
                vec(lambda e, ng=ng: e.scalar_tensor_tensor(
                    small[:, r0:r0 + ng], small[:, m20:m20 + ng], 1.5,
                    small[:, r0:r0 + ng], alu.add, alu.mult),
                    reads=("smt2", "smr"), writes=("smr",))
            vec(lambda e, ng=ng: e.tensor_tensor(
                small[:, m20:m20 + ng], small[:, mu0:mu0 + ng],
                small[:, r0:r0 + ng], alu.mult),
                reads=("smmu", "smr"), writes=("smm2",))

        def emit_ln_tail2(src_f32, src_key, outname, need_f32, hilo,
                          affine, cast_n="s1"):
            r0, m20 = 24, 28
            emit_bcast_normalize(small, r0, m20, "smr", "smm2",
                                 src_f32, src_key, outname,
                                 need_f32, hilo, affine, NG,
                                 use_bf_src=not (need_f32 or affine),
                                 cast_n=cast_n)

        def emit_ln_tail(src_f32, src_key, outname, need_f32, hilo,
                         affine, cast_n="s1", sq_n="s2"):
            emit_ln_head()
            emit_ln_tail2(src_f32, src_key, outname, need_f32, hilo,
                          affine, cast_n=cast_n)

        def emit_ln(src_f32, src_key, outname, need_f32, hilo,
                    affine, cast_n="s1", sq_n="s2"):
            for c in range(NCHUNK):
                emit_ln_stats_chunk(src_f32, src_key, c, cast_n, sq_n)
                emit_ln_stats_mm(c, cast_n, sq_n)
            emit_ln_tail(src_f32, src_key, outname, need_f32, hilo,
                         affine, cast_n, sq_n)

        aqueues = ["sync", "gpsimd", "scalar"]

        def emit_ffn(srcname="s3", post_oc=None):
            src = SB[srcname]
            for o in range(HCHUNK):
                par = o % 3
                q = "sync"
                dma(q, bass.AP(a_st[par], 0, [[1024, 128], [1, 1024]]),
                    bass.AP(asp.tensor, 1024 * o,
                            [[8 * HID, 128], [1, 1024]]),
                    reads=tuple("asp:%d:%d" % (o, dj) for dj in range(4)),
                    writes=("ast:%d" % par,), ch="ast%d" % par)
                pr.mm_group(
                    [(lambda e, d=d, par=par, src=src: e.matmul(
                        P[par][:, 0:T], a_st[par][:, 128 * d:128 * d + 128],
                        src[:, T * d:T * d + T],
                        start=(d == 0), stop=(d == 7)))
                     for d in range(8)],
                    reads=("ast:%d" % par,), writes=("p%d" % par,),
                    reads_each=[(rkey(srcname, d),) for d in range(8)])
                sca(lambda e, o=o, par=par: e.activation(
                    g_t[:, T * o:T * o + T], P[par][:, 0:T], AF.Gelu,
                    bias=bfull[:, o:o + 1]),
                    reads=("p%d" % par, "bfull"), writes=("gu:%d" % (o // 2),))
            for oc in range(8):
                par = oc % 4
                pr.mm_group(
                    [(lambda e, k=k, oc=oc, par=par: e.matmul(
                        P[par][:, 0:T],
                        c_sb[:, 1024 * k + 128 * oc:1024 * k + 128 * oc + 128],
                        g_t[:, T * k:T * k + T],
                        start=(k == 0), stop=(k == HCHUNK - 1)))
                     for k in range(HCHUNK)],
                    reads=ALLC, writes=("p%d" % par,),
                    reads_each=[("gu:%d" % (k // 2),) for k in range(HCHUNK)])
                vec(lambda e, oc=oc, par=par: e.scalar_tensor_tensor(
                    x_t[:, T * oc:T * oc + T], P[par][:, 0:T],
                    cch[:, oc:oc + 1], x_t[:, T * oc:T * oc + T],
                    alu.add, alu.add),
                    reads=("p%d" % par, "cch", "x:%d" % oc),
                    writes=("x:%d" % oc,))
                if post_oc is not None:
                    post_oc(oc)

        def emit_kv_group(iseq, it, g):
            if True:
                for half in range(2):
                    kp = (2 * g + half) % 2
                    pr.mm_group(
                        [(lambda e, d=d, g=g, half=half, kp=kp: e.matmul(
                            P[kp][:, 0:512],
                            s3[:, T * d + 128 * g:T * d + 128 * g + 128],
                            w_a[:, 1024 * d + 512 * half:
                                1024 * d + 512 * half + 512],
                            start=(d == 0), stop=(d == 7)))
                         for d in range(8)],
                        reads=ALLWA, writes=("p%d" % kp,),
                        reads_each=[("s3:%d" % d,) for d in range(8)])
                    sca(lambda e, half=half, kp=kp: e.activation(
                        u_t[:, 512 * half:512 * half + 512],
                        P[kp][:, 0:512], AF.Tanh, scale=0.5),
                        reads=("p%d" % kp, "gu:%d" % half),
                        writes=("gu:%d" % half,)),
                    vec(lambda e, half=half: e.tensor_scalar(
                        kv_k[:, 512 * half:512 * half + 512],
                        u_t[:, 512 * half:512 * half + 512],
                        0.5, 0.5, alu.mult, alu.add),
                        reads=("gu:%d" % half,),
                        writes=("kvk", "kvkA", "kvkB"))
                    pr.mm_group(
                        [(lambda e, d=d, g=g, half=half, kp=kp: e.matmul(
                            P[2 + kp][:, 0:512],
                            s3[:, T * d + 128 * g:T * d + 128 * g + 128],
                            w_b[:, 1024 * d + 512 * half:
                                1024 * d + 512 * half + 512],
                            start=(d == 0), stop=(d == 7)))
                         for d in range(8)],
                        reads=ALLWB, writes=("p%d" % (2 + kp),),
                        reads_each=[("s3:%d" % d,) for d in range(8)])
                    vec(lambda e, half=half, kp=kp: e.tensor_copy(
                        bass.AP(kv_v, 129 * 4 * half,
                                [[8 * 129, 128], [129, 4], [1, 128]]),
                        bass.AP(P[2 + kp], 0,
                                [[512, 128], [128, 4], [1, 128]])),
                        reads=("p%d" % (2 + kp),), writes=("kvv",))
                first = (it == 0 and g == 0)
                banks = [(0, [0, 1, 2]), (1, [3, 4, 5]), (2, [6, 7])]
                for bank, pairs in banks:
                    pr.mm_group(
                        [(lambda e, j=j, bank=bank, st0=(j == pairs[0]),
                          st1=(j == pairs[-1]): e.matmul(
                              P[5 + bank][:, 129 * (j - 3 * bank):
                                          129 * (j - 3 * bank) + 129],
                              kv_k[:, 128 * j:128 * j + 128],
                              kv_v[:, 129 * j:129 * j + 129],
                              start=st0, stop=st1))
                         for j in pairs],
                        reads=("kvk", "kvv", "kvv1"),
                        writes=("p%d" % (5 + bank),))
                    w0 = 387 * bank
                    wd = 129 * len(pairs)
                    if first:
                        vec(lambda e, bank=bank, w0=w0, wd=wd:
                            e.tensor_copy(kvacc[:, w0:w0 + wd],
                                          P[5 + bank][:, 0:wd]),
                            reads=("p%d" % (5 + bank),),
                            writes=("kvacc", "cs:0", "cs:1", "cs:2", "cs:3"))
                    else:
                        vec(lambda e, bank=bank, w0=w0, wd=wd:
                            e.tensor_tensor(kvacc[:, w0:w0 + wd],
                                            kvacc[:, w0:w0 + wd],
                                            P[5 + bank][:, 0:wd], alu.add),
                            reads=("p%d" % (5 + bank), "kvacc"),
                            writes=("kvacc",))

        def emit_kv(iseq, it):
            for g in range(NG):
                emit_kv_group(iseq, it, g)

        def emit_kv_evacs(iseq):
            for j in range(8):
                base = (iseq * 8 + j) * 128
                vec(lambda e, j=j, base=base: e.tensor_copy(
                    kvbd[0:64, base:base + 64],
                    kvacc[0:64, 129 * j:129 * j + 64]),
                    reads=("kvacc",), writes=("kvbd",))
                vec(lambda e, j=j, base=base: e.tensor_copy(
                    kvbd[64:128, base + 64:base + 128],
                    kvacc[64:128, 129 * j + 64:129 * j + 128]),
                    reads=("kvacc",), writes=("kvbd",))
                kb = (iseq * 8 + j) * 16
                vec(lambda e, kb=kb, j=j: e.tensor_copy(
                    ksc[0:64, kb + 2 * j:kb + 2 * j + 1],
                    kvacc[0:64, 129 * j + 128:129 * j + 129]),
                    reads=("kvacc",), writes=("ksc",))
                vec(lambda e, kb=kb, j=j: e.tensor_copy(
                    ksc[64:128, kb + 2 * j + 1:kb + 2 * j + 2],
                    kvacc[64:128, 129 * j + 128:129 * j + 129]),
                    reads=("kvacc",), writes=("ksc",))

        pending_kv = None
        for iseq in range(n_seq):
            for it in range(n_tiles):
                if (iseq, it) != (0, 0):
                    emit_x_load(iseq, it)
                if pending_kv is not None:
                    ps, pt = pending_kv
                    emit_kv_group(ps, pt, 0)
                    for c in range(4):
                        emit_ln_stats_chunk(x_t, "x:%d", c, "s1", "s2",
                                            cast_eng="gpsimd",
                                            sq_eng="gpsimd")
                    emit_kv_group(ps, pt, 1)
                    for c in range(4, 8):
                        emit_ln_stats_chunk(x_t, "x:%d", c, "s1", "s2",
                                            cast_eng="gpsimd",
                                            sq_eng="gpsimd")
                    emit_kv_group(ps, pt, 2)
                    for c in range(NCHUNK):
                        emit_ln_stats_mm(c, "s1", "s2")
                    emit_ln_head()
                    emit_kv_group(ps, pt, 3)
                    if pt == n_tiles - 1:
                        emit_kv_evacs(ps)
                    emit_ln_tail2(x_t, "x:%d", "s2", False, False, False)
                else:
                    emit_ln(x_t, "x:%d", "s2", False, False, False)

                def ln2_cb(oc):
                    emit_ln_stats_chunk(x_t, "x:%d", oc, "s1", "s2")
                    if oc > 0:
                        emit_ln_stats_mm(oc - 1, "s1", "s2")
                emit_ffn("s2", post_oc=ln2_cb)
                emit_ln_stats_mm(7, "s1", "s2")
                dma("gpsimd",
                    bass.AP(x1sp.tensor, iseq * D * seq_len + it * T,
                            [[seq_len, 128], [128 * seq_len, 8], [1, T]]),
                    bass.AP(x_t, 0, [[NCHUNK * T, 128], [T, 8], [1, T]]),
                    reads=tuple("x:%d" % c for c in range(NCHUNK)),
                    writes=("x1sp:%d:%d" % (iseq, it),),
                    ch="spill")
                emit_ln_tail(x_t, "x:%d", "s3", True, True, False)
                for c in range(NCHUNK):
                    pr.op("vector" if c % 2 == 0 else "gpsimd",
                          lambda e, c=c: e.tensor_tensor(
                              s1[:, T * c:T * c + T], u_t[:, T * c:T * c + T],
                              s3[:, T * c:T * c + T], alu.subtract),
                          reads=("gu:%d" % c, "s3:%d" % c),
                          writes=wkeys("s1", c))
                dma("sync",
                    bass.AP(xn2hi.tensor, iseq * D * seq_len + it * T,
                            [[seq_len, 128], [128 * seq_len, 8], [1, T]]),
                    bass.AP(s3, 0, [[NCHUNK * T, 128], [T, 8], [1, T]]),
                    reads=tuple("s3:%d" % c for c in range(8)),
                    writes=("xh:%d:%d" % (iseq, it),), ch="xh")
                dma("sync",
                    bass.AP(xn2lo.tensor, iseq * D * seq_len + it * T,
                            [[seq_len, 128], [128 * seq_len, 8], [1, T]]),
                    bass.AP(s1, 0, [[NCHUNK * T, 128], [T, 8], [1, T]]),
                    reads=tuple("s1x:%d" % c for c in range(8)),
                    writes=("xl:%d:%d" % (iseq, it),), ch="xl")
                pending_kv = (iseq, it)
        emit_kv(*pending_kv)
        emit_kv_evacs(n_seq - 1)

        for half in range(2):
            wa_toks = tuple("wa:%d" % c for c in range(4 * half, 4 * half + 4))
            wb_toks = tuple("wb:%d" % c for c in range(4 * half, 4 * half + 4))
            dma("sync" if half == 0 else "scalar",
                bass.AP(w_a, 4096 * half, [[8192, 128], [1, 4096]]),
                bass.AP(wqsp.tensor, 4096 * half, [[8192, 128], [1, 4096]]),
                reads=("wsp0",) + wa_toks, writes=wa_toks,
                ch="wrl0" if half == 0 else "wrl2")
            dma("gpsimd",
                bass.AP(w_b, 4096 * half, [[8192, 128], [1, 4096]]),
                bass.AP(wfsp.tensor, 4096 * half, [[8192, 128], [1, 4096]]),
                reads=("wsp1",) + wb_toks, writes=wb_toks, ch="wrl1")

        for iseq in range(n_seq):
            for it in range(n_tiles):
                dma("scalar",
                    bass.AP(x_t, 0, [[NCHUNK * T, 128], [T, 8], [1, T]]),
                    bass.AP(x1sp.tensor, iseq * D * seq_len + it * T,
                            [[seq_len, 128], [128 * seq_len, 8], [1, T]]),
                    reads=("x1sp:%d:%d" % (iseq, it),)
                    + tuple("x:%d" % c for c in range(NCHUNK)),
                    writes=tuple("x:%d" % c for c in range(NCHUNK)),
                    ch="x")
                dma("sync",
                    bass.AP(s3, 0, [[NCHUNK * T, 128], [T, 8], [1, T]]),
                    bass.AP(xn2hi.tensor, iseq * D * seq_len + it * T,
                            [[seq_len, 128], [128 * seq_len, 8], [1, T]]),
                    reads=("s3", "xh:%d:%d" % (iseq, it))
                    + tuple("s3:%d" % c for c in range(8)),
                    writes=wkeys("s3"), ch="xh")
                dma("sync",
                    bass.AP(s1, 0, [[NCHUNK * T, 128], [T, 8], [1, T]]),
                    bass.AP(xn2lo.tensor, iseq * D * seq_len + it * T,
                            [[seq_len, 128], [128 * seq_len, 8], [1, T]]),
                    reads=("s1", "xl:%d:%d" % (iseq, it))
                    + tuple("s1x:%d" % c for c in range(8)),
                    writes=wkeys("s1"), ch="xl")
                for c in range(NCHUNK):
                    pr.op("vector" if c % 2 == 0 else "gpsimd",
                          lambda e, c=c: e.tensor_tensor(
                              u_t[:, T * c:T * c + T], s3[:, T * c:T * c + T],
                              s1[:, T * c:T * c + T], alu.add),
                          reads=("s3:%d" % c, "s1x:%d" % c, "gu:%d" % c),
                          writes=("gu:%d" % c,))
                for oc in range(8):
                    par = oc % 4
                    pr.mm_group(
                        [(lambda e, d=d, oc=oc, par=par: e.matmul(
                            P[par][:, 0:T],
                            w_a[:, 1024 * d + 128 * oc:
                                1024 * d + 128 * oc + 128],
                            s3[:, T * d:T * d + T],
                            start=(d == 0), stop=(d == 7)))
                         for d in range(8)],
                        reads=ALLWA, writes=("p%d" % par,),
                        reads_each=[("s3:%d" % d,) for d in range(8)])
                    sca(lambda e, oc=oc, par=par: e.activation(
                        s2[:, T * oc:T * oc + T], P[par][:, 0:T],
                        AF.Tanh, scale=0.5),
                        reads=("p%d" % par,), writes=("s2",) + wkeys("s2", oc))
                    vec(lambda e, oc=oc: e.tensor_scalar(
                        s2[:, T * oc:T * oc + T], s2[:, T * oc:T * oc + T],
                        0.5, 0.5, alu.mult, alu.add),
                        reads=wkeys("s2", oc), writes=wkeys("s2", oc))
                kb0 = iseq * 8 * 16
                pr.mm_group(
                    [(lambda e, d=d, kb0=kb0: e.matmul(
                        P[4][0:16, 0:T],
                        ksc[:, kb0 + 16 * d:kb0 + 16 * d + 16],
                        s2[:, T * d:T * d + T],
                        start=(d == 0), stop=(d == 7)))
                     for d in range(8)],
                    reads=("ksc",), writes=("p4",),
                    reads_each=[("s2:%d" % d,) for d in range(8)])
                vec(lambda e: e.tensor_scalar(
                    den_sb[0:16, 0:T], P[4][0:16, 0:T], 1e-8, None,
                    alu.add, alu.bypass),
                    reads=("p4",), writes=("den",))
                warm(4)
                for g in range(NG):
                    tens(lambda e, g=g: e.transpose(
                        P[4][0:128, 16 * g:16 * g + 16],
                        den_sb[0:16, 128 * g:128 * g + 128],
                        ident_f[0:16, 0:16]),
                        reads=("den",), writes=("p4",))
                warm(6)
                vec(lambda e: e.tensor_copy(dsp[:, 0:16 * NG],
                                            P[4][0:128, 0:16 * NG]),
                    reads=("p4",), writes=("dsp",))
                vec(lambda e: e.reciprocal(dsp[:, 0:16 * NG],
                                           dsp[:, 0:16 * NG]),
                    reads=("dsp",), writes=("dsp",))
                for g in range(NG):
                    tens(lambda e, g=g: e.transpose(
                        P[4][0:16, 128 * g:128 * g + 128],
                        dsp[:, 16 * g:16 * g + 16], ident_f[:, :]),
                        reads=("dsp",), writes=("p4",))
                vec(lambda e: e.tensor_copy(rdh_sb[0:16, 0:T],
                                            P[4][0:16, 0:T]),
                    reads=("p4",), writes=("rdh",))
                vec(lambda e: e.tensor_tensor(
                    rdl_sb[0:16, 0:T], P[4][0:16, 0:T], rdh_sb[0:16, 0:T],
                    alu.subtract),
                    reads=("p4", "rdh"), writes=("rdl",))
                for j in range(8):
                    pn = P[j % 2]
                    pnk = "p%d" % (j % 2)
                    pe = P[5 + (j % 2)]
                    pek = "p%d" % (5 + (j % 2))
                    kvb = (iseq * 8 + j) * 128
                    tens(lambda e, pn=pn, kvb=kvb, j=j: e.matmul(
                        pn[:, 0:T], kvbd[:, kvb:kvb + 128],
                        s2[:, T * j:T * j + T], start=True, stop=True),
                        reads=("kvbd", "s2:%d" % j), writes=(pnk,))
                    pr.mm_group(
                        [lambda e, pe=pe, j=j: e.matmul(
                            pe[:, 0:T], ej_b[:, 128 * j:128 * j + 128],
                            rdh_sb[0:16, 0:T], start=True, stop=False),
                         lambda e, pe=pe, j=j: e.matmul(
                            pe[:, 0:T], ej_b[:, 128 * j:128 * j + 128],
                            rdl_sb[0:16, 0:T], start=False, stop=True)],
                        reads=("ej", "rdh", "rdl"), writes=(pek,))
                    sca(lambda e, pn=pn: e.activation(
                        rdb_sb[:, 0:T], pn[:, 0:T], AF.Identity),
                        reads=(pnk,), writes=("rdb", "wtmp0", "wtmp1"))
                    vec(lambda e, pe=pe, j=j: e.tensor_tensor(
                        s1[:, T * j:T * j + T], rdb_sb[:, 0:T],
                        pe[:, 0:T], alu.mult),
                        reads=(pek, "rdb"), writes=("s1x:%d" % j,))
                for oc in range(8):
                    par = oc % 4
                    pr.mm_group(
                        [(lambda e, d=d, oc=oc, par=par: e.matmul(
                            P[par][:, 0:T],
                            w_b[:, 1024 * d + 128 * oc:
                                1024 * d + 128 * oc + 128],
                            s1[:, T * d:T * d + T],
                            start=(d == 0), stop=(d == 7)))
                         for d in range(8)],
                        reads=ALLWB, writes=("p%d" % par,),
                        reads_each=[("s1x:%d" % d,) for d in range(8)])
                    vec(lambda e, oc=oc, par=par: e.tensor_tensor(
                        u_t[:, T * oc:T * oc + T], P[par][:, 0:T],
                        u_t[:, T * oc:T * oc + T], alu.add),
                        reads=("p%d" % par, "gu:%d" % oc),
                        writes=("gu:%d" % oc,))
                    emit_ln_stats_chunk(u_t, "gu:%d", oc, "s2", "s3")
                    if oc > 0:
                        emit_ln_stats_mm(oc - 1, "s2", "s3")
                emit_ln_stats_mm(7, "s2", "s3")
                emit_ln_tail(u_t, "gu:%d", None, False, True, True,
                             cast_n="s2", sq_n="s3")
                warm(26)
                for oc in range(8):
                    pr.op("vector" if oc % 2 == 0 else "gpsimd",
                          lambda e, oc=oc: e.tensor_tensor(
                              x_t[:, T * oc:T * oc + T],
                              x_t[:, T * oc:T * oc + T],
                              u_t[:, T * oc:T * oc + T], alu.add),
                          reads=("x:%d" % oc, "gu:%d" % oc),
                          writes=("x:%d" % oc,))
                emit_ln(x_t, "x:%d", "s2", False, False, False,
                        cast_n="s1", sq_n="s3")
                warm(14)
                emit_ffn("s2")
                dma("gpsimd",
                    bass.AP(dout.tensor, iseq * D * seq_len + it * T,
                            [[seq_len, 128], [128 * seq_len, 8], [1, T]]),
                    bass.AP(x_t, 0, [[NCHUNK * T, 128], [T, 8], [1, T]]),
                    reads=tuple("x:%d" % c for c in range(NCHUNK)),
                    writes=("outd",), ch="out")

        pr.raw_wait("gpsimd", "dch_out", pr.cnt["dch_out"])
        pr.realize(blk)
    return nc



def kernel(**inputs):
    x = np.asarray(inputs["L0_attn_inputs"], dtype=np.float32)
    consts = host_constants(inputs)
    nc = build_program(n_seq=2, seq_len=4096, T=512)
    in_maps = []
    for core in range(8):
        xs = x[2 * core:2 * core + 2]
        xfm = np.ascontiguousarray(xs.transpose(0, 2, 1)).reshape(2 * D, 4096)
        m = {"x": xfm}
        m.update(consts)
        in_maps.append(m)
    res = run_bass_kernel_spmd(nc, in_maps, core_ids=list(range(8)))
    outs = []
    for core in range(8):
        o = np.asarray(res.results[core]["out"]).reshape(2, D, 4096)
        outs.append(o.transpose(0, 2, 1))
    return np.ascontiguousarray(np.concatenate(outs, axis=0), dtype=np.float32)
